# version 08
# baseline (speedup 1.0000x reference)
import numpy as np
import ml_dtypes
import concourse.bass as bass
import concourse.mybir as mybir
from concourse.bass_utils import run_bass_kernel_spmd

F32 = mybir.dt.float32
BF16 = mybir.dt.bfloat16
I32 = mybir.dt.int32
AF = mybir.ActivationFunctionType
ALU = mybir.AluOpType
AX = mybir.AxisListType.X
BF = ml_dtypes.bfloat16

NCORES = 8
SEQ = 2
T = 2048
D = 1024
NT = T // 128
EPS_LN = 1e-5

ENGINES = ("sync", "tensor", "vector", "scalar", "gpsimd")
STRICT_SAME_ENGINE_WARWAW_OFF = False

_BLOCKS = list(range(8))
_GROUPS = None
_DEBUG_TAPS = []
_DEBUG_RES = {}
_L0F_NFB = 3


class _Op:
    __slots__ = ("eng", "idx", "dom", "dpos", "fn", "deps", "inc", "amt", "clock", "dsem")


class K:

    def __init__(self, nc):
        self.nc = nc
        self.ops = {e: [] for e in ENGINES}
        self.lastw = {}
        self.readers = {}
        self.know = {e: {} for e in ENGINES}
        self.ncomp = {e: 0 for e in ENGINES}
        self.ndma = {e: 0 for e in ENGINES}
        self._sb = {}
        self._dram_in = {}
        self._dram_out = {}
        self._banks = None
        self.out_dma_ops = []
        self.dma_cum = {}
        self.dma_sem_of = {}

    ARENAS = {"arA": 32768, "arB": 32768, "arC": 16448, "arD": 16384,
              "arE": 8192, "arF": 8192}
    ALIAS = {
        "L0_xT": ("arE", 0),
        "L0_lamT": ("arA", 0), "L0_ugT": ("arA", 16384),
        "L0_ysT": ("arB", 0), "L0_ysTb": ("arB", 16384),
        "L0_WTb": ("arD", 0),
        "L0f_hT": ("arA", 0), "L0f_y1": ("arB", 0), "L0f_wdq": ("arD", 0),
        "L0f_sq": ("arC", 0),
        "L0f_scb": ("arE", 0), "L0f_muB": ("arE", 4096), "L0f_musqB": ("arE", 6144),
        "L0f_varB": ("arF", 0), "L0f_AB": ("arF", 2048), "L0f_BB": ("arF", 4096),
        "L0f_tmpB": ("arF", 6144),
        "at_xT": ("arA", 0), "at_OT": ("arB", 0),
        "at_QT": ("arC", 0), "at_KT": ("arC", 4096), "at_V": ("arC", 8192),
        "at_P": ("arC", 12352),
        "at_rdn": ("arD", 4096), "at_drow": ("arD", 0),
        "at_wo": ("arD", 0),
        "L1_lat": ("arE", 0),
        "at_wq": ("arF", 0), "at_wk": ("arF", 2048), "at_wv": ("arF", 2560),
        "L1_wkvd": ("arF", 4096),
        "f1_y": ("arB", 0), "f1_hT": ("arD", 0), "f1_ie": ("arE", 0),
        "f1_wr": ("arF", 0), "f1_oe": ("arF", 2048),
        "f1_z": ("arF", 2560), "f1_zsq": ("arF", 3072), "f1_zbf": ("arF", 3328),
        "f1_gates": ("arF", 3584), "f1_cent": ("arF", 4608),
        "ldstage0": ("arC", 0), "ldstage1": ("arC", 4096),
        "wstage0": ("arC", 8192), "wstage1": ("arC", 10240),
        "p0_qi": ("arC", 12288), "p0_qf": ("arD", 0), "p0_scb": ("arD", 4096),
        "L2_wk": ("arF", 2048), "L2_wv": ("arF", 4096), "L2_qt": ("arE", 0),
        "f2_h": ("arE", 0), "f2_win": ("arF", 0), "f2_w2s": ("arF", 1024),
        "f2_wout": ("arF", 2048), "f2_bout": ("arF", 4096), "f2_rep": ("arD", 0),
        "f2_s": ("arF", 6144), "f2_g": ("arF", 7168),
        "f3_rr": ("arF", 6144),
        "L3_bp": ("arE", 0),
        "f3_y": ("arB", 0), "f3_hsT": ("arD", 0),
        "f3_b2s": ("arF", 0), "f3_bs2s": ("arF", 2048), "f3_hg": ("arF", 4096),
        "f3_hgT": ("arF", 5120), "f3_b1r": ("arF", 6144), "f3_gates": ("arF", 7168), "f3_gw": ("arF", 7680), "f3_gT": ("arF", 7808),
        "f3_ws1": ("arE", 0), "f3_ws2": ("arE", 0), "f3_w1": ("arE", 0),
        "f3_w2": ("arD", 0), "f3_sc": ("arE", 0), "f3_ge1": ("arE", 64), "f3_slx": ("arE", 128), "f3_mask": ("arE", 192), "f3_m1": ("arE", 256), "f3_m2": ("arE", 288), "f3_den": ("arE", 320), "f3_rden": ("arE", 352),
    }

    def sb(self, tag, shape, dtype):
        self._sbinfo = getattr(self, "_sbinfo", {})
        self._sbinfo[tag] = (list(shape), dtype)
        if tag in self.ALIAS:
            assert len(shape) == 2, tag
            ar, byte0 = self.ALIAS[tag]
            base = self.sb(ar, [128, self.ARENAS[ar] // 2], BF16)
            esz = 2 if dtype == BF16 else 4
            c0 = byte0 // esz
            n = shape[1]
            assert (byte0 + n * esz) <= self.ARENAS[ar], (tag, ar, byte0, shape)
            v = base.bitcast(dtype) if dtype != BF16 else base
            return v[0:shape[0], c0:c0 + n]
        if tag not in self._sb:
            self._sb[tag] = self.nc.alloc_sbuf_tensor(tag, list(shape), dtype).ap()
        ap = self._sb[tag]
        assert tuple(ap.shape) == tuple(shape), (tag, ap.shape, shape)
        return ap

    def col(self, tag):
        return self.sb(tag, [128, 1], F32)

    def constcol(self, tag, value):
        if tag not in self._sb:
            ap = self.sb(tag, [128, 1], F32)
            self.memset("vector", ap, value)
            return ap
        return self._sb[tag]

    def bank2(self, j):
        if self._banks is None:
            self._banks = [self.nc.alloc_psum_tensor(f"psb{q}", [128, 1024], F32).ap()
                           for q in range(4)]
        return self._banks[j]

    def bank(self, i):
        b = self.bank2(i // 2)
        return b[:, (i % 2) * 512:(i % 2 + 1) * 512]

    def bank_bf(self, i):
        b = self.bank2(i // 2).bitcast(BF16)
        return b[:, (i % 2) * 1024:(i % 2 + 1) * 1024]

    def inp(self, name, shape, dtype):
        if name not in self._dram_in:
            self._dram_in[name] = self.nc.dram_tensor(name, list(shape), dtype,
                                                      kind="ExternalInput").ap()
        return self._dram_in[name]

    def outp(self, name, shape, dtype):
        if name not in self._dram_out:
            self._dram_out[name] = self.nc.dram_tensor(name, list(shape), dtype,
                                                       kind="ExternalOutput").ap()
        return self._dram_out[name]

    def tw(self, name):
        return self._dram_in["#" + name + "_bf"]

    def idram(self, name, shape, dtype):
        if name not in self._dram_in:
            self._dram_in["#" + name] = self.nc.dram_tensor(name, list(shape), dtype,
                                                            kind="Internal").ap()
        return self._dram_in["#" + name]

    BUCKET = 2048

    @staticmethod
    def _keys(x):
        if isinstance(x, str):
            return [x]
        t = x.tensor
        name = t.name
        try:
            space = str(t.__class__.__name__)
        except Exception:
            space = ""
        if "PSum" in space:
            esz0 = 2 if x.dtype == BF16 else 4
            off0 = x.offset
            if not isinstance(off0, int):
                return [name]
            dims0 = list(x.ap)
            rs0 = dims0[0][0] if dims0 else 0
            if rs0 > 0:
                off0 = off0 % rs0
            span0 = 1
            for st0, ct0 in dims0[1:]:
                if st0 > 0:
                    span0 += st0 * (ct0 - 1)
            b0_ = (off0 * esz0) // 2048
            b1_ = ((off0 + span0) * esz0 - 1) // 2048
            return [(name, "bk", b) for b in range(b0_, b1_ + 1)]
        if "SBTensor" not in space:
            return [name]
        esz = 2 if x.dtype == BF16 else 4
        off = x.offset
        if not isinstance(off, int):
            return [name]
        dims = list(x.ap)
        rowstride = dims[0][0] if dims else 0
        if rowstride > 0:
            off = off % rowstride
        span = 1
        for st, ct in dims[1:]:
            if st > 0:
                span += st * (ct - 1)
        b0 = (off * esz) // K.BUCKET
        b1 = ((off + span) * esz - 1) // K.BUCKET
        return [(name, b) for b in range(b0, b1 + 1)]

    NDMASEM = 80

    def _rec(self, eng, fn, reads=(), writes=(), sreads=(), dma=False, batch=None):
        op = _Op()
        op.eng = eng
        op.fn = fn
        op.inc = False
        op.amt = 16 if dma else 1
        op.idx = len(self.ops[eng])
        if dma:
            wname = None
            for x in writes:
                if not isinstance(x, str):
                    wname = x.tensor.name
                    break
            if wname is None:
                wname = "_anon"
            if wname not in self.dma_sem_of:
                self.dma_sem_of[wname] = len(self.dma_sem_of)
            si = self.dma_sem_of[wname]
            assert si < self.NDMASEM, f"too many DMA-written buffers: {wname}"
            op.dom = f"dsem{si}"
            op.dsem = si
            op.dpos = 0
        else:
            op.dpos = self.ncomp[eng]
            self.ncomp[eng] += 1
            op.dom = eng
            op.dsem = None
        rk = [kk for x in reads for kk in self._keys(x)]
        sk = [kk for x in sreads for kk in self._keys(x)]
        wk = [kk for x in writes for kk in self._keys(x)]
        raw = {}
        for kk in rk + sk:
            for lw in self.lastw.get(kk, {}).values():
                raw[lw] = True
        for kk in wk:
            for lw in self.lastw.get(kk, {}).values():
                raw[lw] = raw.get(lw, False)
            for r in self.readers.get(kk, ()):
                if r is not None:
                    raw[r] = raw.get(r, False)
        kn = self.know[eng]
        deps = []
        for prod in sorted(raw.keys(), key=lambda o: (o.eng, o.idx)):
            strict = raw[prod]
            if prod is op:
                continue
            if prod.dsem is not None:
                val = self.dma_cum[prod.dsem]
                if kn.get(prod.dom, -1) >= val:
                    continue
                deps.append((prod, val))
                kn[prod.dom] = val
                for d2, p2 in prod.clock.items():
                    if kn.get(d2, -1) < p2:
                        kn[d2] = p2
                continue
            if kn.get(prod.dom, -1) >= prod.dpos:
                if prod.eng != "tensor":
                    prod.inc = True
                continue
            if STRICT_SAME_ENGINE_WARWAW_OFF and (not strict) and (not dma) and prod.dom == eng:
                continue
            deps.append((prod, None))
            for d2, p2 in prod.clock.items():
                if kn.get(d2, -1) < p2:
                    kn[d2] = p2
        if dma:
            self.dma_cum[op.dsem] = self.dma_cum.get(op.dsem, 0) + 16
            op.dpos = self.dma_cum[op.dsem]
        op.deps = deps
        op.clock = dict(kn)
        op.clock[op.dom] = op.dpos
        self.ops[eng].append(op)
        for kk in rk + sk:
            self.readers.setdefault(kk, []).append(op)
        for kk in wk:
            d = self.lastw.setdefault(kk, {})
            old = d.get(op.dom)
            if old is not None and old is not op and old.eng != "tensor":
                old.inc = True
            d[op.dom] = op
            self.readers[kk] = []
        return op

    def dma(self, q, out, in_, batch=None):
        self._rec(q, lambda e, o=out, i=in_: e.dma_start(o, i),
                  reads=[in_], writes=[out], dma=True, batch=batch)

    def copy(self, eng, out, in_):
        if eng == "scalar":
            self._rec(eng, lambda e, o=out, i=in_: e.copy(o, i),
                      reads=[in_], writes=[out])
        else:
            self._rec(eng, lambda e, o=out, i=in_: e.tensor_copy(o, i),
                      reads=[in_], writes=[out])

    def memset(self, eng, ap, v):
        self._rec(eng, lambda e, a=ap, vv=v: e.memset(a, vv), writes=[ap])

    def tt(self, eng, out, a, b, op):
        self._rec(eng, lambda e, o=out, x=a, y=b, p=op: e.tensor_tensor(o, x, y, p),
                  reads=[a, b], writes=[out])

    def ts(self, eng, out, a, s1, op0, s2=None, op1=None, accum=None):
        sreads = [s for s in (s1, s2) if isinstance(s, bass.AP)]
        outs = [out] + ([accum] if accum is not None else [])
        def f(e, o=out, x=a, s1=s1, s2=s2, op0=op0, op1=op1, acc=accum):
            if op1 is None:
                return e.tensor_single_scalar(o, x, s1, op0)
            return e.tensor_scalar(o, x, s1, s2, op0, op1, accum_out=acc)
        self._rec(eng, f, reads=[a], writes=outs, sreads=sreads)

    def stt(self, eng, out, in0, s, in1, op0, op1, accum=None):
        sreads = [s] if isinstance(s, bass.AP) else []
        outs = [out] + ([accum] if accum is not None else [])
        def f(e, o=out, a=in0, ss=s, b=in1, p0=op0, p1=op1, acc=accum):
            return e.scalar_tensor_tensor(o, a, ss, b, p0, p1, accum_out=acc)
        self._rec(eng, f, reads=[in0, in1], writes=outs, sreads=sreads)

    def scan(self, eng, out, a, b, init):
        sreads = [init] if isinstance(init, bass.AP) else []
        def f(e, o=out, x=a, y=b, i=init):
            return e.tensor_tensor_scan(o, x, y, i, ALU.mult, ALU.add)
        self._rec(eng, f, reads=[a, b], writes=[out], sreads=sreads)

    def act(self, eng, out, in_, func, bias=0.0, scale=1.0, accum=None):
        sreads = [s for s in (bias, scale) if isinstance(s, bass.AP)]
        outs = [out] + ([accum] if accum is not None else [])
        def f(e, o=out, i=in_, fn=func, b=bias, s=scale, acc=accum):
            return e.activation(o, i, fn, bias=b, scale=s, accum_out=acc)
        self._rec(eng, f, reads=[in_], writes=outs, sreads=sreads)

    def recip(self, out, in_):
        self._rec("vector", lambda e, o=out, i=in_: e.reciprocal(o, i),
                  reads=[in_], writes=[out])

    def red(self, eng, out, in_, op, absval=None):
        def f(e, o=out, i=in_, p=op, av=absval):
            return e.tensor_reduce(o, i, AX, p, apply_absolute_value=av)
        self._rec(eng, f, reads=[in_], writes=[out])

    def mm(self, out, lhsT, rhs, start, stop, tp=None):
        def f(e, o=out, l=lhsT, r=rhs, st=start, sp=stop, t=tp):
            return e.matmul(o, l, r, start=st, stop=sp, tile_position=t)
        self._rec("tensor", f, reads=[lhsT, rhs], writes=[out])

    def trans(self, out, in_, ident):
        def f(e, o=out, i=in_, d=ident):
            return e.transpose(o, i, d)
        self._rec("tensor", f, reads=[in_, ident], writes=[out])

    def out_dma(self, q, out, in_):
        src = self._rec(q, lambda e, o=out, i=in_: e.dma_start(o, i),
                        reads=[in_], writes=[out], dma=True)
        self.out_dma_ops.append(src)

    def load_cast(self, dst, src, stage_tag, eng="vector"):
        n = 1
        for d in src.shape[1:]:
            n *= d
        p = src.shape[0]
        assert n <= 1024 and len(src.shape) == 2
        stage = self.sb(stage_tag, [128, 1024], F32)
        self.dma("sync", stage[0:p, 0:n], src)
        self.copy(eng, dst, stage[0:p, 0:n])

    def build(self):
        if self.out_dma_ops:
            kn = self.know["sync"]
            op = _Op()
            op.eng = "sync"
            op.idx = len(self.ops["sync"])
            op.dom = "sync"
            op.dpos = self.ncomp["sync"]
            op.fn = lambda e: e.nop()
            op.inc = False
            op.amt = 1
            op.dsem = None
            deps = []
            for prod in self.out_dma_ops:
                val = self.dma_cum[prod.dsem]
                if kn.get(prod.dom, -1) < val:
                    deps.append((prod, val))
                    kn[prod.dom] = val
            op.deps = deps
            op.clock = dict(kn)
            self.ops["sync"].append(op)
        for e in ENGINES:
            for op in self.ops[e]:
                if op.amt == 16:
                    op.inc = True
                for prod, _v in op.deps:
                    prod.inc = True
        pc = {}
        for e in ENGINES:
            tot = 0
            lst = []
            for op in self.ops[e]:
                if op.inc and op.dsem is None:
                    tot += op.amt
                lst.append(tot)
            pc[e] = lst
        sems = {e: self.nc.alloc_semaphore(f"prog_{e}") for e in ENGINES}
        dsems = [self.nc.alloc_semaphore(f"dma_{i}") for i in range(self.NDMASEM)]
        nwaits = sum(len(op.deps) for e in ENGINES for op in self.ops[e])
        nincs = sum(1 for e in ENGINES for op in self.ops[e] if op.inc)
        nops = sum(len(self.ops[e]) for e in ENGINES)
        print(f"[K] ops={nops} waits={nwaits} incs={nincs} "
              + " ".join(f"{e}:{len(self.ops[e])}" for e in ENGINES))
        with self.nc.Block() as block:
            for e in ENGINES:
                def mk(e):
                    def body(eng):
                        for op in self.ops[e]:
                            for prod, val in op.deps:
                                if prod.dsem is not None:
                                    eng.wait_ge(dsems[prod.dsem], val)
                                else:
                                    eng.wait_ge(sems[prod.eng], pc[prod.eng][prod.idx])
                            inst = op.fn(eng)
                            if op.inc:
                                if op.dsem is not None:
                                    inst.then_inc(dsems[op.dsem], 16)
                                else:
                                    inst.then_inc(sems[e], op.amt)
                    return body
                getattr(block, e)(mk(e))



W_MANIFEST = [
    ("L0_W_T", [D, D]),
    ("L0f_laT2", [6 * D, 32]), ("L0f_lbT2", [6 * 32, D]),
    ("L1_Wq", [D, D]), ("L1_Wkvd", [D, 256]), ("L1_Wku", [256, D]),
    ("L1_Wvu", [256, D]), ("L1_Wout", [D, D]),
    ("L1f_wrT", [D, 128]), ("L1f_ie2", [16 * D, 512]), ("L1f_oe2", [16 * 512, 64]),
    ("L2f_win", [D, 64]), ("L2f_w2s", [D, 64]), ("L2f_wout", [64, D]),
    ("L2f_bout", [1, D]),
    ("L3_wqkv", [D, 3 * D]), ("L3_wp", [D, D]), ("L3_bproj", [1, D]),
    ("L3f_gw", [D, 8]), ("L3f_w12", [8 * D, 512]), ("L3f_w22", [8 * 512, D]),
    ("L3f_ws1", [D, 512]), ("L3f_ws2", [512, D]),
    ("L3f_b1", [1, 8 * 512]), ("L3f_b2", [8, D]), ("L3f_bs2", [1, D]),
]


def conv_chunk(k, dst_bf_dram, src_f32_dram, i):
    stg = k.sb(f"ldstage{i % 2}", [128, 1024], F32)
    wst = k.sb(f"wstage{i % 2}", [128, 1024], BF16)
    p, n = src_f32_dram.shape
    k.dma("sync", stg[0:p, 0:n], src_f32_dram)
    k.copy("vector" if i % 2 else "scalar", wst[0:p, 0:n], stg[0:p, 0:n])
    k.dma("sync", dst_bf_dram, wst[0:p, 0:n])


def emit_weight_prep(k):
    ci = 0
    for name, shape in W_MANIFEST:
        src = k.inp(name, shape, F32)
        twin = k.idram(name + "_bf", shape, BF16)
        R, C = shape
        for r0 in range(0, R, 128):
            p = min(128, R - r0)
            for c0 in range(0, C, 1024):
                n = min(1024, C - c0)
                conv_chunk(k, twin[r0:r0 + p, c0:c0 + n],
                           src[r0:r0 + p, c0:c0 + n], ci)
                ci += 1
    qT = k.inp("L0f_qT", [6, D, D], I32)
    scT = k.inp("L0f_scT", [6, 64, D], F32)
    wdqt = k.idram("L0f_wdq_bf", [6, D, D], BF16)
    qi = k.sb("p0_qi", [128, D], I32)
    qf = k.sb("p0_qf", [128, D], F32)
    sc_sb = k.sb("p0_scb", [128, D], F32)
    for l in range(6):
        for dc in range(8):
            k.dma("sync", qi, qT[l, dc * 128:(dc + 1) * 128, :])
            for g in range(8):
                k.dma("sync", sc_sb[16 * g:16 * (g + 1), :],
                      scT[l, 8 * dc + g, :].unsqueeze(0).broadcast_to([16, D]))
            k.copy("vector", qf, qi)
            k.ts("vector", qf, qf, 1.0 / 7.5, ALU.mult, -1.0, ALU.add)
            wst = k.sb(f"wstage{dc % 2}", [128, 1024], BF16)
            k.tt("vector", wst, qf, sc_sb, ALU.mult)
            k.dma("sync", wdqt[l, dc * 128:(dc + 1) * 128, :], wst)
    waT = k.inp("L2_waT", [D, 3 * D], F32)
    wpT = k.inp("L2_wpT", [D, D], F32)
    waq = k.idram("L2_waq_bf", [D, 3 * D], BF16)
    wpq = k.idram("L2_wpq_bf", [D, D], BF16)
    sa = k.col("L2_sa")
    ra = k.col("L2_ra")
    sp = k.col("L2_sp")
    rp = k.col("L2_rp")
    quant_mean(k, "L2qa", waT, 3 * D, sa, ra, 3 * D * D)
    quant_mean(k, "L2qp", wpT, D, sp, rp, D * D)
    for (W, Q, nc_, s_, r_) in ((waT, waq, 3 * D, sa, ra), (wpT, wpq, D, sp, rp)):
        for pc in range(8):
            for cc in range(nc_ // 1024):
                wst = k.sb(f"wstage{(pc + cc) % 2}", [128, 1024], BF16)
                quant_slice(k, wst, W[pc * 128:(pc + 1) * 128, cc * 1024:(cc + 1) * 1024],
                            f"ldstage{(pc + cc) % 2}", s_, r_)
                k.dma("sync", Q[pc * 128:(pc + 1) * 128, cc * 1024:(cc + 1) * 1024], wst)



def stream_tile(k, s, t):
    st = k.sb("stream", [128, SEQ * NT * D], BF16)
    i = s * NT + t
    return st[:, i * D:(i + 1) * D]


def ident_bf(k):
    ap = k.sb("ident_bf", [128, 128], BF16)
    return ap


def ln_stats_core(k, tag, src, mu_sl, var_sl, n=D):
    s_col = k.col(f"{tag}_s")
    ss_col = k.col(f"{tag}_ss")
    musq = k.col(f"{tag}_musq")
    sq = k.sb("lnsq", [128, n], BF16)
    k.red("vector", s_col, src, ALU.add)
    k.stt("vector", sq, src, 1.0, src, ALU.mult, ALU.mult, accum=ss_col)
    k.ts("vector", mu_sl, s_col, 1.0 / n, ALU.mult)
    k.tt("vector", musq, mu_sl, mu_sl, ALU.mult)
    k.stt("vector", var_sl, ss_col, 1.0 / n, musq, ALU.mult, ALU.subtract)


def ln_finish(k, tag, muB, varB, rstdB, nmrB):
    k.act("scalar", varB, varB, AF.Ln, bias=k.constcol("c_epsln", EPS_LN))
    k.act("scalar", rstdB, varB, AF.Exp, scale=-0.5)
    k.stt("vector", nmrB, muB, -1.0, rstdB, ALU.mult, ALU.mult)


def pre_ln_xT(k, tag, s, tiles, xT, bank0=6):
    W = 128 * len(tiles)
    B = len(tiles)
    idb = ident_bf(k)
    muB = k.sb("pre_muB", [128, 16], F32)
    varB = k.sb("pre_varB", [128, 16], F32)
    rstdB = k.sb("pre_rstdB", [128, 16], F32)
    nmrB = k.sb("pre_nmrB", [128, 16], F32)
    for j, t in enumerate(tiles):
        ln_stats_core(k, f"ln{j % 2}", stream_tile(k, s, t),
                      muB[:, j:j + 1], varB[:, j:j + 1])
    ln_finish(k, "pre", muB[:, 0:B], varB[:, 0:B], rstdB[:, 0:B], nmrB[:, 0:B])
    for j, t in enumerate(tiles):
        st = stream_tile(k, s, t)
        xn = k.sb(f"xn{j % 2}", [128, D], BF16)
        k.ts("vector", xn, st, rstdB[:, j:j + 1], ALU.mult, nmrB[:, j:j + 1], ALU.add)
        bank = k.bank_bf(bank0 - 1 + 2 * (j % 2))
        for dc in range(8):
            k.trans(bank[:, dc * 128:(dc + 1) * 128], xn[:, dc * 128:(dc + 1) * 128], idb)
        dst = xT.rearrange("p (dc w) -> p dc w", w=W)[:, :, j * 128:(j + 1) * 128]
        src = bank.rearrange("p (dc w) -> p dc w", w=128)
        k.copy("vector", dst, src)


def post_ln_add(k, tag, s, t, ybuf):
    st = stream_tile(k, s, t)
    p = t % 4
    mu = k.col(f"pl{p}_mu")
    var = k.col(f"pl{p}_var")
    rstd = k.col(f"pl{p}_rstd")
    nmr = k.col(f"pl{p}_nmr")
    ln_stats_core(k, f"pl{p}", ybuf, mu, var)
    ln_finish(k, f"pl{p}", mu, var, rstd, nmr)
    k.ts("vector", st, st, nmr, ALU.add)
    k.stt("vector", st, ybuf, rstd, st, ALU.mult, ALU.add)



def emit_L0_attn(k):
    bcol = k.inp("L0_b", [D, 1], F32)
    idb = ident_bf(k)
    WTb = k.sb("L0_WTb", [128, 8 * D], BF16)
    WTt = k.tw("L0_W_T")
    k.dma("sync", WTb.rearrange("p (dc c) -> p dc c", c=D),
          WTt.rearrange("(dc p) c -> p dc c", p=128))
    bsb = k.sb("L0_bias", [128, 8], F32)
    for ec in range(8):
        k.dma("sync", bsb[:, ec:ec + 1], bcol[ec * 128:(ec + 1) * 128, :])

    CH = 512
    NCH = T // CH
    xT = k.sb("L0_xT", [128, 8 * CH], BF16)
    lamT = k.sb("L0_lamT", [128, 8 * CH], F32)
    ugT = k.sb("L0_ugT", [128, 8 * CH], F32)
    ysT = k.sb("L0_ysT", [128, 8 * CH], F32)
    ysTb = k.sb("L0_ysTb", [128, 8 * CH], BF16)
    state = k.sb("L0_state", [128, 8], F32)
    ybuf = k.sb("at_ybuf", [128, D], BF16)

    for s in range(SEQ):
        k.memset("vector", state, 0.0)
        for c in range(NCH):
            tiles = [c * 4 + j for j in range(4)]
            pre_ln_xT(k, "L0", s, tiles, xT)
            for ec in range(8):
                bank = k.bank(2 * (ec % 2))
                for dc in range(8):
                    k.mm(bank, WTb[:, dc * D + ec * 128: dc * D + (ec + 1) * 128],
                         xT[:, dc * CH:(dc + 1) * CH], start=(dc == 0), stop=(dc == 7))
                k.act("scalar", lamT[:, ec * CH:(ec + 1) * CH], bank, AF.Sigmoid,
                      bias=bsb[:, ec:ec + 1])
            k.tt("vector", ugT, lamT, xT, ALU.mult)
            k.tt("vector", ugT, xT, ugT, ALU.subtract)
            for dc in range(8):
                k.scan("vector", ysT[:, dc * CH:(dc + 1) * CH],
                       lamT[:, dc * CH:(dc + 1) * CH], ugT[:, dc * CH:(dc + 1) * CH],
                       state[:, dc:dc + 1])
                k.copy("vector", state[:, dc:dc + 1], ysT[:, (dc + 1) * CH - 1:(dc + 1) * CH])
            k.copy("vector", ysTb, ysT)
            for j in range(4):
                bank = k.bank_bf(1 + 2 * (j % 2))
                for dc in range(8):
                    k.trans(bank[:, dc * 128:(dc + 1) * 128],
                            ysTb[:, dc * CH + j * 128: dc * CH + (j + 1) * 128], idb)
                k.copy("vector", ybuf, bank[:, 0:1024])
                post_ln_add(k, "L0", s, tiles[j], ybuf)


def ones_bf(k):
    if "ones_bf" not in k._sb:
        ap = k.sb("ones_bf", [128, 128], BF16)
        k.memset("vector", ap, 1.0)
        return ap
    return k._sb["ones_bf"]


def emit_L0_ffn(k):
    wdqt = k.tw("L0f_wdq")
    laT = k.tw("L0f_laT2")
    lbT = k.tw("L0f_lbT2")
    bias_in = k.inp("L0f_biases", [6, D, 1], F32)
    gam_in = k.inp("L0f_gam", [3, D, 1], F32)
    bet_in = k.inp("L0f_bet", [3, D, 1], F32)
    idb = ident_bf(k)
    ones = ones_bf(k)

    bias_sb = k.sb("L0f_bias", [128, 48], F32)
    gam_sb = k.sb("L0f_gamc", [128, 24], F32)
    bet_sb = k.sb("L0f_betc", [128, 24], F32)
    for l in range(6):
        for oc in range(8):
            k.dma("sync", bias_sb[:, l * 8 + oc: l * 8 + oc + 1],
                  bias_in[l, oc * 128:(oc + 1) * 128, :])
    for fb in range(3):
        for oc in range(8):
            k.dma("sync", gam_sb[:, fb * 8 + oc: fb * 8 + oc + 1],
                  gam_in[fb, oc * 128:(oc + 1) * 128, :])
            k.dma("sync", bet_sb[:, fb * 8 + oc: fb * 8 + oc + 1],
                  bet_in[fb, oc * 128:(oc + 1) * 128, :])

    hT = k.sb("L0f_hT", [128, 8 * T], BF16)
    y1 = k.sb("L0f_y1", [128, 8 * T], BF16)
    wdq = k.sb("L0f_wdq", [128, 8 * D], BF16)
    la_sb = k.sb("L0f_la", [128, 8 * 32], BF16)
    lb_sb = k.sb("L0f_lb", [32, D], BF16)
    u_sb = k.sb("L0f_u", [32, 512], BF16)
    sq_sb = k.sb("L0f_sq", [128, 8 * 512], BF16)
    muB = k.sb("L0f_muB", [128, 512], F32)
    musqB = k.sb("L0f_musqB", [128, 512], F32)
    varB = k.sb("L0f_varB", [128, 512], F32)
    AB = k.sb("L0f_AB", [128, 512], F32)
    BB = k.sb("L0f_BB", [128, 512], F32)
    tmpB = k.sb("L0f_tmpB", [128, 512], F32)
    ybuf = k.sb("at_ybuf", [128, D], BF16)
    epsc = k.constcol("c_epsln", EPS_LN)

    def load_qlora(l):
        k.dma("sync", wdq.rearrange("p (dc c) -> p dc c", c=D),
              wdqt[l].rearrange("(dc p) c -> p dc c", p=128))
        k.dma("sync", la_sb.rearrange("p (dc r) -> p dc r", r=32),
              laT[l * D:(l + 1) * D, :].rearrange("(dc p) r -> p dc r", p=128))
        k.dma("sync", lb_sb, lbT[l * 32:(l + 1) * 32, :])

    def qlora(l, rhs, out_mode, fb):
        load_qlora(l)
        for tc in range(4):
            ub = k.bank(5)
            for dc in range(8):
                k.mm(ub[0:32, :], la_sb[:, dc * 32:(dc + 1) * 32],
                     rhs[:, dc * T + tc * 512: dc * T + (tc + 1) * 512],
                     start=(dc == 0), stop=(dc == 7))
            k.copy("vector", u_sb, ub[0:32, :])
            for oc in range(8):
                b = k.bank(2 * (oc % 4))
                for dc in range(8):
                    k.mm(b, wdq[:, dc * D + oc * 128: dc * D + (oc + 1) * 128],
                         rhs[:, dc * T + tc * 512: dc * T + (tc + 1) * 512],
                         start=(dc == 0), stop=False)
                k.mm(b, lb_sb[:, oc * 128:(oc + 1) * 128], u_sb,
                     start=False, stop=True)
                dst = slice(oc * T + tc * 512, oc * T + (tc + 1) * 512)
                bc = bias_sb[:, l * 8 + oc: l * 8 + oc + 1]
                if out_mode == "y1":
                    k.ts("vector", y1[:, dst], b, bc, ALU.add)
                else:
                    k.stt("vector", hT[:, dst], b, bc, hT[:, dst], ALU.add, ALU.add)

    for s in range(SEQ):
        pre_ln_xT(k, "L0f", s, list(range(NT)), hT, bank0=6)
        for fb in range(_L0F_NFB):
            qlora(2 * fb, hT, "y1", fb)
            for tc in range(4):
                for oc in range(8):
                    sl = slice(oc * T + tc * 512, oc * T + (tc + 1) * 512)
                    k.tt("vector", sq_sb[:, oc * 512:(oc + 1) * 512], y1[:, sl], y1[:, sl], ALU.mult)
                bS, bQ = k.bank(1), k.bank(3)
                for oc in range(8):
                    sl = slice(oc * T + tc * 512, oc * T + (tc + 1) * 512)
                    k.mm(bS, ones, y1[:, sl], start=(oc == 0), stop=(oc == 7))
                for oc in range(8):
                    k.mm(bQ, ones, sq_sb[:, oc * 512:(oc + 1) * 512],
                         start=(oc == 0), stop=(oc == 7))
                k.ts("vector", muB, bS, 1.0 / D, ALU.mult)
                k.tt("vector", musqB, muB, muB, ALU.mult)
                k.stt("vector", varB, bQ, 1.0 / D, musqB, ALU.mult, ALU.subtract)
                k.act("scalar", varB, varB, AF.Ln, bias=epsc)
                k.act("scalar", AB, varB, AF.Exp, scale=-0.5)
                k.stt("vector", BB, muB, -1.0, AB, ALU.mult, ALU.mult)
                for oc in range(8):
                    sl = slice(oc * T + tc * 512, oc * T + (tc + 1) * 512)
                    k.tt("vector", tmpB, y1[:, sl], AB, ALU.mult)
                    k.tt("vector", tmpB, tmpB, BB, ALU.add)
                    k.act("scalar", y1[:, sl], tmpB, AF.Gelu,
                          bias=bet_sb[:, fb * 8 + oc: fb * 8 + oc + 1],
                          scale=gam_sb[:, fb * 8 + oc: fb * 8 + oc + 1])
            qlora(2 * fb + 1, y1, "resid", fb)
        for t in range(NT):
            bank = k.bank_bf(5 + 2 * (t % 2))
            for dc in range(8):
                k.trans(bank[:, dc * 128:(dc + 1) * 128],
                        hT[:, dc * T + t * 128: dc * T + (t + 1) * 128], idb)
            k.copy("vector", ybuf, bank[:, 0:1024])
            post_ln_add(k, "L0f", s, t, ybuf)



def attn_pair(k, s, pair, QpT, KpT, Vp, OT, causal):
    ones = ones_bf(k)
    cmT = k.sb("c_cmaskT", [128, 2048], BF16)
    PT = k.sb("at_P", [128, 2048], BF16)
    rdn = k.sb("at_rdn", [128, 1024], F32)
    drow = k.sb("at_drow", [1, 512], BF16)

    for qst in range(4):
        nk = 4 * qst + 4 if causal else 16
        qsl = slice(qst * 512, (qst + 1) * 512)

        def smm(kc):
            sl = k.bank2(kc % 3)
            k.mm(sl[:, 0:512], KpT[0:64, kc * 128:(kc + 1) * 128],
                 QpT[0:64, qsl], start=True, stop=True)
            k.mm(sl[:, 512:1024], KpT[64:128, kc * 128:(kc + 1) * 128],
                 QpT[64:128, qsl], start=True, stop=True, tp=(64, 0))

        def body(kc):
            slot = PT[:, (kc % 2) * 1024:(kc % 2 + 1) * 1024]
            k.act("scalar", slot, k.bank2(kc % 3), AF.Exp, scale=0.125)
            if causal and kc >= 4 * qst:
                r = kc - 4 * qst
                for h in range(2):
                    k.tt("vector", slot[:, h * 512:(h + 1) * 512],
                         slot[:, h * 512:(h + 1) * 512],
                         cmT[:, r * 512:(r + 1) * 512], ALU.mult)
            for h in range(2):
                k.mm(k.bank2(3)[0:65, h * 512:(h + 1) * 512],
                     Vp[:, kc * 130 + h * 65: kc * 130 + (h + 1) * 65],
                     slot[:, h * 512:(h + 1) * 512],
                     start=(kc == 0), stop=(kc == nk - 1))

        for kc in range(min(2, nk)):
            smm(kc)
        for kc in range(2, nk):
            smm(kc)
            body(kc - 2)
        for kc in range(max(0, nk - 2), nk):
            body(kc)
        bb = k.bank2(0)
        for h in range(2):
            k.copy("vector", drow, k.bank2(3)[64:65, h * 512:(h + 1) * 512])
            k.mm(bb[0:64, h * 512:(h + 1) * 512], ones[0:1, 0:64], drow,
                 start=True, stop=True)
        k.recip(rdn[0:64, :], bb[0:64, :])
        for h in range(2):
            k.stt("vector", OT[h * 64:(h + 1) * 64, pair * 2048 + qst * 512: pair * 2048 + (qst + 1) * 512],
                  k.bank2(3)[0:64, h * 512:(h + 1) * 512], 1.0,
                  rdn[0:64, h * 512:(h + 1) * 512], ALU.bypass, ALU.mult)


def emit_L1_attn(k):
    Wq = k.tw("L1_Wq")
    Wkvd = k.tw("L1_Wkvd")
    Wku = k.tw("L1_Wku")
    Wvu = k.tw("L1_Wvu")
    Wout = k.tw("L1_Wout")
    idb = ident_bf(k)
    xT = k.sb("at_xT", [128, 8 * T], BF16)
    OT = k.sb("at_OT", [128, 8 * T], BF16)
    lat = k.sb("L1_lat", [128, 2 * T], BF16)
    QpT = k.sb("at_QT", [128, T], BF16)
    KpT = k.sb("at_KT", [128, T], BF16)
    Vp = k.sb("at_V", [128, 2080], BF16)
    k.memset("vector", Vp, 1.0)
    wq_sl = k.sb("at_wq", [128, 1024], BF16)
    wk_sl = k.sb("at_wk", [128, 256], BF16)
    wv_sl = k.sb("at_wv", [128, 256], BF16)
    wkvd = k.sb("L1_wkvd", [128, 2048], BF16)
    ybuf = k.sb("at_ybuf", [128, D], BF16)

    for s in range(SEQ):
        pre_ln_xT(k, "at", s, list(range(NT)), xT, bank0=6)
        k.dma("sync", wkvd.rearrange("p (dc c) -> p dc c", c=256),
              Wkvd.rearrange("(dc p) c -> p dc c", p=128))
        for cc in range(2):
            for tc in range(4):
                b = k.bank(2 * (cc % 2))
                for dc in range(8):
                    k.mm(b, wkvd[:, dc * 256 + cc * 128: dc * 256 + (cc + 1) * 128],
                         xT[:, dc * T + tc * 512: dc * T + (tc + 1) * 512],
                         start=(dc == 0), stop=(dc == 7))
                k.copy("vector", lat[:, cc * T + tc * 512: cc * T + (tc + 1) * 512], b)
        for pair in range(8):
            k.dma("sync", wq_sl.rearrange("p (dc c) -> p dc c", c=128),
                  Wq[:, pair * 128:(pair + 1) * 128].rearrange("(dc p) c -> p dc c", p=128))
            k.dma("sync", wk_sl.rearrange("p (cc c) -> p cc c", c=128),
                  Wku[:, pair * 128:(pair + 1) * 128].rearrange("(cc p) c -> p cc c", p=128))
            k.dma("sync", wv_sl.rearrange("p (cc c) -> p cc c", c=128),
                  Wvu[:, pair * 128:(pair + 1) * 128].rearrange("(cc p) c -> p cc c", p=128))
            for tc in range(4):
                bq = k.bank(0)
                for dc in range(8):
                    k.mm(bq, wq_sl[:, dc * 128:(dc + 1) * 128],
                         xT[:, dc * T + tc * 512: dc * T + (tc + 1) * 512],
                         start=(dc == 0), stop=(dc == 7))
                k.copy("vector", QpT[:, tc * 512:(tc + 1) * 512], bq)
                bk = k.bank(2)
                for cc in range(2):
                    k.mm(bk, wk_sl[:, cc * 128:(cc + 1) * 128],
                         lat[:, cc * T + tc * 512: cc * T + (tc + 1) * 512],
                         start=(cc == 0), stop=(cc == 1))
                k.copy("vector", KpT[:, tc * 512:(tc + 1) * 512], bk)
            for tt in range(NT):
                bv = k.bank(4)
                for cc in range(2):
                    k.mm(bv[:, 0:128], lat[:, cc * T + tt * 128: cc * T + (tt + 1) * 128],
                         wv_sl[:, cc * 128:(cc + 1) * 128],
                         start=(cc == 0), stop=(cc == 1))
                k.copy("vector", Vp[:, tt * 130: tt * 130 + 64], bv[:, 0:64])
                k.copy("vector", Vp[:, tt * 130 + 65: tt * 130 + 129], bv[:, 64:128])
            attn_pair(k, s, pair, QpT, KpT, Vp, OT, causal=True)
        wo = k.sb("at_wo", [128, 8192], BF16)
        k.dma("sync", wo.rearrange("p (pc c) -> p pc c", c=1024),
              Wout.rearrange("(pc p) c -> p pc c", p=128))
        for tt in range(NT):
            for half in range(2):
                b = k.bank(half)
                for pc in range(8):
                    k.mm(b, OT[:, pc * 2048 + tt * 128: pc * 2048 + (tt + 1) * 128],
                         wo[:, pc * 1024 + half * 512: pc * 1024 + (half + 1) * 512],
                         start=(pc == 0), stop=(pc == 7))
                k.copy("vector", ybuf[:, half * 512:(half + 1) * 512], b)
            post_ln_add(k, "at", s, tt, ybuf)


def emit_L1_ffn(k):
    wrT = k.tw("L1f_wrT")
    cent = k.inp("L1f_cent", [16, 128], F32)
    temp = k.inp("L1f_temp", [1, 1], F32)
    IE = k.tw("L1f_ie2")
    OE = k.tw("L1f_oe2")
    idb = ident_bf(k)

    xT = k.sb("at_xT", [128, 8 * T], BF16)
    y = k.sb("f1_y", [128, NT * D], BF16)
    hT = k.sb("f1_hT", [128, 4 * T], BF16)
    wr = k.sb("f1_wr", [128, 1024], BF16)
    ie_sb = k.sb("f1_ie", [128, 8 * 512], BF16)
    oe_sb = k.sb("f1_oe", [128, 256], BF16)
    gates = k.sb("f1_gates", [128, NT * 16], F32)
    z_sb = k.sb("f1_z", [128, 128], F32)
    zsq = k.sb("f1_zsq", [128, 128], BF16)
    zbf = k.sb("f1_zbf", [128, 128], BF16)
    sl_sb = k.sb("f1_sl", [128, 16], F32)
    ge1 = k.sb("f1_ge1", [128, 16], F32)
    slx = k.sb("f1_slx", [128, 16], F32)
    maskb = k.sb("f1_mask", [128, 16], F32)
    pe = k.sb("f1_pe", [128, 16], F32)
    cent_sb = k.sb("f1_cent", [16, 128], F32)
    cnb = k.sb("f1_cnb", [16, 128], BF16)
    cnT = k.sb("f1_cnT", [128, 16], BF16)
    csq = k.sb("f1_csq", [16, 128], BF16)
    n2c = k.col("f1_n2")
    rnc = k.col("f1_rn")
    cn2 = k.sb("f1_cn2", [16, 1], F32)
    m1 = k.col("f1_m1")
    negm1 = k.col("f1_negm1")
    m2 = k.col("f1_m2")
    den = k.col("f1_den")
    rden = k.col("f1_rden")
    fcol = k.col("f1_f")
    tcol = k.col("f1_t")

    k.dma("sync", cent_sb, cent)
    k.act("scalar", csq, cent_sb, AF.Square, accum=cn2[0:16, :])
    k.ts("vector", cn2[0:16, :], cn2[0:16, :], 1e-30, ALU.max)
    k.act("scalar", cn2[0:16, :], cn2[0:16, :], AF.Ln)
    k.act("scalar", cn2[0:16, :], cn2[0:16, :], AF.Exp, scale=-0.5)
    k.ts("vector", cnb, cent_sb, cn2[0:16, 0:1], ALU.mult)
    bT = k.bank_bf(1)
    k.trans(bT[:, 0:16], cnb, idb[0:16, 0:16])
    k.copy("vector", cnT, bT[:, 0:16])
    k.dma("sync", tcol, temp.broadcast_to([128, 1]))
    k.act("scalar", fcol, tcol, AF.Exp)
    k.ts("vector", fcol, fcol, 0.08838834764831845, ALU.mult)
    k.dma("sync", wr.rearrange("p (dc c) -> p dc c", c=128),
          wrT.rearrange("(dc p) c -> p dc c", p=128))

    for s in range(SEQ):
        pre_ln_xT(k, "f1", s, list(range(NT)), xT, bank0=6)
        for tt in range(NT):
            bz = k.bank(0)
            for dc in range(8):
                k.mm(bz[:, 0:128], xT[:, dc * T + tt * 128: dc * T + (tt + 1) * 128],
                     wr[:, dc * 128:(dc + 1) * 128], start=(dc == 0), stop=(dc == 7))
            k.copy("vector", z_sb, bz[:, 0:128])
            k.stt("vector", zsq, z_sb, 1.0, z_sb, ALU.mult, ALU.mult, accum=n2c)
            k.ts("vector", n2c, n2c, 1e-30, ALU.max)
            k.act("scalar", n2c, n2c, AF.Ln)
            k.act("scalar", rnc, n2c, AF.Exp, scale=-0.5)
            k.ts("vector", zbf, z_sb, rnc, ALU.mult)
            bzt = k.bank_bf(1)
            k.trans(bzt[:, 0:128], zbf, idb)
            k.copy("vector", zbf, bzt[:, 0:128])
            bl = k.bank(2)
            k.mm(bl[:, 0:16], zbf, cnT, start=True, stop=True)
            k.ts("vector", sl_sb, bl[:, 0:16], fcol, ALU.mult)
            k.red("vector", m1, sl_sb, ALU.max)
            k.ts("vector", ge1, sl_sb, m1, ALU.is_ge)
            k.stt("vector", slx, ge1, -1e30, sl_sb, ALU.mult, ALU.add)
            k.red("vector", m2, slx, ALU.max)
            k.ts("vector", maskb, sl_sb, m2, ALU.is_ge)
            k.ts("vector", negm1, m1, -1.0, ALU.mult)
            k.act("scalar", pe, sl_sb, AF.Exp, bias=negm1, accum=den)
            k.recip(rden, den)
            k.tt("vector", pe, pe, maskb, ALU.mult)
            k.ts("vector", gates[:, tt * 16:(tt + 1) * 16], pe, rden, ALU.mult)
        for e in range(16):
            ie_sb = k.sb("wring1", [128, 8 * 512], BF16) if e % 2 else k.sb("f1_ie", [128, 8 * 512], BF16)
            k.dma("sync", ie_sb.rearrange("p (dc c) -> p dc c", c=512),
                  IE[e * D:(e + 1) * D, :].rearrange("(dc p) c -> p dc c", p=128))
            k.dma("sync", oe_sb.rearrange("p (hc c) -> p hc c", c=64),
                  OE[e * 512:(e + 1) * 512, :].rearrange("(hc p) c -> p hc c", p=128))
            for hc in range(4):
                for tc in range(4):
                    bh = k.bank(2 * (tc % 2))
                    for dc in range(8):
                        k.mm(bh, ie_sb[:, dc * 512 + hc * 128: dc * 512 + (hc + 1) * 128],
                             xT[:, dc * T + tc * 512: dc * T + (tc + 1) * 512],
                             start=(dc == 0), stop=(dc == 7))
                    k.act("scalar", hT[:, hc * T + tc * 512: hc * T + (tc + 1) * 512],
                          bh, AF.Gelu)
            for tt in range(NT):
                bo = k.bank(4 + 2 * (tt % 2))
                for hc in range(4):
                    k.mm(bo[:, 0:64], hT[:, hc * T + tt * 128: hc * T + (tt + 1) * 128],
                         oe_sb[:, hc * 64:(hc + 1) * 64],
                         start=(hc == 0), stop=(hc == 3))
                k.ts("vector", y[:, tt * D + e * 64: tt * D + (e + 1) * 64],
                     bo[:, 0:64], gates[:, tt * 16 + e: tt * 16 + e + 1], ALU.mult)
        for tt in range(NT):
            post_ln_add(k, "f1", s, tt, y[:, tt * D:(tt + 1) * D])


MAGIC = 12582912.0


def quant_mean(k, tag, W, ncols, scale_col, rscale_col, nelem):
    acc = k.col(f"{tag}_acc")
    tmp = k.col(f"{tag}_tmp")
    one = k.col(f"{tag}_one")
    row = k.sb(f"{tag}_row", [1, 128], F32)
    scr = k.idram(f"{tag}_scr", [128, 1], F32)
    scr2 = k.idram(f"{tag}_scr2", [1, 1], F32)
    nch = ncols // 1024
    for pc in range(8):
        for cc in range(nch):
            stg = k.sb(f"ldstage{(pc + cc) % 2}", [128, 1024], F32)
            k.dma("sync", stg, W[pc * 128:(pc + 1) * 128, cc * 1024:(cc + 1) * 1024])
            if pc == 0 and cc == 0:
                k.red("vector", acc, stg, ALU.add, absval=True)
            else:
                k.red("vector", tmp, stg, ALU.add, absval=True)
                k.tt("vector", acc, acc, tmp, ALU.add)
    k.dma("sync", scr, acc)
    k.dma("sync", row, scr.rearrange("a b -> b a"))
    k.red("vector", one[0:1, :], row, ALU.add)
    k.ts("vector", one[0:1, :], one[0:1, :], 1.0 / nelem, ALU.mult, 1e-8, ALU.max)
    k.dma("sync", scr2, one[0:1, :])
    k.dma("sync", scale_col, scr2.broadcast_to([128, 1]))
    k.act("scalar", rscale_col, scale_col, AF.Ln)
    k.act("scalar", rscale_col, rscale_col, AF.Exp, scale=-1.0)


def quant_slice(k, dst, src_dram, stage_tag, scale_col, rscale_col, qt_tag="L2_qt"):
    p, n = src_dram.shape
    stg = k.sb(stage_tag, [128, 1024], F32)
    qt = k.sb(qt_tag, [128, 1024], F32)
    k.dma("sync", stg[0:p, 0:n], src_dram)
    k.ts("vector", qt[0:p, 0:n], stg[0:p, 0:n], rscale_col, ALU.mult, MAGIC, ALU.add)
    k.ts("vector", qt[0:p, 0:n], qt[0:p, 0:n], MAGIC, ALU.subtract, -2.0, ALU.max)
    k.ts("vector", dst, qt[0:p, 0:n], 2.0, ALU.min, scale_col, ALU.mult)


def emit_L2_attn(k):
    waq = k.tw("L2_waq")
    wpq = k.tw("L2_wpq")
    xT = k.sb("at_xT", [128, 8 * T], BF16)
    OT = k.sb("at_OT", [128, 8 * T], BF16)
    QpT = k.sb("at_QT", [128, T], BF16)
    KpT = k.sb("at_KT", [128, T], BF16)
    Vp = k.sb("at_V", [128, 2080], BF16)
    k.memset("vector", Vp, 1.0)
    wq_sl = k.sb("at_wq", [128, 1024], BF16)
    wk_sl = k.sb("L2_wk", [128, 1024], BF16)
    wv_sl = k.sb("L2_wv", [128, 1024], BF16)
    ybuf = k.sb("at_ybuf", [128, D], BF16)

    for s in range(SEQ):
        pre_ln_xT(k, "at", s, list(range(NT)), xT, bank0=6)
        for pair in range(8):
            k.dma("sync", wq_sl.rearrange("p (dc c) -> p dc c", c=128),
                  waq[:, pair * 128:(pair + 1) * 128].rearrange("(dc p) c -> p dc c", p=128))
            k.dma("sync", wk_sl.rearrange("p (dc c) -> p dc c", c=128),
                  waq[:, D + pair * 128: D + (pair + 1) * 128].rearrange("(dc p) c -> p dc c", p=128))
            k.dma("sync", wv_sl.rearrange("p (dc c) -> p dc c", c=128),
                  waq[:, 2 * D + pair * 128: 2 * D + (pair + 1) * 128].rearrange("(dc p) c -> p dc c", p=128))
            for tc in range(4):
                bq = k.bank(0)
                bk = k.bank(2)
                for dc in range(8):
                    k.mm(bq, wq_sl[:, dc * 128:(dc + 1) * 128],
                         xT[:, dc * T + tc * 512: dc * T + (tc + 1) * 512],
                         start=(dc == 0), stop=(dc == 7))
                for dc in range(8):
                    k.mm(bk, wk_sl[:, dc * 128:(dc + 1) * 128],
                         xT[:, dc * T + tc * 512: dc * T + (tc + 1) * 512],
                         start=(dc == 0), stop=(dc == 7))
                k.copy("vector", QpT[:, tc * 512:(tc + 1) * 512], bq)
                k.copy("vector", KpT[:, tc * 512:(tc + 1) * 512], bk)
            for tt in range(NT):
                bv = k.bank(4)
                for dc in range(8):
                    k.mm(bv[:, 0:128], xT[:, dc * T + tt * 128: dc * T + (tt + 1) * 128],
                         wv_sl[:, dc * 128:(dc + 1) * 128],
                         start=(dc == 0), stop=(dc == 7))
                k.copy("vector", Vp[:, tt * 130: tt * 130 + 64], bv[:, 0:64])
                k.copy("vector", Vp[:, tt * 130 + 65: tt * 130 + 129], bv[:, 64:128])
            attn_pair(k, s, pair, QpT, KpT, Vp, OT, causal=True)
        wo = k.sb("at_wo", [128, 8192], BF16)
        k.dma("sync", wo.rearrange("p (pc c) -> p pc c", c=1024),
              wpq.rearrange("(pc p) c -> p pc c", p=128))
        for tt in range(NT):
            for half in range(2):
                b = k.bank(half)
                for pc in range(8):
                    k.mm(b, OT[:, pc * 2048 + tt * 128: pc * 2048 + (tt + 1) * 128],
                         wo[:, pc * 1024 + half * 512: pc * 1024 + (half + 1) * 512],
                         start=(pc == 0), stop=(pc == 7))
                k.copy("vector", ybuf[:, half * 512:(half + 1) * 512], b)
            post_ln_add(k, "at", s, tt, ybuf)


def emit_L2_ffn(k):
    Win = k.tw("L2f_win")
    binc = k.inp("L2f_binc", [64, 1], F32)
    w1c = k.inp("L2f_w1c", [D, 1], F32)
    b1c = k.inp("L2f_b1c", [D, 1], F32)
    w2s = k.tw("L2f_w2s")
    b2c = k.inp("L2f_b2c", [64, 1], F32)
    Wout = k.tw("L2f_wout")
    bout = k.tw("L2f_bout")
    repT = k.inp("cin_repT", [64, 1024], BF16)

    xT = k.sb("at_xT", [128, 8 * T], BF16)
    h_sb = k.sb("f2_h", [128, 8 * 512], BF16)
    win_sb = k.sb("f2_win", [128, 512], BF16)
    w2s_sb = k.sb("f2_w2s", [128, 512], BF16)
    rep_sb = k.sb("f2_rep", [64, 1024], BF16)
    wout_sb = k.sb("f2_wout", [64, D], BF16)
    bout_sb = k.sb("f2_bout", [1, D], BF16)
    s_bf = k.sb("f2_s", [64, 512], BF16)
    g_bf = k.sb("f2_g", [64, 512], BF16)
    w1cols = k.sb("f2_w1c", [128, 8], F32)
    b1cols = k.sb("f2_b1c", [128, 8], F32)
    bin_col = k.sb("f2_bin", [64, 1], F32)
    b2_col = k.sb("f2_b2", [64, 1], F32)
    ybuf = k.sb("at_ybuf", [128, D], BF16)
    ones = ones_bf(k)

    k.dma("sync", rep_sb, repT)
    k.dma("sync", wout_sb, Wout)
    k.dma("sync", bout_sb, bout)
    k.dma("sync", win_sb.rearrange("p (dc c) -> p dc c", c=64),
          Win.rearrange("(dc p) c -> p dc c", p=128))
    k.dma("sync", w2s_sb.rearrange("p (dc c) -> p dc c", c=64),
          w2s.rearrange("(dc p) c -> p dc c", p=128))
    for dc in range(8):
        k.dma("sync", w1cols[:, dc:dc + 1], w1c[dc * 128:(dc + 1) * 128, :])
        k.dma("sync", b1cols[:, dc:dc + 1], b1c[dc * 128:(dc + 1) * 128, :])
    k.dma("sync", bin_col, binc)
    k.dma("sync", b2_col, b2c)

    for s in range(SEQ):
        pre_ln_xT(k, "f2", s, list(range(NT)), xT, bank0=6)
        for tc in range(4):
            b0 = k.bank(0)
            for dc in range(8):
                k.mm(b0[0:64, :], win_sb[:, dc * 64:(dc + 1) * 64],
                     xT[:, dc * T + tc * 512: dc * T + (tc + 1) * 512],
                     start=(dc == 0), stop=(dc == 7))
            k.ts("vector", s_bf, b0[0:64, :], bin_col[0:64, 0:1], ALU.add)
            for mi in range(8):
                b1_ = k.bank(2 + 2 * (mi % 2))
                k.mm(b1_, rep_sb[:, mi * 128:(mi + 1) * 128], s_bf,
                     start=True, stop=True)
                k.ts("vector", h_sb[:, mi * 512:(mi + 1) * 512], b1_,
                     w1cols[:, mi:mi + 1], ALU.mult, b1cols[:, mi:mi + 1], ALU.add)
                k.ts("vector", h_sb[:, mi * 512:(mi + 1) * 512],
                     h_sb[:, mi * 512:(mi + 1) * 512], 0.0, ALU.max)
            b3 = k.bank(6)
            for mi in range(8):
                k.mm(b3[0:64, :], w2s_sb[:, mi * 64:(mi + 1) * 64],
                     h_sb[:, mi * 512:(mi + 1) * 512],
                     start=(mi == 0), stop=(mi == 7))
            k.ts("vector", g_bf, b3[0:64, :], b2_col[0:64, 0:1], ALU.add)
            for j in range(4):
                tt = tc * 4 + j
                for half in range(2):
                    bo = k.bank(1 + 2 * half)
                    k.mm(bo, g_bf[:, j * 128:(j + 1) * 128],
                         wout_sb[:, half * 512:(half + 1) * 512],
                         start=True, stop=False)
                    k.mm(bo, ones[0:1, 0:128], bout_sb[0:1, half * 512:(half + 1) * 512],
                         start=False, stop=True)
                    k.copy("vector", ybuf[:, half * 512:(half + 1) * 512], bo)
                post_ln_add(k, "f2", s, tt, ybuf)


def emit_L3_attn(k):
    Wqkv = k.tw("L3_wqkv")
    Wp = k.tw("L3_wp")
    bproj = k.tw("L3_bproj")
    qs_in = k.inp("L3_qs", [128, 1], F32)
    ks_in = k.inp("L3_ks", [128, 1], F32)
    blk = k.sb("c_blkones", [128, 128], BF16)
    xT = k.sb("at_xT", [128, 8 * T], BF16)
    OT = k.sb("at_OT", [128, 8 * T], BF16)
    QpT = k.sb("at_QT", [128, T], BF16)
    KpT = k.sb("at_KT", [128, T], BF16)
    Vp = k.sb("at_V", [128, 2080], BF16)
    k.memset("vector", Vp, 1.0)
    wq_sl = k.sb("at_wq", [128, 1024], BF16)
    wk_sl = k.sb("L2_wk", [128, 1024], BF16)
    wv_sl = k.sb("L2_wv", [128, 1024], BF16)
    sqb = k.sb("at_sqr", [128, 1024], BF16)
    rr = k.sb("f3_rr", [128, 512], F32)
    ybuf = k.sb("at_ybuf", [128, D], BF16)
    qs = k.col("L3_qsc")
    ks = k.col("L3_ksc")
    bproj_sb = k.sb("L3_bp", [1, D], BF16)
    eps8 = k.constcol("c_eps8", 1e-8)
    negln8 = k.constcol("c_negln8", -2.0794415416798357)
    ones = ones_bf(k)
    k.dma("sync", qs, qs_in)
    k.dma("sync", ks, ks_in)
    k.dma("sync", bproj_sb, bproj)

    def rmsnorm_evac(bq, dstsl, scol):
        sq_sl = sqb[:, 0:512]
        k.act("scalar", sq_sl, bq, AF.Square)
        bN = k.bank(5)
        k.mm(bN, blk, sq_sl, start=True, stop=True)
        k.act("scalar", rr, bN, AF.Ln)
        k.act("scalar", rr, rr, AF.Exp, scale=0.5, bias=negln8)
        k.ts("vector", rr, rr, 1e-8, ALU.add)
        k.recip(rr, rr)
        k.stt("vector", dstsl, bq, scol, rr, ALU.mult, ALU.mult)

    for s in range(SEQ):
        pre_ln_xT(k, "at", s, list(range(NT)), xT, bank0=6)
        for pair in range(8):
            k.dma("sync", wq_sl.rearrange("p (dc c) -> p dc c", c=128),
                  Wqkv[:, pair * 128:(pair + 1) * 128].rearrange("(dc p) c -> p dc c", p=128))
            k.dma("sync", wk_sl.rearrange("p (dc c) -> p dc c", c=128),
                  Wqkv[:, D + pair * 128: D + (pair + 1) * 128].rearrange("(dc p) c -> p dc c", p=128))
            k.dma("sync", wv_sl.rearrange("p (dc c) -> p dc c", c=128),
                  Wqkv[:, 2 * D + pair * 128: 2 * D + (pair + 1) * 128].rearrange("(dc p) c -> p dc c", p=128))
            for tc in range(4):
                bq = k.bank(0)
                bk = k.bank(2)
                for dc in range(8):
                    k.mm(bq, wq_sl[:, dc * 128:(dc + 1) * 128],
                         xT[:, dc * T + tc * 512: dc * T + (tc + 1) * 512],
                         start=(dc == 0), stop=(dc == 7))
                for dc in range(8):
                    k.mm(bk, wk_sl[:, dc * 128:(dc + 1) * 128],
                         xT[:, dc * T + tc * 512: dc * T + (tc + 1) * 512],
                         start=(dc == 0), stop=(dc == 7))
                rmsnorm_evac(bq, QpT[:, tc * 512:(tc + 1) * 512], qs)
                rmsnorm_evac(bk, KpT[:, tc * 512:(tc + 1) * 512], ks)
            for tt in range(NT):
                bv = k.bank(4)
                for dc in range(8):
                    k.mm(bv[:, 0:128], xT[:, dc * T + tt * 128: dc * T + (tt + 1) * 128],
                         wv_sl[:, dc * 128:(dc + 1) * 128],
                         start=(dc == 0), stop=(dc == 7))
                k.copy("vector", Vp[:, tt * 130: tt * 130 + 64], bv[:, 0:64])
                k.copy("vector", Vp[:, tt * 130 + 65: tt * 130 + 129], bv[:, 64:128])
            attn_pair(k, s, pair, QpT, KpT, Vp, OT, causal=False)
        wo = k.sb("at_wo", [128, 8192], BF16)
        k.dma("sync", wo.rearrange("p (pc c) -> p pc c", c=1024),
              Wp.rearrange("(pc p) c -> p pc c", p=128))
        for tt in range(NT):
            for half in range(2):
                b = k.bank(half)
                for pc in range(8):
                    k.mm(b, OT[:, pc * 2048 + tt * 128: pc * 2048 + (tt + 1) * 128],
                         wo[:, pc * 1024 + half * 512: pc * 1024 + (half + 1) * 512],
                         start=(pc == 0), stop=False)
                k.mm(b, ones[0:1, 0:128], bproj_sb[0:1, half * 512:(half + 1) * 512],
                     start=False, stop=True)
                k.copy("vector", ybuf[:, half * 512:(half + 1) * 512], b)
            post_ln_add(k, "at", s, tt, ybuf)


def emit_L3_ffn(k):
    gw = k.tw("L3f_gw")
    rs_in = k.inp("L3f_rs", [1, 1], F32)
    W1 = k.tw("L3f_w12")
    b1f = k.tw("L3f_b1")
    W2 = k.tw("L3f_w22")
    b2_in = k.tw("L3f_b2")
    Ws1 = k.tw("L3f_ws1")
    bs1_in = k.inp("L3f_bs1", [512, 1], F32)
    Ws2 = k.tw("L3f_ws2")
    bs2_in = k.tw("L3f_bs2")
    idb = ident_bf(k)
    ones = ones_bf(k)

    xT = k.sb("at_xT", [128, 8 * T], BF16)
    y = k.sb("f3_y", [128, NT * D], BF16)
    hsT = k.sb("f3_hsT", [128, 4 * T], BF16)
    ws1_sb = k.sb("f3_ws1", [128, 8 * 512], BF16)
    ws2_sb = k.sb("f3_ws2", [128, 4 * D], BF16)
    w1_sb = k.sb("f3_w1", [128, 8 * 512], BF16)
    w2_sb = k.sb("f3_w2", [128, 4 * D], BF16)
    gw_sb = k.sb("f3_gw", [128, 64], BF16)
    gates = k.sb("f3_gates", [128, NT * 8], F32)
    gb = k.sb("f3_gb", [128, 8], BF16)
    gT = k.sb("f3_gT", [8, 128], BF16)
    b1r = k.sb("f3_b1r", [1, 512], BF16)
    b2_sb = k.sb("f3_b2s", [8, D], BF16)
    bs1c = k.sb("f3_bs1c", [128, 4], F32)
    bs2_sb = k.sb("f3_bs2s", [1, D], BF16)
    hg = k.sb("f3_hg", [128, 512], BF16)
    hgT = k.sb("f3_hgT", [128, 512], BF16)
    sc_sb = k.sb("f3_sc", [128, 8], F32)
    ge1 = k.sb("f3_ge1", [128, 8], F32)
    slx = k.sb("f3_slx", [128, 8], F32)
    maskb = k.sb("f3_mask", [128, 8], F32)
    rs = k.col("f3_rs")
    m1 = k.col("f3_m1")
    m2 = k.col("f3_m2")
    den = k.col("f3_den")
    rden = k.col("f3_rden")

    k.dma("sync", rs, rs_in.broadcast_to([128, 1]))
    k.dma("sync", gw_sb.rearrange("p (dc c) -> p dc c", c=8),
          gw.rearrange("(dc p) c -> p dc c", p=128))
    for hc in range(4):
        k.dma("sync", bs1c[:, hc:hc + 1], bs1_in[hc * 128:(hc + 1) * 128, :])
    k.dma("sync", b2_sb, b2_in)
    k.dma("sync", bs2_sb, bs2_in)

    for s in range(SEQ):
        pre_ln_xT(k, "f3", s, list(range(NT)), xT, bank0=6)
        for tt in range(NT):
            bg = k.bank(5)
            for dc in range(8):
                k.mm(bg[:, 0:8], xT[:, dc * T + tt * 128: dc * T + (tt + 1) * 128],
                     gw_sb[:, dc * 8:(dc + 1) * 8], start=(dc == 0), stop=(dc == 7))
            k.act("scalar", sc_sb, bg[:, 0:8], AF.Exp, scale=-1.0)
            k.ts("vector", sc_sb, sc_sb, 1.0, ALU.add)
            k.recip(sc_sb, sc_sb)
            k.ts("vector", sc_sb, sc_sb, rs, ALU.mult)
            k.red("vector", m1, sc_sb, ALU.max)
            k.ts("vector", ge1, sc_sb, m1, ALU.is_ge)
            k.stt("vector", slx, ge1, -1e30, sc_sb, ALU.mult, ALU.add)
            k.red("vector", m2, slx, ALU.max)
            k.ts("vector", maskb, sc_sb, m2, ALU.is_ge)
            k.tt("vector", den, m1, m2, ALU.add)
            self_recip(k, rden, den)
            k.tt("vector", ge1, sc_sb, maskb, ALU.mult)
            k.ts("vector", gates[:, tt * 8:(tt + 1) * 8], ge1, rden, ALU.mult)
        k.dma("sync", ws1_sb.rearrange("p (dc c) -> p dc c", c=512),
              Ws1.rearrange("(dc p) c -> p dc c", p=128))
        for hc in range(4):
            for tc in range(4):
                bh = k.bank(2 * (tc % 2))
                for dc in range(8):
                    k.mm(bh, ws1_sb[:, dc * 512 + hc * 128: dc * 512 + (hc + 1) * 128],
                         xT[:, dc * T + tc * 512: dc * T + (tc + 1) * 512],
                         start=(dc == 0), stop=(dc == 7))
                k.act("scalar", hsT[:, hc * T + tc * 512: hc * T + (tc + 1) * 512],
                      bh, AF.Gelu, bias=bs1c[:, hc:hc + 1])
        k.dma("sync", ws2_sb.rearrange("p (hc c) -> p hc c", c=D),
              Ws2.rearrange("(hc p) c -> p hc c", p=128))
        for tt in range(NT):
            k.copy("vector", gb, gates[:, tt * 8:(tt + 1) * 8])
            bt = k.bank_bf(4)
            k.trans(bt[0:8, 0:128], gb, idb)
            k.copy("vector", gT, bt[0:8, 0:128])
            for half in range(2):
                bo = k.bank(2 + half)
                for hc in range(4):
                    k.mm(bo, hsT[:, hc * T + tt * 128: hc * T + (tt + 1) * 128],
                         ws2_sb[:, hc * D + half * 512: hc * D + (half + 1) * 512],
                         start=(hc == 0), stop=False)
                k.mm(bo, ones[0:1, 0:128], bs2_sb[0:1, half * 512:(half + 1) * 512],
                     start=False, stop=False)
                k.mm(bo, gT, b2_sb[0:8, half * 512:(half + 1) * 512],
                     start=False, stop=True)
                k.copy("vector", y[:, tt * D + half * 512: tt * D + (half + 1) * 512], bo)
        for e in range(8):
            w1_sb = k.sb("wring1", [128, 8 * 512], BF16) if e % 2 else k.sb("f3_w1", [128, 8 * 512], BF16)
            k.dma("sync", b1r, b1f[0:1, e * 512:(e + 1) * 512])
            k.dma("sync", w1_sb.rearrange("p (dc c) -> p dc c", c=512),
                  W1[e * D:(e + 1) * D, :].rearrange("(dc p) c -> p dc c", p=128))
            k.dma("sync", w2_sb.rearrange("p (hc c) -> p hc c", c=D),
                  W2[e * 512:(e + 1) * 512, :].rearrange("(hc p) c -> p hc c", p=128))
            for tt in range(NT):
                bh = k.bank(2 * (tt % 2))
                for dc in range(8):
                    k.mm(bh, xT[:, dc * T + tt * 128: dc * T + (tt + 1) * 128],
                         w1_sb[:, dc * 512:(dc + 1) * 512],
                         start=(dc == 0), stop=False)
                k.mm(bh, ones[0:1, 0:128], b1r,
                     start=False, stop=True)
                k.act("scalar", hg, bh, AF.Gelu)
                k.ts("vector", hg, hg, gates[:, tt * 8 + e: tt * 8 + e + 1], ALU.mult)
                bt = k.bank_bf(4)
                for j in range(4):
                    k.trans(bt[:, j * 128:(j + 1) * 128], hg[:, j * 128:(j + 1) * 128], idb)
                k.copy("vector", hgT, bt[:, 0:512])
                for half in range(2):
                    bo = k.bank(5 + 2 * half)
                    for hc in range(4):
                        k.mm(bo, hgT[:, hc * 128:(hc + 1) * 128],
                             w2_sb[:, hc * D + half * 512: hc * D + (half + 1) * 512],
                             start=(hc == 0), stop=(hc == 3))
                    ysl = y[:, tt * D + half * 512: tt * D + (half + 1) * 512]
                    k.stt("vector", ysl, bo, 1.0, ysl, ALU.bypass, ALU.add)
        for tt in range(NT):
            post_ln_add(k, "f3", s, tt, y[:, tt * D:(tt + 1) * D])


def self_recip(k, out, in_):
    def f(e, o=out, i=in_):
        return e.reciprocal(o, i)
    k._rec("vector", f, reads=[in_], writes=[out])


BLOCK_EMITTERS = {
    0: emit_L0_attn,
    1: emit_L0_ffn,
    2: emit_L1_attn,
    3: emit_L1_ffn,
    4: emit_L2_attn,
    5: emit_L2_ffn,
    6: emit_L3_attn,
    7: emit_L3_ffn,
}



def build_program(blocks, first, last):
    nc = bass.Bass("TRN2", target_bir_lowering=False)
    k = K(nc)
    if first:
        x_in = k.inp("x_in", [SEQ * T, D], F32)
    else:
        x_in = k.inp("x_in", [SEQ * T, D], BF16)
    st = k.sb("stream", [128, SEQ * NT * D], BF16)
    xr = x_in.rearrange("(n p) d -> p n d", p=128)
    if first:
        for i in range(SEQ * NT):
            stg = k.sb(f"ldstage{i % 2}", [128, D], F32)
            k.dma("sync", stg, xr[:, i, :])
            k.copy("scalar" if i % 2 else "vector", st[:, i * D:(i + 1) * D], stg)
    else:
        for g in range(8):
            q = "sync" if g % 2 else "gpsimd"
            k.dma(q, st.rearrange("p (n d) -> p n d", d=D)[:, g * 4:(g + 1) * 4, :],
                  xr[:, g * 4:(g + 1) * 4, :])
    idb = ident_bf(k)
    k.dma("sync", idb, k.inp("c_ident", [128, 128], BF16))
    k.dma("sync", k.sb("c_blkones", [128, 128], BF16), k.inp("cin_blkones", [128, 128], BF16))
    k.dma("sync", k.sb("c_trimask", [128, 128], BF16), k.inp("cin_trimask", [128, 128], BF16))
    k.dma("sync", k.sb("c_cmaskT", [128, 2048], BF16), k.inp("cin_cmaskT", [128, 2048], BF16))
    emit_weight_prep(k)

    for b in blocks:
        BLOCK_EMITTERS[b](k)

    for tag in _DEBUG_TAPS:
        info = getattr(k, "_sbinfo", {}).get(tag)
        if info is not None:
            ap = k.sb(tag, info[0], info[1])
            o = k.outp("dbg_" + tag, list(ap.shape), ap.dtype)
            k.out_dma("sync", o, ap)

    x_out = k.outp("x_out", [SEQ * T, D], F32 if last else BF16)
    xo = x_out.rearrange("(n p) d -> p n d", p=128)
    for g in range(8):
        q = "gpsimd" if last else ("sync" if g % 2 else "gpsimd")
        k.out_dma(q, xo[:, g * 4:(g + 1) * 4, :],
                  st.rearrange("p (n d) -> p n d", d=D)[:, g * 4:(g + 1) * 4, :])
    k.build()
    return nc, sorted(n for n in k._dram_in if not n.startswith('#'))


def host_prep(inputs):
    w = {}
    c = np.ascontiguousarray
    w["c_ident"] = np.eye(128, dtype=BF)
    w["L0_W_T"] = c(inputs["L0_attn_W"].T)
    w["L0_b"] = c(inputs["L0_attn_b"].reshape(D, 1))
    w["L0f_qT"] = c(inputs["L0_ffn_qweights"].transpose(0, 2, 1))
    w["L0f_scT"] = c(inputs["L0_ffn_scales"].transpose(0, 2, 1))
    w["L0f_laT"] = c(inputs["L0_ffn_lora_a"].transpose(0, 2, 1))
    w["L0f_lbT"] = c(inputs["L0_ffn_lora_b"].transpose(0, 2, 1))
    w["L0f_biases"] = c(inputs["L0_ffn_biases"].reshape(6, D, 1))
    w["L0f_gam"] = c(inputs["L0_ffn_ln_gamma"].reshape(3, D, 1))
    w["L0f_bet"] = c(inputs["L0_ffn_ln_beta"].reshape(3, D, 1))
    blk = np.zeros((128, 128), dtype=BF)
    blk[0:64, 0:64] = 1
    blk[64:128, 64:128] = 1
    w["cin_blkones"] = blk
    w["cin_trimask"] = np.tril(np.ones((128, 128))).astype(BF)
    cmT = np.zeros((128, 4, 512), np.float32)
    for r in range(4):
        for i_ in range(128):
            cmT[i_, r, :] = (128 * r + i_ <= np.arange(512))
    w["cin_cmaskT"] = np.ascontiguousarray(cmT.reshape(128, 2048)).astype(BF)
    w["L1_Wq"] = c(inputs["L1_attn_Wq"].reshape(D, D))
    w["L1_Wkvd"] = c(inputs["L1_attn_Wkv_down"])
    w["L1_Wku"] = c(inputs["L1_attn_Wk_up"].reshape(256, D))
    w["L1_Wvu"] = c(inputs["L1_attn_Wv_up"].reshape(256, D))
    w["L1_Wout"] = c(inputs["L1_attn_Wout"].reshape(D, D))
    w["L1f_wrT"] = c(inputs["L1_ffn_w_router"].T)
    w["L1f_cent"] = c(inputs["L1_ffn_head_centroids"])
    w["L1f_temp"] = c(inputs["L1_ffn_temperature"].reshape(1, 1))
    w["L1f_ie"] = c(inputs["L1_ffn_input_experts"])
    w["L1f_oe"] = c(inputs["L1_ffn_output_experts"])
    w["L2_waT"] = c(inputs["L2_attn_w_attn"].T)
    w["L2_wpT"] = c(inputs["L2_attn_w_proj"].T)
    w["L2f_win"] = c(inputs["L2_ffn_Win"])
    w["L2f_binc"] = c(inputs["L2_ffn_bin_"].reshape(64, 1))
    w["L2f_w1c"] = c(inputs["L2_ffn_W1"].reshape(D, 1))
    w["L2f_b1c"] = c(inputs["L2_ffn_b1"].reshape(D, 1))
    w2s = np.zeros((D, 64), np.float32)
    for m in range(64):
        w2s[m * 16:(m + 1) * 16, m] = inputs["L2_ffn_W2"][m]
    w["L2f_w2s"] = w2s
    w["L2f_b2c"] = c(inputs["L2_ffn_b2"].reshape(64, 1))
    w["L2f_wout"] = c(inputs["L2_ffn_Wout"])
    w["L2f_bout"] = c(inputs["L2_ffn_bout"].reshape(1, D))
    w["L3_wqkv"] = c(inputs["L3_attn_Wqkv"])
    w["L3_wp"] = c(inputs["L3_attn_Wproj"])
    w["L3_bproj"] = c(inputs["L3_attn_bproj"].reshape(1, D))
    w["L3_qs"] = c(np.tile(inputs["L3_attn_q_scale"], 2).reshape(128, 1))
    w["L3_ks"] = c(np.tile(inputs["L3_attn_k_scale"], 2).reshape(128, 1))
    w["L3f_gw"] = c(inputs["L3_ffn_gate_W"])
    w["L3f_rs"] = c(np.asarray(inputs["L3_ffn_route_scale"], np.float32).reshape(1, 1))
    w["L3f_w1"] = c(inputs["L3_ffn_W1"])
    w["L3f_b1"] = c(inputs["L3_ffn_b1"].reshape(1, 8 * 512))
    w["L3f_w2"] = c(inputs["L3_ffn_W2"])
    w["L3f_b2"] = c(inputs["L3_ffn_b2"])
    w["L3f_ws1"] = c(inputs["L3_ffn_Ws1"])
    w["L3f_bs1"] = c(inputs["L3_ffn_bs1"].reshape(512, 1))
    w["L3f_ws2"] = c(inputs["L3_ffn_Ws2"])
    w["L3f_bs2"] = c(inputs["L3_ffn_bs2"].reshape(1, D))
    w["L0f_laT2"] = w["L0f_laT"].reshape(6 * D, 32)
    w["L0f_lbT2"] = w["L0f_lbT"].reshape(6 * 32, D)
    w["L1f_ie2"] = w["L1f_ie"].reshape(16 * D, 512)
    w["L1f_oe2"] = w["L1f_oe"].reshape(16 * 512, 64)
    w["L3f_w12"] = w["L3f_w1"].reshape(8 * D, 512)
    w["L3f_w22"] = w["L3f_w2"].reshape(8 * 512, D)
    rep = np.zeros((64, 1024), dtype=BF)
    for m in range(64):
        rep[m, m * 16:(m + 1) * 16] = 1
    w["cin_repT"] = rep
    return w


def kernel(**inputs):
    groups = _GROUPS if _GROUPS is not None else [list(_BLOCKS)]
    x = inputs["L0_attn_x"].astype(np.float32)
    shards = [c_.reshape(SEQ * T, D) for c_ in np.split(x, NCORES, axis=0)]
    hw = host_prep(inputs)

    cur = shards
    for gi, blocks in enumerate(groups):
        first = (gi == 0)
        last = (gi == len(groups) - 1)
        nc, in_names = build_program(blocks, first, last)
        in_maps = []
        for ci in range(NCORES):
            m = {}
            for name in in_names:
                if name == "x_in":
                    m[name] = np.ascontiguousarray(cur[ci])
                else:
                    m[name] = hw[name]
            in_maps.append(m)
        res = run_bass_kernel_spmd(nc, in_maps, core_ids=list(range(NCORES)))
        for name in res.results[0]:
            if name.startswith("dbg_"):
                _DEBUG_RES[name] = [np.asarray(res.results[ci][name]) for ci in range(NCORES)]
        cur = [res.results[ci]["x_out"] for ci in range(NCORES)]

    out = np.stack([np.asarray(cur[ci], dtype=np.float32).reshape(SEQ, T, D)
                    for ci in range(NCORES)], axis=0).reshape(16, T, D)
    return out


# version 09
# speedup vs baseline: 1.4157x; 1.4157x over previous
import numpy as np
import ml_dtypes
import concourse.bass as bass
import concourse.mybir as mybir
from concourse.bass_utils import run_bass_kernel_spmd

F32 = mybir.dt.float32
BF16 = mybir.dt.bfloat16
I32 = mybir.dt.int32
AF = mybir.ActivationFunctionType
ALU = mybir.AluOpType
AX = mybir.AxisListType.X
BF = ml_dtypes.bfloat16

NCORES = 8
SEQ = 2
T = 2048
D = 1024
NT = T // 128
EPS_LN = 1e-5

ENGINES = ("sync", "tensor", "vector", "scalar", "gpsimd")
STRICT_SAME_ENGINE_WARWAW_OFF = False

_BLOCKS = list(range(8))
_GROUPS = None
_DEBUG_TAPS = []
_DEBUG_RES = {}
_L0F_NFB = 3


class _Op:
    __slots__ = ("eng", "idx", "dom", "dpos", "fn", "deps", "inc", "amt", "clock", "dsem")


class K:

    def __init__(self, nc):
        self.nc = nc
        self.ops = {e: [] for e in ENGINES}
        self.lastw = {}
        self.readers = {}
        self.know = {e: {} for e in ENGINES}
        self.ncomp = {e: 0 for e in ENGINES}
        self.ndma = {e: 0 for e in ENGINES}
        self._sb = {}
        self._dram_in = {}
        self._dram_out = {}
        self._banks = None
        self.out_dma_ops = []
        self.dma_cum = {}
        self.dma_sem_of = {}

    ARENAS = {"arA": 32768, "arB": 32768, "arC": 16448, "arD": 16384,
              "arE": 8192, "arF": 8192}
    ALIAS = {
        "L0_xT": ("arE", 0),
        "L0_lamT": ("arA", 0), "L0_ugT": ("arA", 16384),
        "L0_ysT": ("arB", 0), "L0_ysTb": ("arB", 16384),
        "L0_WTb": ("arD", 0),
        "L0f_hT": ("arA", 0), "L0f_y1": ("arB", 0), "L0f_wdq": ("arD", 0),
        "L0f_sq": ("arC", 0),
        "L0f_scb": ("arE", 0), "L0f_muB": ("arE", 4096), "L0f_musqB": ("arE", 6144),
        "L0f_varB": ("arF", 0), "L0f_AB": ("arF", 2048), "L0f_BB": ("arF", 4096),
        "L0f_tmpB": ("arF", 6144),
        "at_xT": ("arA", 0), "at_OT": ("arB", 0),
        "at_QT": ("arC", 0), "at_KT": ("arC", 4096), "at_V": ("arC", 8192),
        "at_P": ("arC", 12352),
        "at_rdn": ("arD", 4096), "at_drow": ("arD", 0),
        "at_wo": ("arD", 0),
        "L1_lat": ("arE", 0),
        "at_wq": ("arF", 0), "at_wk": ("arF", 2048), "at_wv": ("arF", 2560),
        "L1_wkvd": ("arF", 4096),
        "f1_y": ("arB", 0), "f1_hT": ("arD", 0), "f1_ie": ("arE", 0),
        "f1_wr": ("arF", 0), "f1_oe": ("arF", 2048),
        "f1_z": ("arF", 2560), "f1_zsq": ("arF", 3072), "f1_zbf": ("arF", 3328),
        "f1_gates": ("arF", 3584), "f1_cent": ("arF", 4608),
        "ldstage0": ("arC", 0), "ldstage1": ("arC", 4096),
        "wstage0": ("arC", 8192), "wstage1": ("arC", 10240),
        "p0_qi": ("arC", 12288), "p0_qf": ("arD", 0), "p0_scb": ("arD", 4096),
        "L2_wk": ("arF", 2048), "L2_wv": ("arF", 4096), "L2_qt": ("arE", 0),
        "f2_h": ("arE", 0), "f2_win": ("arF", 0), "f2_w2s": ("arF", 1024),
        "f2_wout": ("arF", 2048), "f2_bout": ("arF", 4096), "f2_rep": ("arD", 0),
        "f2_s": ("arF", 6144), "f2_g": ("arF", 7168),
        "f3_rr": ("arF", 6144),
        "L3_bp": ("arE", 0),
        "f3_y": ("arB", 0), "f3_hsT": ("arD", 0),
        "f3_b2s": ("arF", 0), "f3_bs2s": ("arF", 2048), "f3_hg": ("arF", 4096),
        "f3_hgT": ("arF", 5120), "f3_b1r": ("arF", 6144), "f3_gates": ("arF", 7168), "f3_gw": ("arF", 7680), "f3_gT": ("arF", 7808),
        "f3_ws1": ("arE", 0), "f3_ws2": ("arE", 0), "f3_w1": ("arE", 0),
        "f3_w2": ("arD", 0), "f3_sc": ("arE", 0), "f3_ge1": ("arE", 64), "f3_slx": ("arE", 128), "f3_mask": ("arE", 192), "f3_m1": ("arE", 256), "f3_m2": ("arE", 288), "f3_den": ("arE", 320), "f3_rden": ("arE", 352),
    }

    def sb(self, tag, shape, dtype):
        self._sbinfo = getattr(self, "_sbinfo", {})
        self._sbinfo[tag] = (list(shape), dtype)
        if tag in self.ALIAS:
            assert len(shape) == 2, tag
            ar, byte0 = self.ALIAS[tag]
            base = self.sb(ar, [128, self.ARENAS[ar] // 2], BF16)
            esz = 2 if dtype == BF16 else 4
            c0 = byte0 // esz
            n = shape[1]
            assert (byte0 + n * esz) <= self.ARENAS[ar], (tag, ar, byte0, shape)
            v = base.bitcast(dtype) if dtype != BF16 else base
            return v[0:shape[0], c0:c0 + n]
        if tag not in self._sb:
            self._sb[tag] = self.nc.alloc_sbuf_tensor(tag, list(shape), dtype).ap()
        ap = self._sb[tag]
        assert tuple(ap.shape) == tuple(shape), (tag, ap.shape, shape)
        return ap

    def col(self, tag):
        return self.sb(tag, [128, 1], F32)

    def constcol(self, tag, value):
        if tag not in self._sb:
            ap = self.sb(tag, [128, 1], F32)
            self.memset("vector", ap, value)
            return ap
        return self._sb[tag]

    def bank2(self, j):
        if self._banks is None:
            self._banks = [self.nc.alloc_psum_tensor(f"psb{q}", [128, 1024], F32).ap()
                           for q in range(4)]
        return self._banks[j]

    def bank(self, i):
        b = self.bank2(i // 2)
        return b[:, (i % 2) * 512:(i % 2 + 1) * 512]

    def bank_bf(self, i):
        b = self.bank2(i // 2).bitcast(BF16)
        return b[:, (i % 2) * 1024:(i % 2 + 1) * 1024]

    def inp(self, name, shape, dtype):
        if name not in self._dram_in:
            self._dram_in[name] = self.nc.dram_tensor(name, list(shape), dtype,
                                                      kind="ExternalInput").ap()
        return self._dram_in[name]

    def outp(self, name, shape, dtype):
        if name not in self._dram_out:
            self._dram_out[name] = self.nc.dram_tensor(name, list(shape), dtype,
                                                       kind="ExternalOutput").ap()
        return self._dram_out[name]

    def tw(self, name):
        return self._dram_in["#" + name + "_bf"]

    def idram(self, name, shape, dtype):
        if name not in self._dram_in:
            self._dram_in["#" + name] = self.nc.dram_tensor(name, list(shape), dtype,
                                                            kind="Internal").ap()
        return self._dram_in["#" + name]

    BUCKET = 2048

    @staticmethod
    def _keys(x):
        if isinstance(x, str):
            return [x]
        t = x.tensor
        name = t.name
        try:
            space = str(t.__class__.__name__)
        except Exception:
            space = ""
        if "PSum" in space:
            esz0 = 2 if x.dtype == BF16 else 4
            off0 = x.offset
            if not isinstance(off0, int):
                return [name]
            dims0 = list(x.ap)
            rs0 = dims0[0][0] if dims0 else 0
            if rs0 > 0:
                off0 = off0 % rs0
            span0 = 1
            for st0, ct0 in dims0[1:]:
                if st0 > 0:
                    span0 += st0 * (ct0 - 1)
            b0_ = (off0 * esz0) // 2048
            b1_ = ((off0 + span0) * esz0 - 1) // 2048
            return [(name, "bk", b) for b in range(b0_, b1_ + 1)]
        if "SBTensor" not in space:
            return [name]
        esz = 2 if x.dtype == BF16 else 4
        off = x.offset
        if not isinstance(off, int):
            return [name]
        dims = list(x.ap)
        rowstride = dims[0][0] if dims else 0
        if rowstride > 0:
            off = off % rowstride
        span = 1
        for st, ct in dims[1:]:
            if st > 0:
                span += st * (ct - 1)
        b0 = (off * esz) // K.BUCKET
        b1 = ((off + span) * esz - 1) // K.BUCKET
        return [(name, b) for b in range(b0, b1 + 1)]

    NDMASEM = 80

    def _rec(self, eng, fn, reads=(), writes=(), sreads=(), dma=False, batch=None):
        op = _Op()
        op.eng = eng
        op.fn = fn
        op.inc = False
        op.amt = 16 if dma else 1
        op.idx = len(self.ops[eng])
        if dma:
            wname = None
            for x in writes:
                if not isinstance(x, str):
                    wname = x.tensor.name
                    break
            if wname is None:
                wname = "_anon"
            if wname not in self.dma_sem_of:
                self.dma_sem_of[wname] = len(self.dma_sem_of)
            si = self.dma_sem_of[wname]
            assert si < self.NDMASEM, f"too many DMA-written buffers: {wname}"
            op.dom = f"dsem{si}"
            op.dsem = si
            op.dpos = 0
        else:
            op.dpos = self.ncomp[eng]
            self.ncomp[eng] += 1
            op.dom = eng
            op.dsem = None
        rk = [kk for x in reads for kk in self._keys(x)]
        sk = [kk for x in sreads for kk in self._keys(x)]
        wk = [kk for x in writes for kk in self._keys(x)]
        raw = {}
        for kk in rk + sk:
            for lw in self.lastw.get(kk, {}).values():
                raw[lw] = True
        for kk in wk:
            for lw in self.lastw.get(kk, {}).values():
                raw[lw] = raw.get(lw, False)
            for r in self.readers.get(kk, ()):
                if r is not None:
                    raw[r] = raw.get(r, False)
        kn = self.know[eng]
        deps = []
        for prod in sorted(raw.keys(), key=lambda o: (o.eng, o.idx)):
            strict = raw[prod]
            if prod is op:
                continue
            if prod.dsem is not None:
                val = self.dma_cum[prod.dsem]
                if kn.get(prod.dom, -1) >= val:
                    continue
                deps.append((prod, val))
                kn[prod.dom] = val
                for d2, p2 in prod.clock.items():
                    if kn.get(d2, -1) < p2:
                        kn[d2] = p2
                continue
            if kn.get(prod.dom, -1) >= prod.dpos:
                if prod.eng != "tensor":
                    prod.inc = True
                continue
            if (not dma) and prod.dom == "tensor" and eng == "tensor":
                continue
            if STRICT_SAME_ENGINE_WARWAW_OFF and (not strict) and (not dma) and prod.dom == eng:
                continue
            deps.append((prod, None))
            for d2, p2 in prod.clock.items():
                if kn.get(d2, -1) < p2:
                    kn[d2] = p2
        if dma:
            self.dma_cum[op.dsem] = self.dma_cum.get(op.dsem, 0) + 16
            op.dpos = self.dma_cum[op.dsem]
        op.deps = deps
        op.clock = dict(kn)
        op.clock[op.dom] = op.dpos
        self.ops[eng].append(op)
        for kk in rk + sk:
            self.readers.setdefault(kk, []).append(op)
        for kk in wk:
            d = self.lastw.setdefault(kk, {})
            old = d.get(op.dom)
            if old is not None and old is not op and old.eng != "tensor":
                old.inc = True
            d[op.dom] = op
            self.readers[kk] = []
        return op

    def dma(self, q, out, in_, batch=None):
        self._rec(q, lambda e, o=out, i=in_: e.dma_start(o, i),
                  reads=[in_], writes=[out], dma=True, batch=batch)

    def copy(self, eng, out, in_):
        if eng == "scalar":
            self._rec(eng, lambda e, o=out, i=in_: e.copy(o, i),
                      reads=[in_], writes=[out])
        else:
            self._rec(eng, lambda e, o=out, i=in_: e.tensor_copy(o, i),
                      reads=[in_], writes=[out])

    def memset(self, eng, ap, v):
        self._rec(eng, lambda e, a=ap, vv=v: e.memset(a, vv), writes=[ap])

    def tt(self, eng, out, a, b, op):
        self._rec(eng, lambda e, o=out, x=a, y=b, p=op: e.tensor_tensor(o, x, y, p),
                  reads=[a, b], writes=[out])

    def ts(self, eng, out, a, s1, op0, s2=None, op1=None, accum=None):
        sreads = [s for s in (s1, s2) if isinstance(s, bass.AP)]
        outs = [out] + ([accum] if accum is not None else [])
        def f(e, o=out, x=a, s1=s1, s2=s2, op0=op0, op1=op1, acc=accum):
            if op1 is None:
                return e.tensor_single_scalar(o, x, s1, op0)
            return e.tensor_scalar(o, x, s1, s2, op0, op1, accum_out=acc)
        self._rec(eng, f, reads=[a], writes=outs, sreads=sreads)

    def stt(self, eng, out, in0, s, in1, op0, op1, accum=None):
        sreads = [s] if isinstance(s, bass.AP) else []
        outs = [out] + ([accum] if accum is not None else [])
        def f(e, o=out, a=in0, ss=s, b=in1, p0=op0, p1=op1, acc=accum):
            return e.scalar_tensor_tensor(o, a, ss, b, p0, p1, accum_out=acc)
        self._rec(eng, f, reads=[in0, in1], writes=outs, sreads=sreads)

    def scan(self, eng, out, a, b, init):
        sreads = [init] if isinstance(init, bass.AP) else []
        def f(e, o=out, x=a, y=b, i=init):
            return e.tensor_tensor_scan(o, x, y, i, ALU.mult, ALU.add)
        self._rec(eng, f, reads=[a, b], writes=[out], sreads=sreads)

    def act(self, eng, out, in_, func, bias=0.0, scale=1.0, accum=None):
        sreads = [s for s in (bias, scale) if isinstance(s, bass.AP)]
        outs = [out] + ([accum] if accum is not None else [])
        def f(e, o=out, i=in_, fn=func, b=bias, s=scale, acc=accum):
            return e.activation(o, i, fn, bias=b, scale=s, accum_out=acc)
        self._rec(eng, f, reads=[in_], writes=outs, sreads=sreads)

    def recip(self, out, in_):
        self._rec("vector", lambda e, o=out, i=in_: e.reciprocal(o, i),
                  reads=[in_], writes=[out])

    def red(self, eng, out, in_, op, absval=None):
        def f(e, o=out, i=in_, p=op, av=absval):
            return e.tensor_reduce(o, i, AX, p, apply_absolute_value=av)
        self._rec(eng, f, reads=[in_], writes=[out])

    def mm(self, out, lhsT, rhs, start, stop, tp=None):
        def f(e, o=out, l=lhsT, r=rhs, st=start, sp=stop, t=tp):
            return e.matmul(o, l, r, start=st, stop=sp, tile_position=t)
        self._rec("tensor", f, reads=[lhsT, rhs], writes=[out])

    def trans(self, out, in_, ident):
        def f(e, o=out, i=in_, d=ident):
            return e.transpose(o, i, d)
        self._rec("tensor", f, reads=[in_, ident], writes=[out])

    def out_dma(self, q, out, in_):
        src = self._rec(q, lambda e, o=out, i=in_: e.dma_start(o, i),
                        reads=[in_], writes=[out], dma=True)
        self.out_dma_ops.append(src)

    def load_cast(self, dst, src, stage_tag, eng="vector"):
        n = 1
        for d in src.shape[1:]:
            n *= d
        p = src.shape[0]
        assert n <= 1024 and len(src.shape) == 2
        stage = self.sb(stage_tag, [128, 1024], F32)
        self.dma("sync", stage[0:p, 0:n], src)
        self.copy(eng, dst, stage[0:p, 0:n])

    def build(self):
        if self.out_dma_ops:
            kn = self.know["sync"]
            op = _Op()
            op.eng = "sync"
            op.idx = len(self.ops["sync"])
            op.dom = "sync"
            op.dpos = self.ncomp["sync"]
            op.fn = lambda e: e.nop()
            op.inc = False
            op.amt = 1
            op.dsem = None
            deps = []
            for prod in self.out_dma_ops:
                val = self.dma_cum[prod.dsem]
                if kn.get(prod.dom, -1) < val:
                    deps.append((prod, val))
                    kn[prod.dom] = val
            op.deps = deps
            op.clock = dict(kn)
            self.ops["sync"].append(op)
        for e in ENGINES:
            for op in self.ops[e]:
                if op.amt == 16:
                    op.inc = True
                for prod, _v in op.deps:
                    prod.inc = True
        pc = {}
        for e in ENGINES:
            tot = 0
            lst = []
            for op in self.ops[e]:
                if op.inc and op.dsem is None:
                    tot += op.amt
                lst.append(tot)
            pc[e] = lst
        sems = {e: self.nc.alloc_semaphore(f"prog_{e}") for e in ENGINES}
        dsems = [self.nc.alloc_semaphore(f"dma_{i}") for i in range(self.NDMASEM)]
        nwaits = sum(len(op.deps) for e in ENGINES for op in self.ops[e])
        nincs = sum(1 for e in ENGINES for op in self.ops[e] if op.inc)
        nops = sum(len(self.ops[e]) for e in ENGINES)
        print(f"[K] ops={nops} waits={nwaits} incs={nincs} "
              + " ".join(f"{e}:{len(self.ops[e])}" for e in ENGINES))
        with self.nc.Block() as block:
            for e in ENGINES:
                def mk(e):
                    def body(eng):
                        for op in self.ops[e]:
                            for prod, val in op.deps:
                                if prod.dsem is not None:
                                    eng.wait_ge(dsems[prod.dsem], val)
                                else:
                                    eng.wait_ge(sems[prod.eng], pc[prod.eng][prod.idx])
                            inst = op.fn(eng)
                            if op.inc:
                                if op.dsem is not None:
                                    inst.then_inc(dsems[op.dsem], 16)
                                else:
                                    inst.then_inc(sems[e], op.amt)
                    return body
                getattr(block, e)(mk(e))



W_MANIFEST = [
    ("L0_W_T", [D, D]),
    ("L0f_laT2", [6 * D, 32]), ("L0f_lbT2", [6 * 32, D]),
    ("L1_Wq", [D, D]), ("L1_Wkvd", [D, 256]), ("L1_Wku", [256, D]),
    ("L1_Wvu", [256, D]), ("L1_Wout", [D, D]),
    ("L1f_wrT", [D, 128]), ("L1f_ie2", [16 * D, 512]), ("L1f_oe2", [16 * 512, 64]),
    ("L2f_win", [D, 64]), ("L2f_w2s", [D, 64]), ("L2f_wout", [64, D]),
    ("L2f_bout", [1, D]),
    ("L3_wqkv", [D, 3 * D]), ("L3_wp", [D, D]), ("L3_bproj", [1, D]),
    ("L3f_gw", [D, 8]), ("L3f_w12", [8 * D, 512]), ("L3f_w22", [8 * 512, D]),
    ("L3f_ws1", [D, 512]), ("L3f_ws2", [512, D]),
    ("L3f_b1", [1, 8 * 512]), ("L3f_b2", [8, D]), ("L3f_bs2", [1, D]),
]


def conv_chunk(k, dst_bf_dram, src_f32_dram, i):
    stg = k.sb(f"ldstage{i % 2}", [128, 1024], F32)
    wst = k.sb(f"wstage{i % 2}", [128, 1024], BF16)
    p, n = src_f32_dram.shape
    k.dma("sync", stg[0:p, 0:n], src_f32_dram)
    k.copy("vector" if i % 2 else "scalar", wst[0:p, 0:n], stg[0:p, 0:n])
    k.dma("sync", dst_bf_dram, wst[0:p, 0:n])


def emit_weight_prep(k):
    ci = 0
    for name, shape in W_MANIFEST:
        src = k.inp(name, shape, F32)
        twin = k.idram(name + "_bf", shape, BF16)
        R, C = shape
        for r0 in range(0, R, 128):
            p = min(128, R - r0)
            for c0 in range(0, C, 1024):
                n = min(1024, C - c0)
                conv_chunk(k, twin[r0:r0 + p, c0:c0 + n],
                           src[r0:r0 + p, c0:c0 + n], ci)
                ci += 1
    qT = k.inp("L0f_qT", [6, D, D], I32)
    scT = k.inp("L0f_scT", [6, 64, D], F32)
    wdqt = k.idram("L0f_wdq_bf", [6, D, D], BF16)
    qi = k.sb("p0_qi", [128, D], I32)
    qf = k.sb("p0_qf", [128, D], F32)
    sc_sb = k.sb("p0_scb", [128, D], F32)
    for l in range(6):
        for dc in range(8):
            k.dma("sync", qi, qT[l, dc * 128:(dc + 1) * 128, :])
            for g in range(8):
                k.dma("sync", sc_sb[16 * g:16 * (g + 1), :],
                      scT[l, 8 * dc + g, :].unsqueeze(0).broadcast_to([16, D]))
            k.copy("vector", qf, qi)
            k.ts("vector", qf, qf, 1.0 / 7.5, ALU.mult, -1.0, ALU.add)
            wst = k.sb(f"wstage{dc % 2}", [128, 1024], BF16)
            k.tt("vector", wst, qf, sc_sb, ALU.mult)
            k.dma("sync", wdqt[l, dc * 128:(dc + 1) * 128, :], wst)
    waT = k.inp("L2_waT", [D, 3 * D], F32)
    wpT = k.inp("L2_wpT", [D, D], F32)
    waq = k.idram("L2_waq_bf", [D, 3 * D], BF16)
    wpq = k.idram("L2_wpq_bf", [D, D], BF16)
    sa = k.col("L2_sa")
    ra = k.col("L2_ra")
    sp = k.col("L2_sp")
    rp = k.col("L2_rp")
    quant_mean(k, "L2qa", waT, 3 * D, sa, ra, 3 * D * D)
    quant_mean(k, "L2qp", wpT, D, sp, rp, D * D)
    for (W, Q, nc_, s_, r_) in ((waT, waq, 3 * D, sa, ra), (wpT, wpq, D, sp, rp)):
        for pc in range(8):
            for cc in range(nc_ // 1024):
                wst = k.sb(f"wstage{(pc + cc) % 2}", [128, 1024], BF16)
                quant_slice(k, wst, W[pc * 128:(pc + 1) * 128, cc * 1024:(cc + 1) * 1024],
                            f"ldstage{(pc + cc) % 2}", s_, r_)
                k.dma("sync", Q[pc * 128:(pc + 1) * 128, cc * 1024:(cc + 1) * 1024], wst)



def stream_tile(k, s, t):
    st = k.sb("stream", [128, SEQ * NT * D], BF16)
    i = s * NT + t
    return st[:, i * D:(i + 1) * D]


def ident_bf(k):
    ap = k.sb("ident_bf", [128, 128], BF16)
    return ap


def ln_stats_core(k, tag, src, mu_sl, var_sl, n=D):
    s_col = k.col(f"{tag}_s")
    ss_col = k.col(f"{tag}_ss")
    musq = k.col(f"{tag}_musq")
    sq = k.sb("lnsq", [128, n], BF16)
    k.red("vector", s_col, src, ALU.add)
    k.stt("vector", sq, src, 1.0, src, ALU.mult, ALU.mult, accum=ss_col)
    k.ts("vector", mu_sl, s_col, 1.0 / n, ALU.mult)
    k.tt("vector", musq, mu_sl, mu_sl, ALU.mult)
    k.stt("vector", var_sl, ss_col, 1.0 / n, musq, ALU.mult, ALU.subtract)


def ln_finish(k, tag, muB, varB, rstdB, nmrB):
    k.act("scalar", varB, varB, AF.Ln, bias=k.constcol("c_epsln", EPS_LN))
    k.act("scalar", rstdB, varB, AF.Exp, scale=-0.5)
    k.stt("vector", nmrB, muB, -1.0, rstdB, ALU.mult, ALU.mult)


def pre_ln_xT(k, tag, s, tiles, xT, bank0=6):
    W = 128 * len(tiles)
    B = len(tiles)
    idb = ident_bf(k)
    muB = k.sb("pre_muB", [128, 16], F32)
    varB = k.sb("pre_varB", [128, 16], F32)
    rstdB = k.sb("pre_rstdB", [128, 16], F32)
    nmrB = k.sb("pre_nmrB", [128, 16], F32)
    for j, t in enumerate(tiles):
        ln_stats_core(k, f"ln{j % 2}", stream_tile(k, s, t),
                      muB[:, j:j + 1], varB[:, j:j + 1])
    ln_finish(k, "pre", muB[:, 0:B], varB[:, 0:B], rstdB[:, 0:B], nmrB[:, 0:B])
    for j, t in enumerate(tiles):
        st = stream_tile(k, s, t)
        xn = k.sb(f"xn{j % 2}", [128, D], BF16)
        k.ts("vector", xn, st, rstdB[:, j:j + 1], ALU.mult, nmrB[:, j:j + 1], ALU.add)
        bank = k.bank_bf(bank0 - 1 + 2 * (j % 2))
        for dc in range(8):
            k.trans(bank[:, dc * 128:(dc + 1) * 128], xn[:, dc * 128:(dc + 1) * 128], idb)
        dst = xT.rearrange("p (dc w) -> p dc w", w=W)[:, :, j * 128:(j + 1) * 128]
        src = bank.rearrange("p (dc w) -> p dc w", w=128)
        k.copy("vector", dst, src)


def post_ln_add(k, tag, s, t, ybuf):
    st = stream_tile(k, s, t)
    p = t % 4
    mu = k.col(f"pl{p}_mu")
    var = k.col(f"pl{p}_var")
    rstd = k.col(f"pl{p}_rstd")
    nmr = k.col(f"pl{p}_nmr")
    ln_stats_core(k, f"pl{p}", ybuf, mu, var)
    ln_finish(k, f"pl{p}", mu, var, rstd, nmr)
    k.ts("vector", st, st, nmr, ALU.add)
    k.stt("vector", st, ybuf, rstd, st, ALU.mult, ALU.add)



def emit_L0_attn(k):
    bcol = k.inp("L0_b", [D, 1], F32)
    idb = ident_bf(k)
    WTb = k.sb("L0_WTb", [128, 8 * D], BF16)
    WTt = k.tw("L0_W_T")
    k.dma("sync", WTb.rearrange("p (dc c) -> p dc c", c=D),
          WTt.rearrange("(dc p) c -> p dc c", p=128))
    bsb = k.sb("L0_bias", [128, 8], F32)
    for ec in range(8):
        k.dma("sync", bsb[:, ec:ec + 1], bcol[ec * 128:(ec + 1) * 128, :])

    CH = 512
    NCH = T // CH
    xT = k.sb("L0_xT", [128, 8 * CH], BF16)
    lamT = k.sb("L0_lamT", [128, 8 * CH], F32)
    ugT = k.sb("L0_ugT", [128, 8 * CH], F32)
    ysT = k.sb("L0_ysT", [128, 8 * CH], F32)
    ysTb = k.sb("L0_ysTb", [128, 8 * CH], BF16)
    state = k.sb("L0_state", [128, 8], F32)
    ybuf = k.sb("at_ybuf", [128, D], BF16)

    for s in range(SEQ):
        k.memset("vector", state, 0.0)
        for c in range(NCH):
            tiles = [c * 4 + j for j in range(4)]
            pre_ln_xT(k, "L0", s, tiles, xT)
            for ec in range(8):
                bank = k.bank(2 * (ec % 2))
                for dc in range(8):
                    k.mm(bank, WTb[:, dc * D + ec * 128: dc * D + (ec + 1) * 128],
                         xT[:, dc * CH:(dc + 1) * CH], start=(dc == 0), stop=(dc == 7))
                k.act("scalar", lamT[:, ec * CH:(ec + 1) * CH], bank, AF.Sigmoid,
                      bias=bsb[:, ec:ec + 1])
            k.tt("vector", ugT, lamT, xT, ALU.mult)
            k.tt("vector", ugT, xT, ugT, ALU.subtract)
            for dc in range(8):
                k.scan("vector", ysT[:, dc * CH:(dc + 1) * CH],
                       lamT[:, dc * CH:(dc + 1) * CH], ugT[:, dc * CH:(dc + 1) * CH],
                       state[:, dc:dc + 1])
                k.copy("vector", state[:, dc:dc + 1], ysT[:, (dc + 1) * CH - 1:(dc + 1) * CH])
            k.copy("vector", ysTb, ysT)
            for j in range(4):
                bank = k.bank_bf(1 + 2 * (j % 2))
                for dc in range(8):
                    k.trans(bank[:, dc * 128:(dc + 1) * 128],
                            ysTb[:, dc * CH + j * 128: dc * CH + (j + 1) * 128], idb)
                k.copy("vector", ybuf, bank[:, 0:1024])
                post_ln_add(k, "L0", s, tiles[j], ybuf)


def ones_bf(k):
    if "ones_bf" not in k._sb:
        ap = k.sb("ones_bf", [128, 128], BF16)
        k.memset("vector", ap, 1.0)
        return ap
    return k._sb["ones_bf"]


def emit_L0_ffn(k):
    wdqt = k.tw("L0f_wdq")
    laT = k.tw("L0f_laT2")
    lbT = k.tw("L0f_lbT2")
    bias_in = k.inp("L0f_biases", [6, D, 1], F32)
    gam_in = k.inp("L0f_gam", [3, D, 1], F32)
    bet_in = k.inp("L0f_bet", [3, D, 1], F32)
    idb = ident_bf(k)
    ones = ones_bf(k)

    bias_sb = k.sb("L0f_bias", [128, 48], F32)
    gam_sb = k.sb("L0f_gamc", [128, 24], F32)
    bet_sb = k.sb("L0f_betc", [128, 24], F32)
    for l in range(6):
        for oc in range(8):
            k.dma("sync", bias_sb[:, l * 8 + oc: l * 8 + oc + 1],
                  bias_in[l, oc * 128:(oc + 1) * 128, :])
    for fb in range(3):
        for oc in range(8):
            k.dma("sync", gam_sb[:, fb * 8 + oc: fb * 8 + oc + 1],
                  gam_in[fb, oc * 128:(oc + 1) * 128, :])
            k.dma("sync", bet_sb[:, fb * 8 + oc: fb * 8 + oc + 1],
                  bet_in[fb, oc * 128:(oc + 1) * 128, :])

    hT = k.sb("L0f_hT", [128, 8 * T], BF16)
    y1 = k.sb("L0f_y1", [128, 8 * T], BF16)
    wdq = k.sb("L0f_wdq", [128, 8 * D], BF16)
    la_sb = k.sb("L0f_la", [128, 8 * 32], BF16)
    lb_sb = k.sb("L0f_lb", [32, D], BF16)
    u_sb = k.sb("L0f_u", [32, 512], BF16)
    sq_sb = k.sb("L0f_sq", [128, 8 * 512], BF16)
    muB = k.sb("L0f_muB", [128, 512], F32)
    musqB = k.sb("L0f_musqB", [128, 512], F32)
    varB = k.sb("L0f_varB", [128, 512], F32)
    AB = k.sb("L0f_AB", [128, 512], F32)
    BB = k.sb("L0f_BB", [128, 512], F32)
    tmpB = k.sb("L0f_tmpB", [128, 512], F32)
    ybuf = k.sb("at_ybuf", [128, D], BF16)
    epsc = k.constcol("c_epsln", EPS_LN)

    def load_qlora(l):
        k.dma("sync", wdq.rearrange("p (dc c) -> p dc c", c=D),
              wdqt[l].rearrange("(dc p) c -> p dc c", p=128))
        k.dma("sync", la_sb.rearrange("p (dc r) -> p dc r", r=32),
              laT[l * D:(l + 1) * D, :].rearrange("(dc p) r -> p dc r", p=128))
        k.dma("sync", lb_sb, lbT[l * 32:(l + 1) * 32, :])

    def qlora(l, rhs, out_mode, fb):
        load_qlora(l)
        for tc in range(4):
            ub = k.bank(5)
            for dc in range(8):
                k.mm(ub[0:32, :], la_sb[:, dc * 32:(dc + 1) * 32],
                     rhs[:, dc * T + tc * 512: dc * T + (tc + 1) * 512],
                     start=(dc == 0), stop=(dc == 7))
            k.copy("vector", u_sb, ub[0:32, :])
            for oc in range(8):
                b = k.bank(2 * (oc % 4))
                for dc in range(8):
                    k.mm(b, wdq[:, dc * D + oc * 128: dc * D + (oc + 1) * 128],
                         rhs[:, dc * T + tc * 512: dc * T + (tc + 1) * 512],
                         start=(dc == 0), stop=False)
                k.mm(b, lb_sb[:, oc * 128:(oc + 1) * 128], u_sb,
                     start=False, stop=True)
                dst = slice(oc * T + tc * 512, oc * T + (tc + 1) * 512)
                bc = bias_sb[:, l * 8 + oc: l * 8 + oc + 1]
                if out_mode == "y1":
                    k.ts("vector", y1[:, dst], b, bc, ALU.add)
                else:
                    k.stt("vector", hT[:, dst], b, bc, hT[:, dst], ALU.add, ALU.add)

    for s in range(SEQ):
        pre_ln_xT(k, "L0f", s, list(range(NT)), hT, bank0=6)
        for fb in range(_L0F_NFB):
            qlora(2 * fb, hT, "y1", fb)
            for tc in range(4):
                for oc in range(8):
                    sl = slice(oc * T + tc * 512, oc * T + (tc + 1) * 512)
                    k.tt("vector", sq_sb[:, oc * 512:(oc + 1) * 512], y1[:, sl], y1[:, sl], ALU.mult)
                bS, bQ = k.bank(1), k.bank(3)
                for oc in range(8):
                    sl = slice(oc * T + tc * 512, oc * T + (tc + 1) * 512)
                    k.mm(bS, ones, y1[:, sl], start=(oc == 0), stop=(oc == 7))
                for oc in range(8):
                    k.mm(bQ, ones, sq_sb[:, oc * 512:(oc + 1) * 512],
                         start=(oc == 0), stop=(oc == 7))
                k.ts("vector", muB, bS, 1.0 / D, ALU.mult)
                k.tt("vector", musqB, muB, muB, ALU.mult)
                k.stt("vector", varB, bQ, 1.0 / D, musqB, ALU.mult, ALU.subtract)
                k.act("scalar", varB, varB, AF.Ln, bias=epsc)
                k.act("scalar", AB, varB, AF.Exp, scale=-0.5)
                k.stt("vector", BB, muB, -1.0, AB, ALU.mult, ALU.mult)
                for oc in range(8):
                    sl = slice(oc * T + tc * 512, oc * T + (tc + 1) * 512)
                    k.tt("vector", tmpB, y1[:, sl], AB, ALU.mult)
                    k.tt("vector", tmpB, tmpB, BB, ALU.add)
                    k.act("scalar", y1[:, sl], tmpB, AF.Gelu,
                          bias=bet_sb[:, fb * 8 + oc: fb * 8 + oc + 1],
                          scale=gam_sb[:, fb * 8 + oc: fb * 8 + oc + 1])
            qlora(2 * fb + 1, y1, "resid", fb)
        for t in range(NT):
            bank = k.bank_bf(5 + 2 * (t % 2))
            for dc in range(8):
                k.trans(bank[:, dc * 128:(dc + 1) * 128],
                        hT[:, dc * T + t * 128: dc * T + (t + 1) * 128], idb)
            k.copy("vector", ybuf, bank[:, 0:1024])
            post_ln_add(k, "L0f", s, t, ybuf)



def attn_pair(k, s, pair, QpT, KpT, Vp, OT, causal):
    ones = ones_bf(k)
    cmT = k.sb("c_cmaskT", [128, 2048], BF16)
    PT = k.sb("at_P", [128, 2048], BF16)
    rdn = k.sb("at_rdn", [128, 1024], F32)
    drow = k.sb("at_drow", [1, 512], BF16)

    for qst in range(4):
        nk = 4 * qst + 4 if causal else 16
        qsl = slice(qst * 512, (qst + 1) * 512)

        def smm(kc):
            sl = k.bank2(kc % 3)
            k.mm(sl[:, 0:512], KpT[0:64, kc * 128:(kc + 1) * 128],
                 QpT[0:64, qsl], start=True, stop=True)
            k.mm(sl[:, 512:1024], KpT[64:128, kc * 128:(kc + 1) * 128],
                 QpT[64:128, qsl], start=True, stop=True, tp=(64, 0))

        def body(kc):
            slot = PT[:, (kc % 2) * 1024:(kc % 2 + 1) * 1024]
            k.act("scalar", slot, k.bank2(kc % 3), AF.Exp, scale=0.125)
            if causal and kc >= 4 * qst:
                r = kc - 4 * qst
                for h in range(2):
                    k.tt("vector", slot[:, h * 512:(h + 1) * 512],
                         slot[:, h * 512:(h + 1) * 512],
                         cmT[:, r * 512:(r + 1) * 512], ALU.mult)
            for h in range(2):
                k.mm(k.bank2(3)[0:65, h * 512:(h + 1) * 512],
                     Vp[:, kc * 130 + h * 65: kc * 130 + (h + 1) * 65],
                     slot[:, h * 512:(h + 1) * 512],
                     start=(kc == 0), stop=(kc == nk - 1))

        for kc in range(min(2, nk)):
            smm(kc)
        for kc in range(2, nk):
            smm(kc)
            body(kc - 2)
        for kc in range(max(0, nk - 2), nk):
            body(kc)
        bb = k.bank2(0)
        for h in range(2):
            k.copy("vector", drow, k.bank2(3)[64:65, h * 512:(h + 1) * 512])
            k.mm(bb[0:64, h * 512:(h + 1) * 512], ones[0:1, 0:64], drow,
                 start=True, stop=True)
        k.recip(rdn[0:64, :], bb[0:64, :])
        for h in range(2):
            k.stt("vector", OT[h * 64:(h + 1) * 64, pair * 2048 + qst * 512: pair * 2048 + (qst + 1) * 512],
                  k.bank2(3)[0:64, h * 512:(h + 1) * 512], 1.0,
                  rdn[0:64, h * 512:(h + 1) * 512], ALU.bypass, ALU.mult)


def emit_L1_attn(k):
    Wq = k.tw("L1_Wq")
    Wkvd = k.tw("L1_Wkvd")
    Wku = k.tw("L1_Wku")
    Wvu = k.tw("L1_Wvu")
    Wout = k.tw("L1_Wout")
    idb = ident_bf(k)
    xT = k.sb("at_xT", [128, 8 * T], BF16)
    OT = k.sb("at_OT", [128, 8 * T], BF16)
    lat = k.sb("L1_lat", [128, 2 * T], BF16)
    QpT = k.sb("at_QT", [128, T], BF16)
    KpT = k.sb("at_KT", [128, T], BF16)
    Vp = k.sb("at_V", [128, 2080], BF16)
    k.memset("vector", Vp, 1.0)
    wq_sl = k.sb("at_wq", [128, 1024], BF16)
    wk_sl = k.sb("at_wk", [128, 256], BF16)
    wv_sl = k.sb("at_wv", [128, 256], BF16)
    wkvd = k.sb("L1_wkvd", [128, 2048], BF16)
    ybuf = k.sb("at_ybuf", [128, D], BF16)

    for s in range(SEQ):
        pre_ln_xT(k, "at", s, list(range(NT)), xT, bank0=6)
        k.dma("sync", wkvd.rearrange("p (dc c) -> p dc c", c=256),
              Wkvd.rearrange("(dc p) c -> p dc c", p=128))
        for cc in range(2):
            for tc in range(4):
                b = k.bank(2 * (cc % 2))
                for dc in range(8):
                    k.mm(b, wkvd[:, dc * 256 + cc * 128: dc * 256 + (cc + 1) * 128],
                         xT[:, dc * T + tc * 512: dc * T + (tc + 1) * 512],
                         start=(dc == 0), stop=(dc == 7))
                k.copy("vector", lat[:, cc * T + tc * 512: cc * T + (tc + 1) * 512], b)
        for pair in range(8):
            k.dma("sync", wq_sl.rearrange("p (dc c) -> p dc c", c=128),
                  Wq[:, pair * 128:(pair + 1) * 128].rearrange("(dc p) c -> p dc c", p=128))
            k.dma("sync", wk_sl.rearrange("p (cc c) -> p cc c", c=128),
                  Wku[:, pair * 128:(pair + 1) * 128].rearrange("(cc p) c -> p cc c", p=128))
            k.dma("sync", wv_sl.rearrange("p (cc c) -> p cc c", c=128),
                  Wvu[:, pair * 128:(pair + 1) * 128].rearrange("(cc p) c -> p cc c", p=128))
            for tc in range(4):
                bq = k.bank(0)
                for dc in range(8):
                    k.mm(bq, wq_sl[:, dc * 128:(dc + 1) * 128],
                         xT[:, dc * T + tc * 512: dc * T + (tc + 1) * 512],
                         start=(dc == 0), stop=(dc == 7))
                k.copy("vector", QpT[:, tc * 512:(tc + 1) * 512], bq)
                bk = k.bank(2)
                for cc in range(2):
                    k.mm(bk, wk_sl[:, cc * 128:(cc + 1) * 128],
                         lat[:, cc * T + tc * 512: cc * T + (tc + 1) * 512],
                         start=(cc == 0), stop=(cc == 1))
                k.copy("vector", KpT[:, tc * 512:(tc + 1) * 512], bk)
            for tt in range(NT):
                bv = k.bank(4)
                for cc in range(2):
                    k.mm(bv[:, 0:128], lat[:, cc * T + tt * 128: cc * T + (tt + 1) * 128],
                         wv_sl[:, cc * 128:(cc + 1) * 128],
                         start=(cc == 0), stop=(cc == 1))
                k.copy("vector", Vp[:, tt * 130: tt * 130 + 64], bv[:, 0:64])
                k.copy("vector", Vp[:, tt * 130 + 65: tt * 130 + 129], bv[:, 64:128])
            attn_pair(k, s, pair, QpT, KpT, Vp, OT, causal=True)
        wo = k.sb("at_wo", [128, 8192], BF16)
        k.dma("sync", wo.rearrange("p (pc c) -> p pc c", c=1024),
              Wout.rearrange("(pc p) c -> p pc c", p=128))
        for tt in range(NT):
            for half in range(2):
                b = k.bank(half)
                for pc in range(8):
                    k.mm(b, OT[:, pc * 2048 + tt * 128: pc * 2048 + (tt + 1) * 128],
                         wo[:, pc * 1024 + half * 512: pc * 1024 + (half + 1) * 512],
                         start=(pc == 0), stop=(pc == 7))
                k.copy("vector", ybuf[:, half * 512:(half + 1) * 512], b)
            post_ln_add(k, "at", s, tt, ybuf)


def emit_L1_ffn(k):
    wrT = k.tw("L1f_wrT")
    cent = k.inp("L1f_cent", [16, 128], F32)
    temp = k.inp("L1f_temp", [1, 1], F32)
    IE = k.tw("L1f_ie2")
    OE = k.tw("L1f_oe2")
    idb = ident_bf(k)

    xT = k.sb("at_xT", [128, 8 * T], BF16)
    y = k.sb("f1_y", [128, NT * D], BF16)
    hT = k.sb("f1_hT", [128, 4 * T], BF16)
    wr = k.sb("f1_wr", [128, 1024], BF16)
    ie_sb = k.sb("f1_ie", [128, 8 * 512], BF16)
    oe_sb = k.sb("f1_oe", [128, 256], BF16)
    gates = k.sb("f1_gates", [128, NT * 16], F32)
    z_sb = k.sb("f1_z", [128, 128], F32)
    zsq = k.sb("f1_zsq", [128, 128], BF16)
    zbf = k.sb("f1_zbf", [128, 128], BF16)
    sl_sb = k.sb("f1_sl", [128, 16], F32)
    ge1 = k.sb("f1_ge1", [128, 16], F32)
    slx = k.sb("f1_slx", [128, 16], F32)
    maskb = k.sb("f1_mask", [128, 16], F32)
    pe = k.sb("f1_pe", [128, 16], F32)
    cent_sb = k.sb("f1_cent", [16, 128], F32)
    cnb = k.sb("f1_cnb", [16, 128], BF16)
    cnT = k.sb("f1_cnT", [128, 16], BF16)
    csq = k.sb("f1_csq", [16, 128], BF16)
    n2c = k.col("f1_n2")
    rnc = k.col("f1_rn")
    cn2 = k.sb("f1_cn2", [16, 1], F32)
    m1 = k.col("f1_m1")
    negm1 = k.col("f1_negm1")
    m2 = k.col("f1_m2")
    den = k.col("f1_den")
    rden = k.col("f1_rden")
    fcol = k.col("f1_f")
    tcol = k.col("f1_t")

    k.dma("sync", cent_sb, cent)
    k.act("scalar", csq, cent_sb, AF.Square, accum=cn2[0:16, :])
    k.ts("vector", cn2[0:16, :], cn2[0:16, :], 1e-30, ALU.max)
    k.act("scalar", cn2[0:16, :], cn2[0:16, :], AF.Ln)
    k.act("scalar", cn2[0:16, :], cn2[0:16, :], AF.Exp, scale=-0.5)
    k.ts("vector", cnb, cent_sb, cn2[0:16, 0:1], ALU.mult)
    bT = k.bank_bf(1)
    k.trans(bT[:, 0:16], cnb, idb[0:16, 0:16])
    k.copy("vector", cnT, bT[:, 0:16])
    k.dma("sync", tcol, temp.broadcast_to([128, 1]))
    k.act("scalar", fcol, tcol, AF.Exp)
    k.ts("vector", fcol, fcol, 0.08838834764831845, ALU.mult)
    k.dma("sync", wr.rearrange("p (dc c) -> p dc c", c=128),
          wrT.rearrange("(dc p) c -> p dc c", p=128))

    for s in range(SEQ):
        pre_ln_xT(k, "f1", s, list(range(NT)), xT, bank0=6)
        for tt in range(NT):
            bz = k.bank(0)
            for dc in range(8):
                k.mm(bz[:, 0:128], xT[:, dc * T + tt * 128: dc * T + (tt + 1) * 128],
                     wr[:, dc * 128:(dc + 1) * 128], start=(dc == 0), stop=(dc == 7))
            k.copy("vector", z_sb, bz[:, 0:128])
            k.stt("vector", zsq, z_sb, 1.0, z_sb, ALU.mult, ALU.mult, accum=n2c)
            k.ts("vector", n2c, n2c, 1e-30, ALU.max)
            k.act("scalar", n2c, n2c, AF.Ln)
            k.act("scalar", rnc, n2c, AF.Exp, scale=-0.5)
            k.ts("vector", zbf, z_sb, rnc, ALU.mult)
            bzt = k.bank_bf(1)
            k.trans(bzt[:, 0:128], zbf, idb)
            k.copy("vector", zbf, bzt[:, 0:128])
            bl = k.bank(2)
            k.mm(bl[:, 0:16], zbf, cnT, start=True, stop=True)
            k.ts("vector", sl_sb, bl[:, 0:16], fcol, ALU.mult)
            k.red("vector", m1, sl_sb, ALU.max)
            k.ts("vector", ge1, sl_sb, m1, ALU.is_ge)
            k.stt("vector", slx, ge1, -1e30, sl_sb, ALU.mult, ALU.add)
            k.red("vector", m2, slx, ALU.max)
            k.ts("vector", maskb, sl_sb, m2, ALU.is_ge)
            k.ts("vector", negm1, m1, -1.0, ALU.mult)
            k.act("scalar", pe, sl_sb, AF.Exp, bias=negm1, accum=den)
            k.recip(rden, den)
            k.tt("vector", pe, pe, maskb, ALU.mult)
            k.ts("vector", gates[:, tt * 16:(tt + 1) * 16], pe, rden, ALU.mult)
        for e in range(16):
            ie_sb = k.sb("wring1", [128, 8 * 512], BF16) if e % 2 else k.sb("f1_ie", [128, 8 * 512], BF16)
            k.dma("sync", ie_sb.rearrange("p (dc c) -> p dc c", c=512),
                  IE[e * D:(e + 1) * D, :].rearrange("(dc p) c -> p dc c", p=128))
            k.dma("sync", oe_sb.rearrange("p (hc c) -> p hc c", c=64),
                  OE[e * 512:(e + 1) * 512, :].rearrange("(hc p) c -> p hc c", p=128))
            for hc in range(4):
                for tc in range(4):
                    bh = k.bank(2 * (tc % 2))
                    for dc in range(8):
                        k.mm(bh, ie_sb[:, dc * 512 + hc * 128: dc * 512 + (hc + 1) * 128],
                             xT[:, dc * T + tc * 512: dc * T + (tc + 1) * 512],
                             start=(dc == 0), stop=(dc == 7))
                    k.act("scalar", hT[:, hc * T + tc * 512: hc * T + (tc + 1) * 512],
                          bh, AF.Gelu)
            for tt in range(NT):
                bo = k.bank(4 + 2 * (tt % 2))
                for hc in range(4):
                    k.mm(bo[:, 0:64], hT[:, hc * T + tt * 128: hc * T + (tt + 1) * 128],
                         oe_sb[:, hc * 64:(hc + 1) * 64],
                         start=(hc == 0), stop=(hc == 3))
                k.ts("vector", y[:, tt * D + e * 64: tt * D + (e + 1) * 64],
                     bo[:, 0:64], gates[:, tt * 16 + e: tt * 16 + e + 1], ALU.mult)
        for tt in range(NT):
            post_ln_add(k, "f1", s, tt, y[:, tt * D:(tt + 1) * D])


MAGIC = 12582912.0


def quant_mean(k, tag, W, ncols, scale_col, rscale_col, nelem):
    acc = k.col(f"{tag}_acc")
    tmp = k.col(f"{tag}_tmp")
    one = k.col(f"{tag}_one")
    row = k.sb(f"{tag}_row", [1, 128], F32)
    scr = k.idram(f"{tag}_scr", [128, 1], F32)
    scr2 = k.idram(f"{tag}_scr2", [1, 1], F32)
    nch = ncols // 1024
    for pc in range(8):
        for cc in range(nch):
            stg = k.sb(f"ldstage{(pc + cc) % 2}", [128, 1024], F32)
            k.dma("sync", stg, W[pc * 128:(pc + 1) * 128, cc * 1024:(cc + 1) * 1024])
            if pc == 0 and cc == 0:
                k.red("vector", acc, stg, ALU.add, absval=True)
            else:
                k.red("vector", tmp, stg, ALU.add, absval=True)
                k.tt("vector", acc, acc, tmp, ALU.add)
    k.dma("sync", scr, acc)
    k.dma("sync", row, scr.rearrange("a b -> b a"))
    k.red("vector", one[0:1, :], row, ALU.add)
    k.ts("vector", one[0:1, :], one[0:1, :], 1.0 / nelem, ALU.mult, 1e-8, ALU.max)
    k.dma("sync", scr2, one[0:1, :])
    k.dma("sync", scale_col, scr2.broadcast_to([128, 1]))
    k.act("scalar", rscale_col, scale_col, AF.Ln)
    k.act("scalar", rscale_col, rscale_col, AF.Exp, scale=-1.0)


def quant_slice(k, dst, src_dram, stage_tag, scale_col, rscale_col, qt_tag="L2_qt"):
    p, n = src_dram.shape
    stg = k.sb(stage_tag, [128, 1024], F32)
    qt = k.sb(qt_tag, [128, 1024], F32)
    k.dma("sync", stg[0:p, 0:n], src_dram)
    k.ts("vector", qt[0:p, 0:n], stg[0:p, 0:n], rscale_col, ALU.mult, MAGIC, ALU.add)
    k.ts("vector", qt[0:p, 0:n], qt[0:p, 0:n], MAGIC, ALU.subtract, -2.0, ALU.max)
    k.ts("vector", dst, qt[0:p, 0:n], 2.0, ALU.min, scale_col, ALU.mult)


def emit_L2_attn(k):
    waq = k.tw("L2_waq")
    wpq = k.tw("L2_wpq")
    xT = k.sb("at_xT", [128, 8 * T], BF16)
    OT = k.sb("at_OT", [128, 8 * T], BF16)
    QpT = k.sb("at_QT", [128, T], BF16)
    KpT = k.sb("at_KT", [128, T], BF16)
    Vp = k.sb("at_V", [128, 2080], BF16)
    k.memset("vector", Vp, 1.0)
    wq_sl = k.sb("at_wq", [128, 1024], BF16)
    wk_sl = k.sb("L2_wk", [128, 1024], BF16)
    wv_sl = k.sb("L2_wv", [128, 1024], BF16)
    ybuf = k.sb("at_ybuf", [128, D], BF16)

    for s in range(SEQ):
        pre_ln_xT(k, "at", s, list(range(NT)), xT, bank0=6)
        for pair in range(8):
            k.dma("sync", wq_sl.rearrange("p (dc c) -> p dc c", c=128),
                  waq[:, pair * 128:(pair + 1) * 128].rearrange("(dc p) c -> p dc c", p=128))
            k.dma("sync", wk_sl.rearrange("p (dc c) -> p dc c", c=128),
                  waq[:, D + pair * 128: D + (pair + 1) * 128].rearrange("(dc p) c -> p dc c", p=128))
            k.dma("sync", wv_sl.rearrange("p (dc c) -> p dc c", c=128),
                  waq[:, 2 * D + pair * 128: 2 * D + (pair + 1) * 128].rearrange("(dc p) c -> p dc c", p=128))
            for tc in range(4):
                bq = k.bank(0)
                bk = k.bank(2)
                for dc in range(8):
                    k.mm(bq, wq_sl[:, dc * 128:(dc + 1) * 128],
                         xT[:, dc * T + tc * 512: dc * T + (tc + 1) * 512],
                         start=(dc == 0), stop=(dc == 7))
                for dc in range(8):
                    k.mm(bk, wk_sl[:, dc * 128:(dc + 1) * 128],
                         xT[:, dc * T + tc * 512: dc * T + (tc + 1) * 512],
                         start=(dc == 0), stop=(dc == 7))
                k.copy("vector", QpT[:, tc * 512:(tc + 1) * 512], bq)
                k.copy("vector", KpT[:, tc * 512:(tc + 1) * 512], bk)
            for tt in range(NT):
                bv = k.bank(4)
                for dc in range(8):
                    k.mm(bv[:, 0:128], xT[:, dc * T + tt * 128: dc * T + (tt + 1) * 128],
                         wv_sl[:, dc * 128:(dc + 1) * 128],
                         start=(dc == 0), stop=(dc == 7))
                k.copy("vector", Vp[:, tt * 130: tt * 130 + 64], bv[:, 0:64])
                k.copy("vector", Vp[:, tt * 130 + 65: tt * 130 + 129], bv[:, 64:128])
            attn_pair(k, s, pair, QpT, KpT, Vp, OT, causal=True)
        wo = k.sb("at_wo", [128, 8192], BF16)
        k.dma("sync", wo.rearrange("p (pc c) -> p pc c", c=1024),
              wpq.rearrange("(pc p) c -> p pc c", p=128))
        for tt in range(NT):
            for half in range(2):
                b = k.bank(half)
                for pc in range(8):
                    k.mm(b, OT[:, pc * 2048 + tt * 128: pc * 2048 + (tt + 1) * 128],
                         wo[:, pc * 1024 + half * 512: pc * 1024 + (half + 1) * 512],
                         start=(pc == 0), stop=(pc == 7))
                k.copy("vector", ybuf[:, half * 512:(half + 1) * 512], b)
            post_ln_add(k, "at", s, tt, ybuf)


def emit_L2_ffn(k):
    Win = k.tw("L2f_win")
    binc = k.inp("L2f_binc", [64, 1], F32)
    w1c = k.inp("L2f_w1c", [D, 1], F32)
    b1c = k.inp("L2f_b1c", [D, 1], F32)
    w2s = k.tw("L2f_w2s")
    b2c = k.inp("L2f_b2c", [64, 1], F32)
    Wout = k.tw("L2f_wout")
    bout = k.tw("L2f_bout")
    repT = k.inp("cin_repT", [64, 1024], BF16)

    xT = k.sb("at_xT", [128, 8 * T], BF16)
    h_sb = k.sb("f2_h", [128, 8 * 512], BF16)
    win_sb = k.sb("f2_win", [128, 512], BF16)
    w2s_sb = k.sb("f2_w2s", [128, 512], BF16)
    rep_sb = k.sb("f2_rep", [64, 1024], BF16)
    wout_sb = k.sb("f2_wout", [64, D], BF16)
    bout_sb = k.sb("f2_bout", [1, D], BF16)
    s_bf = k.sb("f2_s", [64, 512], BF16)
    g_bf = k.sb("f2_g", [64, 512], BF16)
    w1cols = k.sb("f2_w1c", [128, 8], F32)
    b1cols = k.sb("f2_b1c", [128, 8], F32)
    bin_col = k.sb("f2_bin", [64, 1], F32)
    b2_col = k.sb("f2_b2", [64, 1], F32)
    ybuf = k.sb("at_ybuf", [128, D], BF16)
    ones = ones_bf(k)

    k.dma("sync", rep_sb, repT)
    k.dma("sync", wout_sb, Wout)
    k.dma("sync", bout_sb, bout)
    k.dma("sync", win_sb.rearrange("p (dc c) -> p dc c", c=64),
          Win.rearrange("(dc p) c -> p dc c", p=128))
    k.dma("sync", w2s_sb.rearrange("p (dc c) -> p dc c", c=64),
          w2s.rearrange("(dc p) c -> p dc c", p=128))
    for dc in range(8):
        k.dma("sync", w1cols[:, dc:dc + 1], w1c[dc * 128:(dc + 1) * 128, :])
        k.dma("sync", b1cols[:, dc:dc + 1], b1c[dc * 128:(dc + 1) * 128, :])
    k.dma("sync", bin_col, binc)
    k.dma("sync", b2_col, b2c)

    for s in range(SEQ):
        pre_ln_xT(k, "f2", s, list(range(NT)), xT, bank0=6)
        for tc in range(4):
            b0 = k.bank(0)
            for dc in range(8):
                k.mm(b0[0:64, :], win_sb[:, dc * 64:(dc + 1) * 64],
                     xT[:, dc * T + tc * 512: dc * T + (tc + 1) * 512],
                     start=(dc == 0), stop=(dc == 7))
            k.ts("vector", s_bf, b0[0:64, :], bin_col[0:64, 0:1], ALU.add)
            for mi in range(8):
                b1_ = k.bank(2 + 2 * (mi % 2))
                k.mm(b1_, rep_sb[:, mi * 128:(mi + 1) * 128], s_bf,
                     start=True, stop=True)
                k.ts("vector", h_sb[:, mi * 512:(mi + 1) * 512], b1_,
                     w1cols[:, mi:mi + 1], ALU.mult, b1cols[:, mi:mi + 1], ALU.add)
                k.ts("vector", h_sb[:, mi * 512:(mi + 1) * 512],
                     h_sb[:, mi * 512:(mi + 1) * 512], 0.0, ALU.max)
            b3 = k.bank(6)
            for mi in range(8):
                k.mm(b3[0:64, :], w2s_sb[:, mi * 64:(mi + 1) * 64],
                     h_sb[:, mi * 512:(mi + 1) * 512],
                     start=(mi == 0), stop=(mi == 7))
            k.ts("vector", g_bf, b3[0:64, :], b2_col[0:64, 0:1], ALU.add)
            for j in range(4):
                tt = tc * 4 + j
                for half in range(2):
                    bo = k.bank(1 + 2 * half)
                    k.mm(bo, g_bf[:, j * 128:(j + 1) * 128],
                         wout_sb[:, half * 512:(half + 1) * 512],
                         start=True, stop=False)
                    k.mm(bo, ones[0:1, 0:128], bout_sb[0:1, half * 512:(half + 1) * 512],
                         start=False, stop=True)
                    k.copy("vector", ybuf[:, half * 512:(half + 1) * 512], bo)
                post_ln_add(k, "f2", s, tt, ybuf)


def emit_L3_attn(k):
    Wqkv = k.tw("L3_wqkv")
    Wp = k.tw("L3_wp")
    bproj = k.tw("L3_bproj")
    qs_in = k.inp("L3_qs", [128, 1], F32)
    ks_in = k.inp("L3_ks", [128, 1], F32)
    blk = k.sb("c_blkones", [128, 128], BF16)
    xT = k.sb("at_xT", [128, 8 * T], BF16)
    OT = k.sb("at_OT", [128, 8 * T], BF16)
    QpT = k.sb("at_QT", [128, T], BF16)
    KpT = k.sb("at_KT", [128, T], BF16)
    Vp = k.sb("at_V", [128, 2080], BF16)
    k.memset("vector", Vp, 1.0)
    wq_sl = k.sb("at_wq", [128, 1024], BF16)
    wk_sl = k.sb("L2_wk", [128, 1024], BF16)
    wv_sl = k.sb("L2_wv", [128, 1024], BF16)
    sqb = k.sb("at_sqr", [128, 1024], BF16)
    rr = k.sb("f3_rr", [128, 512], F32)
    ybuf = k.sb("at_ybuf", [128, D], BF16)
    qs = k.col("L3_qsc")
    ks = k.col("L3_ksc")
    bproj_sb = k.sb("L3_bp", [1, D], BF16)
    eps8 = k.constcol("c_eps8", 1e-8)
    negln8 = k.constcol("c_negln8", -2.0794415416798357)
    ones = ones_bf(k)
    k.dma("sync", qs, qs_in)
    k.dma("sync", ks, ks_in)
    k.dma("sync", bproj_sb, bproj)

    def rmsnorm_evac(bq, dstsl, scol):
        sq_sl = sqb[:, 0:512]
        k.act("scalar", sq_sl, bq, AF.Square)
        bN = k.bank(5)
        k.mm(bN, blk, sq_sl, start=True, stop=True)
        k.act("scalar", rr, bN, AF.Ln)
        k.act("scalar", rr, rr, AF.Exp, scale=0.5, bias=negln8)
        k.ts("vector", rr, rr, 1e-8, ALU.add)
        k.recip(rr, rr)
        k.stt("vector", dstsl, bq, scol, rr, ALU.mult, ALU.mult)

    for s in range(SEQ):
        pre_ln_xT(k, "at", s, list(range(NT)), xT, bank0=6)
        for pair in range(8):
            k.dma("sync", wq_sl.rearrange("p (dc c) -> p dc c", c=128),
                  Wqkv[:, pair * 128:(pair + 1) * 128].rearrange("(dc p) c -> p dc c", p=128))
            k.dma("sync", wk_sl.rearrange("p (dc c) -> p dc c", c=128),
                  Wqkv[:, D + pair * 128: D + (pair + 1) * 128].rearrange("(dc p) c -> p dc c", p=128))
            k.dma("sync", wv_sl.rearrange("p (dc c) -> p dc c", c=128),
                  Wqkv[:, 2 * D + pair * 128: 2 * D + (pair + 1) * 128].rearrange("(dc p) c -> p dc c", p=128))
            for tc in range(4):
                bq = k.bank(0)
                bk = k.bank(2)
                for dc in range(8):
                    k.mm(bq, wq_sl[:, dc * 128:(dc + 1) * 128],
                         xT[:, dc * T + tc * 512: dc * T + (tc + 1) * 512],
                         start=(dc == 0), stop=(dc == 7))
                for dc in range(8):
                    k.mm(bk, wk_sl[:, dc * 128:(dc + 1) * 128],
                         xT[:, dc * T + tc * 512: dc * T + (tc + 1) * 512],
                         start=(dc == 0), stop=(dc == 7))
                rmsnorm_evac(bq, QpT[:, tc * 512:(tc + 1) * 512], qs)
                rmsnorm_evac(bk, KpT[:, tc * 512:(tc + 1) * 512], ks)
            for tt in range(NT):
                bv = k.bank(4)
                for dc in range(8):
                    k.mm(bv[:, 0:128], xT[:, dc * T + tt * 128: dc * T + (tt + 1) * 128],
                         wv_sl[:, dc * 128:(dc + 1) * 128],
                         start=(dc == 0), stop=(dc == 7))
                k.copy("vector", Vp[:, tt * 130: tt * 130 + 64], bv[:, 0:64])
                k.copy("vector", Vp[:, tt * 130 + 65: tt * 130 + 129], bv[:, 64:128])
            attn_pair(k, s, pair, QpT, KpT, Vp, OT, causal=False)
        wo = k.sb("at_wo", [128, 8192], BF16)
        k.dma("sync", wo.rearrange("p (pc c) -> p pc c", c=1024),
              Wp.rearrange("(pc p) c -> p pc c", p=128))
        for tt in range(NT):
            for half in range(2):
                b = k.bank(half)
                for pc in range(8):
                    k.mm(b, OT[:, pc * 2048 + tt * 128: pc * 2048 + (tt + 1) * 128],
                         wo[:, pc * 1024 + half * 512: pc * 1024 + (half + 1) * 512],
                         start=(pc == 0), stop=False)
                k.mm(b, ones[0:1, 0:128], bproj_sb[0:1, half * 512:(half + 1) * 512],
                     start=False, stop=True)
                k.copy("vector", ybuf[:, half * 512:(half + 1) * 512], b)
            post_ln_add(k, "at", s, tt, ybuf)


def emit_L3_ffn(k):
    gw = k.tw("L3f_gw")
    rs_in = k.inp("L3f_rs", [1, 1], F32)
    W1 = k.tw("L3f_w12")
    b1f = k.tw("L3f_b1")
    W2 = k.tw("L3f_w22")
    b2_in = k.tw("L3f_b2")
    Ws1 = k.tw("L3f_ws1")
    bs1_in = k.inp("L3f_bs1", [512, 1], F32)
    Ws2 = k.tw("L3f_ws2")
    bs2_in = k.tw("L3f_bs2")
    idb = ident_bf(k)
    ones = ones_bf(k)

    xT = k.sb("at_xT", [128, 8 * T], BF16)
    y = k.sb("f3_y", [128, NT * D], BF16)
    hsT = k.sb("f3_hsT", [128, 4 * T], BF16)
    ws1_sb = k.sb("f3_ws1", [128, 8 * 512], BF16)
    ws2_sb = k.sb("f3_ws2", [128, 4 * D], BF16)
    w1_sb = k.sb("f3_w1", [128, 8 * 512], BF16)
    w2_sb = k.sb("f3_w2", [128, 4 * D], BF16)
    gw_sb = k.sb("f3_gw", [128, 64], BF16)
    gates = k.sb("f3_gates", [128, NT * 8], F32)
    gb = k.sb("f3_gb", [128, 8], BF16)
    gT = k.sb("f3_gT", [8, 128], BF16)
    b1r = k.sb("f3_b1r", [1, 512], BF16)
    b2_sb = k.sb("f3_b2s", [8, D], BF16)
    bs1c = k.sb("f3_bs1c", [128, 4], F32)
    bs2_sb = k.sb("f3_bs2s", [1, D], BF16)
    hg = k.sb("f3_hg", [128, 512], BF16)
    hgT = k.sb("f3_hgT", [128, 512], BF16)
    sc_sb = k.sb("f3_sc", [128, 8], F32)
    ge1 = k.sb("f3_ge1", [128, 8], F32)
    slx = k.sb("f3_slx", [128, 8], F32)
    maskb = k.sb("f3_mask", [128, 8], F32)
    rs = k.col("f3_rs")
    m1 = k.col("f3_m1")
    m2 = k.col("f3_m2")
    den = k.col("f3_den")
    rden = k.col("f3_rden")

    k.dma("sync", rs, rs_in.broadcast_to([128, 1]))
    k.dma("sync", gw_sb.rearrange("p (dc c) -> p dc c", c=8),
          gw.rearrange("(dc p) c -> p dc c", p=128))
    for hc in range(4):
        k.dma("sync", bs1c[:, hc:hc + 1], bs1_in[hc * 128:(hc + 1) * 128, :])
    k.dma("sync", b2_sb, b2_in)
    k.dma("sync", bs2_sb, bs2_in)

    for s in range(SEQ):
        pre_ln_xT(k, "f3", s, list(range(NT)), xT, bank0=6)
        for tt in range(NT):
            bg = k.bank(5)
            for dc in range(8):
                k.mm(bg[:, 0:8], xT[:, dc * T + tt * 128: dc * T + (tt + 1) * 128],
                     gw_sb[:, dc * 8:(dc + 1) * 8], start=(dc == 0), stop=(dc == 7))
            k.act("scalar", sc_sb, bg[:, 0:8], AF.Exp, scale=-1.0)
            k.ts("vector", sc_sb, sc_sb, 1.0, ALU.add)
            k.recip(sc_sb, sc_sb)
            k.ts("vector", sc_sb, sc_sb, rs, ALU.mult)
            k.red("vector", m1, sc_sb, ALU.max)
            k.ts("vector", ge1, sc_sb, m1, ALU.is_ge)
            k.stt("vector", slx, ge1, -1e30, sc_sb, ALU.mult, ALU.add)
            k.red("vector", m2, slx, ALU.max)
            k.ts("vector", maskb, sc_sb, m2, ALU.is_ge)
            k.tt("vector", den, m1, m2, ALU.add)
            self_recip(k, rden, den)
            k.tt("vector", ge1, sc_sb, maskb, ALU.mult)
            k.ts("vector", gates[:, tt * 8:(tt + 1) * 8], ge1, rden, ALU.mult)
        k.dma("sync", ws1_sb.rearrange("p (dc c) -> p dc c", c=512),
              Ws1.rearrange("(dc p) c -> p dc c", p=128))
        for hc in range(4):
            for tc in range(4):
                bh = k.bank(2 * (tc % 2))
                for dc in range(8):
                    k.mm(bh, ws1_sb[:, dc * 512 + hc * 128: dc * 512 + (hc + 1) * 128],
                         xT[:, dc * T + tc * 512: dc * T + (tc + 1) * 512],
                         start=(dc == 0), stop=(dc == 7))
                k.act("scalar", hsT[:, hc * T + tc * 512: hc * T + (tc + 1) * 512],
                      bh, AF.Gelu, bias=bs1c[:, hc:hc + 1])
        k.dma("sync", ws2_sb.rearrange("p (hc c) -> p hc c", c=D),
              Ws2.rearrange("(hc p) c -> p hc c", p=128))
        for tt in range(NT):
            k.copy("vector", gb, gates[:, tt * 8:(tt + 1) * 8])
            bt = k.bank_bf(4)
            k.trans(bt[0:8, 0:128], gb, idb)
            k.copy("vector", gT, bt[0:8, 0:128])
            for half in range(2):
                bo = k.bank(2 + half)
                for hc in range(4):
                    k.mm(bo, hsT[:, hc * T + tt * 128: hc * T + (tt + 1) * 128],
                         ws2_sb[:, hc * D + half * 512: hc * D + (half + 1) * 512],
                         start=(hc == 0), stop=False)
                k.mm(bo, ones[0:1, 0:128], bs2_sb[0:1, half * 512:(half + 1) * 512],
                     start=False, stop=False)
                k.mm(bo, gT, b2_sb[0:8, half * 512:(half + 1) * 512],
                     start=False, stop=True)
                k.copy("vector", y[:, tt * D + half * 512: tt * D + (half + 1) * 512], bo)
        for e in range(8):
            w1_sb = k.sb("wring1", [128, 8 * 512], BF16) if e % 2 else k.sb("f3_w1", [128, 8 * 512], BF16)
            k.dma("sync", b1r, b1f[0:1, e * 512:(e + 1) * 512])
            k.dma("sync", w1_sb.rearrange("p (dc c) -> p dc c", c=512),
                  W1[e * D:(e + 1) * D, :].rearrange("(dc p) c -> p dc c", p=128))
            k.dma("sync", w2_sb.rearrange("p (hc c) -> p hc c", c=D),
                  W2[e * 512:(e + 1) * 512, :].rearrange("(hc p) c -> p hc c", p=128))
            for tt in range(NT):
                bh = k.bank(2 * (tt % 2))
                for dc in range(8):
                    k.mm(bh, xT[:, dc * T + tt * 128: dc * T + (tt + 1) * 128],
                         w1_sb[:, dc * 512:(dc + 1) * 512],
                         start=(dc == 0), stop=False)
                k.mm(bh, ones[0:1, 0:128], b1r,
                     start=False, stop=True)
                k.act("scalar", hg, bh, AF.Gelu)
                k.ts("vector", hg, hg, gates[:, tt * 8 + e: tt * 8 + e + 1], ALU.mult)
                bt = k.bank_bf(4)
                for j in range(4):
                    k.trans(bt[:, j * 128:(j + 1) * 128], hg[:, j * 128:(j + 1) * 128], idb)
                k.copy("vector", hgT, bt[:, 0:512])
                for half in range(2):
                    bo = k.bank(5 + 2 * half)
                    for hc in range(4):
                        k.mm(bo, hgT[:, hc * 128:(hc + 1) * 128],
                             w2_sb[:, hc * D + half * 512: hc * D + (half + 1) * 512],
                             start=(hc == 0), stop=(hc == 3))
                    ysl = y[:, tt * D + half * 512: tt * D + (half + 1) * 512]
                    k.stt("vector", ysl, bo, 1.0, ysl, ALU.bypass, ALU.add)
        for tt in range(NT):
            post_ln_add(k, "f3", s, tt, y[:, tt * D:(tt + 1) * D])


def self_recip(k, out, in_):
    def f(e, o=out, i=in_):
        return e.reciprocal(o, i)
    k._rec("vector", f, reads=[in_], writes=[out])


BLOCK_EMITTERS = {
    0: emit_L0_attn,
    1: emit_L0_ffn,
    2: emit_L1_attn,
    3: emit_L1_ffn,
    4: emit_L2_attn,
    5: emit_L2_ffn,
    6: emit_L3_attn,
    7: emit_L3_ffn,
}



def build_program(blocks, first, last):
    nc = bass.Bass("TRN2", target_bir_lowering=False)
    k = K(nc)
    if first:
        x_in = k.inp("x_in", [SEQ * T, D], F32)
    else:
        x_in = k.inp("x_in", [SEQ * T, D], BF16)
    st = k.sb("stream", [128, SEQ * NT * D], BF16)
    xr = x_in.rearrange("(n p) d -> p n d", p=128)
    if first:
        for i in range(SEQ * NT):
            stg = k.sb(f"ldstage{i % 2}", [128, D], F32)
            k.dma("sync", stg, xr[:, i, :])
            k.copy("scalar" if i % 2 else "vector", st[:, i * D:(i + 1) * D], stg)
    else:
        for g in range(8):
            q = "sync" if g % 2 else "gpsimd"
            k.dma(q, st.rearrange("p (n d) -> p n d", d=D)[:, g * 4:(g + 1) * 4, :],
                  xr[:, g * 4:(g + 1) * 4, :])
    idb = ident_bf(k)
    k.dma("sync", idb, k.inp("c_ident", [128, 128], BF16))
    k.dma("sync", k.sb("c_blkones", [128, 128], BF16), k.inp("cin_blkones", [128, 128], BF16))
    k.dma("sync", k.sb("c_trimask", [128, 128], BF16), k.inp("cin_trimask", [128, 128], BF16))
    k.dma("sync", k.sb("c_cmaskT", [128, 2048], BF16), k.inp("cin_cmaskT", [128, 2048], BF16))
    emit_weight_prep(k)

    for b in blocks:
        BLOCK_EMITTERS[b](k)

    for tag in _DEBUG_TAPS:
        info = getattr(k, "_sbinfo", {}).get(tag)
        if info is not None:
            ap = k.sb(tag, info[0], info[1])
            o = k.outp("dbg_" + tag, list(ap.shape), ap.dtype)
            k.out_dma("sync", o, ap)

    x_out = k.outp("x_out", [SEQ * T, D], F32 if last else BF16)
    xo = x_out.rearrange("(n p) d -> p n d", p=128)
    for g in range(8):
        q = "gpsimd" if last else ("sync" if g % 2 else "gpsimd")
        k.out_dma(q, xo[:, g * 4:(g + 1) * 4, :],
                  st.rearrange("p (n d) -> p n d", d=D)[:, g * 4:(g + 1) * 4, :])
    k.build()
    return nc, sorted(n for n in k._dram_in if not n.startswith('#'))


def host_prep(inputs):
    w = {}
    c = np.ascontiguousarray
    w["c_ident"] = np.eye(128, dtype=BF)
    w["L0_W_T"] = c(inputs["L0_attn_W"].T)
    w["L0_b"] = c(inputs["L0_attn_b"].reshape(D, 1))
    w["L0f_qT"] = c(inputs["L0_ffn_qweights"].transpose(0, 2, 1))
    w["L0f_scT"] = c(inputs["L0_ffn_scales"].transpose(0, 2, 1))
    w["L0f_laT"] = c(inputs["L0_ffn_lora_a"].transpose(0, 2, 1))
    w["L0f_lbT"] = c(inputs["L0_ffn_lora_b"].transpose(0, 2, 1))
    w["L0f_biases"] = c(inputs["L0_ffn_biases"].reshape(6, D, 1))
    w["L0f_gam"] = c(inputs["L0_ffn_ln_gamma"].reshape(3, D, 1))
    w["L0f_bet"] = c(inputs["L0_ffn_ln_beta"].reshape(3, D, 1))
    blk = np.zeros((128, 128), dtype=BF)
    blk[0:64, 0:64] = 1
    blk[64:128, 64:128] = 1
    w["cin_blkones"] = blk
    w["cin_trimask"] = np.tril(np.ones((128, 128))).astype(BF)
    cmT = np.zeros((128, 4, 512), np.float32)
    for r in range(4):
        for i_ in range(128):
            cmT[i_, r, :] = (128 * r + i_ <= np.arange(512))
    w["cin_cmaskT"] = np.ascontiguousarray(cmT.reshape(128, 2048)).astype(BF)
    w["L1_Wq"] = c(inputs["L1_attn_Wq"].reshape(D, D))
    w["L1_Wkvd"] = c(inputs["L1_attn_Wkv_down"])
    w["L1_Wku"] = c(inputs["L1_attn_Wk_up"].reshape(256, D))
    w["L1_Wvu"] = c(inputs["L1_attn_Wv_up"].reshape(256, D))
    w["L1_Wout"] = c(inputs["L1_attn_Wout"].reshape(D, D))
    w["L1f_wrT"] = c(inputs["L1_ffn_w_router"].T)
    w["L1f_cent"] = c(inputs["L1_ffn_head_centroids"])
    w["L1f_temp"] = c(inputs["L1_ffn_temperature"].reshape(1, 1))
    w["L1f_ie"] = c(inputs["L1_ffn_input_experts"])
    w["L1f_oe"] = c(inputs["L1_ffn_output_experts"])
    w["L2_waT"] = c(inputs["L2_attn_w_attn"].T)
    w["L2_wpT"] = c(inputs["L2_attn_w_proj"].T)
    w["L2f_win"] = c(inputs["L2_ffn_Win"])
    w["L2f_binc"] = c(inputs["L2_ffn_bin_"].reshape(64, 1))
    w["L2f_w1c"] = c(inputs["L2_ffn_W1"].reshape(D, 1))
    w["L2f_b1c"] = c(inputs["L2_ffn_b1"].reshape(D, 1))
    w2s = np.zeros((D, 64), np.float32)
    for m in range(64):
        w2s[m * 16:(m + 1) * 16, m] = inputs["L2_ffn_W2"][m]
    w["L2f_w2s"] = w2s
    w["L2f_b2c"] = c(inputs["L2_ffn_b2"].reshape(64, 1))
    w["L2f_wout"] = c(inputs["L2_ffn_Wout"])
    w["L2f_bout"] = c(inputs["L2_ffn_bout"].reshape(1, D))
    w["L3_wqkv"] = c(inputs["L3_attn_Wqkv"])
    w["L3_wp"] = c(inputs["L3_attn_Wproj"])
    w["L3_bproj"] = c(inputs["L3_attn_bproj"].reshape(1, D))
    w["L3_qs"] = c(np.tile(inputs["L3_attn_q_scale"], 2).reshape(128, 1))
    w["L3_ks"] = c(np.tile(inputs["L3_attn_k_scale"], 2).reshape(128, 1))
    w["L3f_gw"] = c(inputs["L3_ffn_gate_W"])
    w["L3f_rs"] = c(np.asarray(inputs["L3_ffn_route_scale"], np.float32).reshape(1, 1))
    w["L3f_w1"] = c(inputs["L3_ffn_W1"])
    w["L3f_b1"] = c(inputs["L3_ffn_b1"].reshape(1, 8 * 512))
    w["L3f_w2"] = c(inputs["L3_ffn_W2"])
    w["L3f_b2"] = c(inputs["L3_ffn_b2"])
    w["L3f_ws1"] = c(inputs["L3_ffn_Ws1"])
    w["L3f_bs1"] = c(inputs["L3_ffn_bs1"].reshape(512, 1))
    w["L3f_ws2"] = c(inputs["L3_ffn_Ws2"])
    w["L3f_bs2"] = c(inputs["L3_ffn_bs2"].reshape(1, D))
    w["L0f_laT2"] = w["L0f_laT"].reshape(6 * D, 32)
    w["L0f_lbT2"] = w["L0f_lbT"].reshape(6 * 32, D)
    w["L1f_ie2"] = w["L1f_ie"].reshape(16 * D, 512)
    w["L1f_oe2"] = w["L1f_oe"].reshape(16 * 512, 64)
    w["L3f_w12"] = w["L3f_w1"].reshape(8 * D, 512)
    w["L3f_w22"] = w["L3f_w2"].reshape(8 * 512, D)
    rep = np.zeros((64, 1024), dtype=BF)
    for m in range(64):
        rep[m, m * 16:(m + 1) * 16] = 1
    w["cin_repT"] = rep
    return w


def kernel(**inputs):
    groups = _GROUPS if _GROUPS is not None else [list(_BLOCKS)]
    x = inputs["L0_attn_x"].astype(np.float32)
    shards = [c_.reshape(SEQ * T, D) for c_ in np.split(x, NCORES, axis=0)]
    hw = host_prep(inputs)

    cur = shards
    for gi, blocks in enumerate(groups):
        first = (gi == 0)
        last = (gi == len(groups) - 1)
        nc, in_names = build_program(blocks, first, last)
        in_maps = []
        for ci in range(NCORES):
            m = {}
            for name in in_names:
                if name == "x_in":
                    m[name] = np.ascontiguousarray(cur[ci])
                else:
                    m[name] = hw[name]
            in_maps.append(m)
        res = run_bass_kernel_spmd(nc, in_maps, core_ids=list(range(NCORES)))
        for name in res.results[0]:
            if name.startswith("dbg_"):
                _DEBUG_RES[name] = [np.asarray(res.results[ci][name]) for ci in range(NCORES)]
        cur = [res.results[ci]["x_out"] for ci in range(NCORES)]

    out = np.stack([np.asarray(cur[ci], dtype=np.float32).reshape(SEQ, T, D)
                    for ci in range(NCORES)], axis=0).reshape(16, T, D)
    return out


# version 10 (final)
# speedup vs baseline: 1.5738x; 1.1117x over previous
import numpy as np
import ml_dtypes
import concourse.bass as bass
import concourse.mybir as mybir
from concourse.bass_utils import run_bass_kernel_spmd

F32 = mybir.dt.float32
BF16 = mybir.dt.bfloat16
I32 = mybir.dt.int32
AF = mybir.ActivationFunctionType
ALU = mybir.AluOpType
AX = mybir.AxisListType.X
BF = ml_dtypes.bfloat16

NCORES = 8
SEQ = 2
T = 2048
D = 1024
NT = T // 128
EPS_LN = 1e-5

ENGINES = ("sync", "tensor", "vector", "scalar", "gpsimd")
STRICT_SAME_ENGINE_WARWAW_OFF = False

_BLOCKS = list(range(8))
_GROUPS = None
_DEBUG_TAPS = []
_DEBUG_RES = {}
_L0F_NFB = 3


class _Op:
    __slots__ = ("eng", "idx", "dom", "dpos", "fn", "deps", "inc", "amt", "clock", "dsem")


class K:

    def __init__(self, nc):
        self.nc = nc
        self.ops = {e: [] for e in ENGINES}
        self.lastw = {}
        self.readers = {}
        self.know = {e: {} for e in ENGINES}
        self.ncomp = {e: 0 for e in ENGINES}
        self.ndma = {e: 0 for e in ENGINES}
        self._sb = {}
        self._dram_in = {}
        self._dram_out = {}
        self._banks = None
        self.out_dma_ops = []
        self.dma_cum = {}
        self.dma_sem_of = {}

    ARENAS = {"arA": 32768, "arB": 32768, "arC": 16448, "arD": 16384,
              "arE": 8192, "arF": 8192}
    ALIAS = {
        "L0_xT": ("arE", 0),
        "L0_lamT": ("arA", 0), "L0_ugT": ("arA", 16384),
        "L0_ysT": ("arB", 0), "L0_ysTb": ("arB", 16384),
        "L0_WTb": ("arD", 0),
        "L0f_hT": ("arA", 0), "L0f_y1": ("arB", 0), "L0f_wdq": ("arD", 0),
        "L0f_sq": ("arC", 0),
        "L0f_scb": ("arE", 0), "L0f_muB": ("arE", 4096), "L0f_musqB": ("arE", 6144),
        "L0f_varB": ("arF", 0), "L0f_AB": ("arF", 2048), "L0f_BB": ("arF", 4096),
        "L0f_tmpB": ("arF", 6144),
        "at_xT": ("arA", 0), "at_OT": ("arB", 0),
        "at_QT": ("arC", 0), "at_KT": ("arC", 4096), "at_V": ("arC", 8192),
        "at_P": ("arC", 12352),
        "at_rdn": ("arD", 4096), "at_drow": ("arD", 0),
        "at_wo": ("arD", 0),
        "L1_lat": ("arE", 0),
        "at_wq": ("arF", 0), "at_wk": ("arF", 2048), "at_wv": ("arF", 2560),
        "L1_wkvd": ("arF", 4096),
        "f1_y": ("arB", 0), "f1_hT": ("arD", 0), "f1_ie": ("arE", 0),
        "f1_wr": ("arF", 0), "f1_oe": ("arF", 2048),
        "f1_z": ("arF", 2560), "f1_zsq": ("arF", 3072), "f1_zbf": ("arF", 3328),
        "f1_gates": ("arF", 3584), "f1_cent": ("arF", 4608),
        "ldstage0": ("arC", 0), "ldstage1": ("arC", 4096),
        "wstage0": ("arC", 8192), "wstage1": ("arC", 10240),
        "p0_qi": ("arC", 12288), "p0_qf": ("arD", 0), "p0_scb": ("arD", 4096),
        "L2_wk": ("arF", 2048), "L2_wv": ("arF", 4096), "L2_qt": ("arE", 0),
        "f2_h": ("arE", 0), "f2_win": ("arF", 0), "f2_w2s": ("arF", 1024),
        "f2_wout": ("arF", 2048), "f2_bout": ("arF", 4096), "f2_rep": ("arD", 0),
        "f2_s": ("arF", 6144), "f2_g": ("arF", 7168),
        "f3_rr": ("arF", 6144),
        "L3_bp": ("arE", 0),
        "f3_y": ("arB", 0), "f3_hsT": ("arD", 0),
        "f3_b2s": ("arF", 0), "f3_bs2s": ("arF", 2048), "f3_hg": ("arF", 4096),
        "f3_hgT": ("arF", 5120), "f3_b1r": ("arF", 6144), "f3_gates": ("arF", 7168), "f3_gw": ("arF", 7680), "f3_gT": ("arF", 7808),
        "f3_ws1": ("arE", 0), "f3_ws2": ("arE", 0), "f3_w1": ("arE", 0),
        "f3_w2": ("arD", 0), "f3_sc": ("arE", 0), "f3_ge1": ("arE", 64), "f3_slx": ("arE", 128), "f3_mask": ("arE", 192), "f3_m1": ("arE", 256), "f3_m2": ("arE", 288), "f3_den": ("arE", 320), "f3_rden": ("arE", 352),
    }

    def sb(self, tag, shape, dtype):
        self._sbinfo = getattr(self, "_sbinfo", {})
        self._sbinfo[tag] = (list(shape), dtype)
        if tag in self.ALIAS:
            assert len(shape) == 2, tag
            ar, byte0 = self.ALIAS[tag]
            base = self.sb(ar, [128, self.ARENAS[ar] // 2], BF16)
            esz = 2 if dtype == BF16 else 4
            c0 = byte0 // esz
            n = shape[1]
            assert (byte0 + n * esz) <= self.ARENAS[ar], (tag, ar, byte0, shape)
            v = base.bitcast(dtype) if dtype != BF16 else base
            return v[0:shape[0], c0:c0 + n]
        if tag not in self._sb:
            self._sb[tag] = self.nc.alloc_sbuf_tensor(tag, list(shape), dtype).ap()
        ap = self._sb[tag]
        assert tuple(ap.shape) == tuple(shape), (tag, ap.shape, shape)
        return ap

    def col(self, tag):
        return self.sb(tag, [128, 1], F32)

    def constcol(self, tag, value):
        if tag not in self._sb:
            ap = self.sb(tag, [128, 1], F32)
            self.memset("vector", ap, value)
            return ap
        return self._sb[tag]

    def bank2(self, j):
        if self._banks is None:
            self._banks = [self.nc.alloc_psum_tensor(f"psb{q}", [128, 1024], F32).ap()
                           for q in range(4)]
        return self._banks[j]

    def bank(self, i):
        b = self.bank2(i // 2)
        return b[:, (i % 2) * 512:(i % 2 + 1) * 512]

    def bank_bf(self, i):
        b = self.bank2(i // 2).bitcast(BF16)
        return b[:, (i % 2) * 1024:(i % 2 + 1) * 1024]

    def inp(self, name, shape, dtype):
        if name not in self._dram_in:
            self._dram_in[name] = self.nc.dram_tensor(name, list(shape), dtype,
                                                      kind="ExternalInput").ap()
        return self._dram_in[name]

    def outp(self, name, shape, dtype):
        if name not in self._dram_out:
            self._dram_out[name] = self.nc.dram_tensor(name, list(shape), dtype,
                                                       kind="ExternalOutput").ap()
        return self._dram_out[name]

    def tw(self, name):
        return self._dram_in["#" + name + "_bf"]

    def idram(self, name, shape, dtype):
        if name not in self._dram_in:
            self._dram_in["#" + name] = self.nc.dram_tensor(name, list(shape), dtype,
                                                            kind="Internal").ap()
        return self._dram_in["#" + name]

    BUCKET = 2048

    @staticmethod
    def _keys(x):
        if isinstance(x, str):
            return [x]
        t = x.tensor
        name = t.name
        try:
            space = str(t.__class__.__name__)
        except Exception:
            space = ""
        if "PSum" in space:
            esz0 = 2 if x.dtype == BF16 else 4
            off0 = x.offset
            if not isinstance(off0, int):
                return [name]
            dims0 = list(x.ap)
            rs0 = dims0[0][0] if dims0 else 0
            if rs0 > 0:
                off0 = off0 % rs0
            span0 = 1
            for st0, ct0 in dims0[1:]:
                if st0 > 0:
                    span0 += st0 * (ct0 - 1)
            b0_ = (off0 * esz0) // 2048
            b1_ = ((off0 + span0) * esz0 - 1) // 2048
            return [(name, "bk", b) for b in range(b0_, b1_ + 1)]
        if "SBTensor" not in space:
            return [name]
        esz = 2 if x.dtype == BF16 else 4
        off = x.offset
        if not isinstance(off, int):
            return [name]
        dims = list(x.ap)
        rowstride = dims[0][0] if dims else 0
        if rowstride > 0:
            off = off % rowstride
        span = 1
        for st, ct in dims[1:]:
            if st > 0:
                span += st * (ct - 1)
        b0 = (off * esz) // K.BUCKET
        b1 = ((off + span) * esz - 1) // K.BUCKET
        return [(name, b) for b in range(b0, b1 + 1)]

    NDMASEM = 80

    def _rec(self, eng, fn, reads=(), writes=(), sreads=(), dma=False, batch=None):
        op = _Op()
        op.eng = eng
        op.fn = fn
        op.inc = False
        op.amt = 16 if dma else 1
        op.idx = len(self.ops[eng])
        if dma:
            wname = None
            for x in writes:
                if not isinstance(x, str):
                    wname = x.tensor.name
                    break
            if wname is None:
                wname = "_anon"
            if wname not in self.dma_sem_of:
                self.dma_sem_of[wname] = len(self.dma_sem_of)
            si = self.dma_sem_of[wname]
            assert si < self.NDMASEM, f"too many DMA-written buffers: {wname}"
            op.dom = f"dsem{si}"
            op.dsem = si
            op.dpos = 0
        else:
            op.dpos = self.ncomp[eng]
            self.ncomp[eng] += 1
            op.dom = eng
            op.dsem = None
        rk = [kk for x in reads for kk in self._keys(x)]
        sk = [kk for x in sreads for kk in self._keys(x)]
        wk = [kk for x in writes for kk in self._keys(x)]
        raw = {}
        for kk in rk + sk:
            for lw in self.lastw.get(kk, {}).values():
                raw[lw] = True
        for kk in wk:
            for lw in self.lastw.get(kk, {}).values():
                raw[lw] = raw.get(lw, False)
            for r in self.readers.get(kk, ()):
                if r is not None:
                    raw[r] = raw.get(r, False)
        kn = self.know[eng]
        deps = []
        for prod in sorted(raw.keys(), key=lambda o: (o.eng, o.idx)):
            strict = raw[prod]
            if prod is op:
                continue
            if prod.dsem is not None:
                val = self.dma_cum[prod.dsem]
                if kn.get(prod.dom, -1) >= val:
                    continue
                deps.append((prod, val))
                kn[prod.dom] = val
                for d2, p2 in prod.clock.items():
                    if kn.get(d2, -1) < p2:
                        kn[d2] = p2
                continue
            if kn.get(prod.dom, -1) >= prod.dpos:
                if prod.eng != "tensor":
                    prod.inc = True
                continue
            if (not dma) and prod.dom == "tensor" and eng == "tensor":
                continue
            if STRICT_SAME_ENGINE_WARWAW_OFF and (not strict) and (not dma) and prod.dom == eng:
                continue
            deps.append((prod, None))
            for d2, p2 in prod.clock.items():
                if kn.get(d2, -1) < p2:
                    kn[d2] = p2
        if dma:
            self.dma_cum[op.dsem] = self.dma_cum.get(op.dsem, 0) + 16
            op.dpos = self.dma_cum[op.dsem]
        op.deps = deps
        op.clock = dict(kn)
        op.clock[op.dom] = op.dpos
        self.ops[eng].append(op)
        for kk in rk + sk:
            self.readers.setdefault(kk, []).append(op)
        for kk in wk:
            d = self.lastw.setdefault(kk, {})
            old = d.get(op.dom)
            if old is not None and old is not op and old.eng != "tensor":
                old.inc = True
            d[op.dom] = op
            self.readers[kk] = []
        return op

    def dma(self, q, out, in_, batch=None):
        self._rec(q, lambda e, o=out, i=in_: e.dma_start(o, i),
                  reads=[in_], writes=[out], dma=True, batch=batch)

    def copy(self, eng, out, in_):
        if eng == "scalar":
            self._rec(eng, lambda e, o=out, i=in_: e.copy(o, i),
                      reads=[in_], writes=[out])
        else:
            self._rec(eng, lambda e, o=out, i=in_: e.tensor_copy(o, i),
                      reads=[in_], writes=[out])

    def memset(self, eng, ap, v):
        self._rec(eng, lambda e, a=ap, vv=v: e.memset(a, vv), writes=[ap])

    def tt(self, eng, out, a, b, op):
        self._rec(eng, lambda e, o=out, x=a, y=b, p=op: e.tensor_tensor(o, x, y, p),
                  reads=[a, b], writes=[out])

    def ts(self, eng, out, a, s1, op0, s2=None, op1=None, accum=None):
        sreads = [s for s in (s1, s2) if isinstance(s, bass.AP)]
        outs = [out] + ([accum] if accum is not None else [])
        def f(e, o=out, x=a, s1=s1, s2=s2, op0=op0, op1=op1, acc=accum):
            if op1 is None:
                return e.tensor_single_scalar(o, x, s1, op0)
            return e.tensor_scalar(o, x, s1, s2, op0, op1, accum_out=acc)
        self._rec(eng, f, reads=[a], writes=outs, sreads=sreads)

    def stt(self, eng, out, in0, s, in1, op0, op1, accum=None):
        sreads = [s] if isinstance(s, bass.AP) else []
        outs = [out] + ([accum] if accum is not None else [])
        def f(e, o=out, a=in0, ss=s, b=in1, p0=op0, p1=op1, acc=accum):
            return e.scalar_tensor_tensor(o, a, ss, b, p0, p1, accum_out=acc)
        self._rec(eng, f, reads=[in0, in1], writes=outs, sreads=sreads)

    def scan(self, eng, out, a, b, init):
        sreads = [init] if isinstance(init, bass.AP) else []
        def f(e, o=out, x=a, y=b, i=init):
            return e.tensor_tensor_scan(o, x, y, i, ALU.mult, ALU.add)
        self._rec(eng, f, reads=[a, b], writes=[out], sreads=sreads)

    def act(self, eng, out, in_, func, bias=0.0, scale=1.0, accum=None):
        sreads = [s for s in (bias, scale) if isinstance(s, bass.AP)]
        outs = [out] + ([accum] if accum is not None else [])
        def f(e, o=out, i=in_, fn=func, b=bias, s=scale, acc=accum):
            return e.activation(o, i, fn, bias=b, scale=s, accum_out=acc)
        self._rec(eng, f, reads=[in_], writes=outs, sreads=sreads)

    def recip(self, out, in_):
        self._rec("vector", lambda e, o=out, i=in_: e.reciprocal(o, i),
                  reads=[in_], writes=[out])

    def red(self, eng, out, in_, op, absval=None):
        def f(e, o=out, i=in_, p=op, av=absval):
            return e.tensor_reduce(o, i, AX, p, apply_absolute_value=av)
        self._rec(eng, f, reads=[in_], writes=[out])

    def mm(self, out, lhsT, rhs, start, stop, tp=None):
        def f(e, o=out, l=lhsT, r=rhs, st=start, sp=stop, t=tp):
            return e.matmul(o, l, r, start=st, stop=sp, tile_position=t)
        self._rec("tensor", f, reads=[lhsT, rhs], writes=[out])

    def trans(self, out, in_, ident):
        def f(e, o=out, i=in_, d=ident):
            return e.transpose(o, i, d)
        self._rec("tensor", f, reads=[in_, ident], writes=[out])

    def out_dma(self, q, out, in_):
        src = self._rec(q, lambda e, o=out, i=in_: e.dma_start(o, i),
                        reads=[in_], writes=[out], dma=True)
        self.out_dma_ops.append(src)

    def load_cast(self, dst, src, stage_tag, eng="vector"):
        n = 1
        for d in src.shape[1:]:
            n *= d
        p = src.shape[0]
        assert n <= 1024 and len(src.shape) == 2
        stage = self.sb(stage_tag, [128, 1024], F32)
        self.dma("sync", stage[0:p, 0:n], src)
        self.copy(eng, dst, stage[0:p, 0:n])

    def build(self):
        if self.out_dma_ops:
            kn = self.know["sync"]
            op = _Op()
            op.eng = "sync"
            op.idx = len(self.ops["sync"])
            op.dom = "sync"
            op.dpos = self.ncomp["sync"]
            op.fn = lambda e: e.nop()
            op.inc = False
            op.amt = 1
            op.dsem = None
            deps = []
            for prod in self.out_dma_ops:
                val = self.dma_cum[prod.dsem]
                if kn.get(prod.dom, -1) < val:
                    deps.append((prod, val))
                    kn[prod.dom] = val
            op.deps = deps
            op.clock = dict(kn)
            self.ops["sync"].append(op)
        for e in ENGINES:
            for op in self.ops[e]:
                if op.amt == 16:
                    op.inc = True
                for prod, _v in op.deps:
                    prod.inc = True
        pc = {}
        for e in ENGINES:
            tot = 0
            lst = []
            for op in self.ops[e]:
                if op.inc and op.dsem is None:
                    tot += op.amt
                lst.append(tot)
            pc[e] = lst
        sems = {e: self.nc.alloc_semaphore(f"prog_{e}") for e in ENGINES}
        dsems = [self.nc.alloc_semaphore(f"dma_{i}") for i in range(self.NDMASEM)]
        nwaits = sum(len(op.deps) for e in ENGINES for op in self.ops[e])
        nincs = sum(1 for e in ENGINES for op in self.ops[e] if op.inc)
        nops = sum(len(self.ops[e]) for e in ENGINES)
        print(f"[K] ops={nops} waits={nwaits} incs={nincs} "
              + " ".join(f"{e}:{len(self.ops[e])}" for e in ENGINES))
        with self.nc.Block() as block:
            for e in ENGINES:
                def mk(e):
                    def body(eng):
                        for op in self.ops[e]:
                            for prod, val in op.deps:
                                if prod.dsem is not None:
                                    eng.wait_ge(dsems[prod.dsem], val)
                                else:
                                    eng.wait_ge(sems[prod.eng], pc[prod.eng][prod.idx])
                            inst = op.fn(eng)
                            if op.inc:
                                if op.dsem is not None:
                                    inst.then_inc(dsems[op.dsem], 16)
                                else:
                                    inst.then_inc(sems[e], op.amt)
                    return body
                getattr(block, e)(mk(e))



W_MANIFEST = [
    ("L0_W_T", [D, D]),
    ("L0f_laT2", [6 * D, 32]), ("L0f_lbT2", [6 * 32, D]),
    ("L1_Wq", [D, D]), ("L1_Wkvd", [D, 256]), ("L1_Wku", [256, D]),
    ("L1_Wvu", [256, D]), ("L1_Wout", [D, D]),
    ("L1f_wrT", [D, 128]), ("L1f_ie2", [16 * D, 512]), ("L1f_oe2", [16 * 512, 64]),
    ("L2f_win", [D, 64]), ("L2f_w2s", [D, 64]), ("L2f_wout", [64, D]),
    ("L2f_bout", [1, D]),
    ("L3_wqkv", [D, 3 * D]), ("L3_wp", [D, D]), ("L3_bproj", [1, D]),
    ("L3f_gw", [D, 8]), ("L3f_w12", [8 * D, 512]), ("L3f_w22", [8 * 512, D]),
    ("L3f_ws1", [D, 512]), ("L3f_ws2", [512, D]),
    ("L3f_b1", [1, 8 * 512]), ("L3f_b2", [8, D]), ("L3f_bs2", [1, D]),
]


def conv_chunk(k, dst_bf_dram, src_f32_dram, i):
    stg = k.sb(f"ldstage{i % 2}", [128, 1024], F32)
    wst = k.sb(f"wstage{i % 2}", [128, 1024], BF16)
    p, n = src_f32_dram.shape
    k.dma("sync", stg[0:p, 0:n], src_f32_dram)
    k.copy("vector" if i % 2 else "scalar", wst[0:p, 0:n], stg[0:p, 0:n])
    k.dma("sync", dst_bf_dram, wst[0:p, 0:n])


def emit_weight_prep(k):
    for name, shape in W_MANIFEST:
        src = k.inp(name, shape, F32)
        twin = k.idram(name + "_bf", shape, BF16)
        k.dma("gpsimd", twin, src)
    qT = k.inp("L0f_qT", [6, D, D], I32)
    scT = k.inp("L0f_scT", [6, 64, D], F32)
    wdqt = k.idram("L0f_wdq_bf", [6, D, D], BF16)
    qi = k.sb("p0_qi", [128, D], I32)
    qf = k.sb("p0_qf", [128, D], F32)
    sc_sb = k.sb("p0_scb", [128, D], F32)
    for l in range(6):
        for dc in range(8):
            k.dma("sync", qi, qT[l, dc * 128:(dc + 1) * 128, :])
            for g in range(8):
                k.dma("sync" if g % 2 else "gpsimd", sc_sb[16 * g:16 * (g + 1), :],
                      scT[l, 8 * dc + g, :].unsqueeze(0).broadcast_to([16, D]))
            k.copy("vector", qf, qi)
            k.ts("vector", qf, qf, 1.0 / 7.5, ALU.mult, -1.0, ALU.add)
            wst = k.sb(f"wstage{dc % 2}", [128, 1024], BF16)
            k.tt("vector", wst, qf, sc_sb, ALU.mult)
            k.dma("sync", wdqt[l, dc * 128:(dc + 1) * 128, :], wst)
    waT = k.inp("L2_waT", [D, 3 * D], F32)
    wpT = k.inp("L2_wpT", [D, D], F32)
    waq = k.idram("L2_waq_bf", [D, 3 * D], BF16)
    wpq = k.idram("L2_wpq_bf", [D, D], BF16)
    sa = k.col("L2_sa")
    ra = k.col("L2_ra")
    sp = k.col("L2_sp")
    rp = k.col("L2_rp")
    quant_mean(k, "L2qa", waT, 3 * D, sa, ra, 3 * D * D)
    quant_mean(k, "L2qp", wpT, D, sp, rp, D * D)
    for (W, Q, nc_, s_, r_) in ((waT, waq, 3 * D, sa, ra), (wpT, wpq, D, sp, rp)):
        for pc in range(8):
            for cc in range(nc_ // 1024):
                wst = k.sb(f"wstage{(pc + cc) % 2}", [128, 1024], BF16)
                quant_slice(k, wst, W[pc * 128:(pc + 1) * 128, cc * 1024:(cc + 1) * 1024],
                            f"ldstage{(pc + cc) % 2}", s_, r_)
                k.dma("sync", Q[pc * 128:(pc + 1) * 128, cc * 1024:(cc + 1) * 1024], wst)



def stream_tile(k, s, t):
    st = k.sb("stream", [128, SEQ * NT * D], BF16)
    i = s * NT + t
    return st[:, i * D:(i + 1) * D]


def ident_bf(k):
    ap = k.sb("ident_bf", [128, 128], BF16)
    return ap


def ln_stats_core(k, tag, src, mu_sl, var_sl, n=D):
    s_col = k.col(f"{tag}_s")
    ss_col = k.col(f"{tag}_ss")
    musq = k.col(f"{tag}_musq")
    sq = k.sb("lnsq", [128, n], BF16)
    k.red("vector", s_col, src, ALU.add)
    k.stt("vector", sq, src, 1.0, src, ALU.mult, ALU.mult, accum=ss_col)
    k.ts("vector", mu_sl, s_col, 1.0 / n, ALU.mult)
    k.tt("vector", musq, mu_sl, mu_sl, ALU.mult)
    k.stt("vector", var_sl, ss_col, 1.0 / n, musq, ALU.mult, ALU.subtract)


def ln_finish(k, tag, muB, varB, rstdB, nmrB):
    k.act("scalar", varB, varB, AF.Ln, bias=k.constcol("c_epsln", EPS_LN))
    k.act("scalar", rstdB, varB, AF.Exp, scale=-0.5)
    k.stt("vector", nmrB, muB, -1.0, rstdB, ALU.mult, ALU.mult)


def pre_ln_xT(k, tag, s, tiles, xT, bank0=6):
    W = 128 * len(tiles)
    B = len(tiles)
    idb = ident_bf(k)
    muB = k.sb("pre_muB", [128, 16], F32)
    varB = k.sb("pre_varB", [128, 16], F32)
    rstdB = k.sb("pre_rstdB", [128, 16], F32)
    nmrB = k.sb("pre_nmrB", [128, 16], F32)
    for j, t in enumerate(tiles):
        ln_stats_core(k, f"ln{j % 2}", stream_tile(k, s, t),
                      muB[:, j:j + 1], varB[:, j:j + 1])
    ln_finish(k, "pre", muB[:, 0:B], varB[:, 0:B], rstdB[:, 0:B], nmrB[:, 0:B])
    for j, t in enumerate(tiles):
        st = stream_tile(k, s, t)
        xn = k.sb(f"xn{j % 2}", [128, D], BF16)
        k.ts("vector", xn, st, rstdB[:, j:j + 1], ALU.mult, nmrB[:, j:j + 1], ALU.add)
        bank = k.bank_bf(bank0 - 1 + 2 * (j % 2))
        for dc in range(8):
            k.trans(bank[:, dc * 128:(dc + 1) * 128], xn[:, dc * 128:(dc + 1) * 128], idb)
        dst = xT.rearrange("p (dc w) -> p dc w", w=W)[:, :, j * 128:(j + 1) * 128]
        src = bank.rearrange("p (dc w) -> p dc w", w=128)
        k.copy("vector", dst, src)


def post_ln_add(k, tag, s, t, ybuf):
    st = stream_tile(k, s, t)
    p = t % 4
    mu = k.col(f"pl{p}_mu")
    var = k.col(f"pl{p}_var")
    rstd = k.col(f"pl{p}_rstd")
    nmr = k.col(f"pl{p}_nmr")
    ln_stats_core(k, f"pl{p}", ybuf, mu, var)
    ln_finish(k, f"pl{p}", mu, var, rstd, nmr)
    k.ts("vector", st, st, nmr, ALU.add)
    k.stt("vector", st, ybuf, rstd, st, ALU.mult, ALU.add)



def emit_L0_attn(k):
    bcol = k.inp("L0_b", [D, 1], F32)
    idb = ident_bf(k)
    WTb = k.sb("L0_WTb", [128, 8 * D], BF16)
    WTt = k.tw("L0_W_T")
    k.dma("sync", WTb.rearrange("p (dc c) -> p dc c", c=D),
          WTt.rearrange("(dc p) c -> p dc c", p=128))
    bsb = k.sb("L0_bias", [128, 8], F32)
    for ec in range(8):
        k.dma("sync", bsb[:, ec:ec + 1], bcol[ec * 128:(ec + 1) * 128, :])

    CH = 512
    NCH = T // CH
    xT = k.sb("L0_xT", [128, 8 * CH], BF16)
    lamT = k.sb("L0_lamT", [128, 8 * CH], F32)
    ugT = k.sb("L0_ugT", [128, 8 * CH], F32)
    ysT = k.sb("L0_ysT", [128, 8 * CH], F32)
    ysTb = k.sb("L0_ysTb", [128, 8 * CH], BF16)
    state = k.sb("L0_state", [128, 8], F32)
    ybuf = k.sb("at_ybuf", [128, D], BF16)

    for s in range(SEQ):
        k.memset("vector", state, 0.0)
        for c in range(NCH):
            tiles = [c * 4 + j for j in range(4)]
            pre_ln_xT(k, "L0", s, tiles, xT)
            for ec in range(8):
                bank = k.bank(2 * (ec % 2))
                for dc in range(8):
                    k.mm(bank, WTb[:, dc * D + ec * 128: dc * D + (ec + 1) * 128],
                         xT[:, dc * CH:(dc + 1) * CH], start=(dc == 0), stop=(dc == 7))
                k.act("scalar", lamT[:, ec * CH:(ec + 1) * CH], bank, AF.Sigmoid,
                      bias=bsb[:, ec:ec + 1])
            k.tt("vector", ugT, lamT, xT, ALU.mult)
            k.tt("vector", ugT, xT, ugT, ALU.subtract)
            for dc in range(8):
                k.scan("vector", ysT[:, dc * CH:(dc + 1) * CH],
                       lamT[:, dc * CH:(dc + 1) * CH], ugT[:, dc * CH:(dc + 1) * CH],
                       state[:, dc:dc + 1])
                k.copy("vector", state[:, dc:dc + 1], ysT[:, (dc + 1) * CH - 1:(dc + 1) * CH])
            k.copy("vector", ysTb, ysT)
            for j in range(4):
                bank = k.bank_bf(1 + 2 * (j % 2))
                for dc in range(8):
                    k.trans(bank[:, dc * 128:(dc + 1) * 128],
                            ysTb[:, dc * CH + j * 128: dc * CH + (j + 1) * 128], idb)
                k.copy("vector", ybuf, bank[:, 0:1024])
                post_ln_add(k, "L0", s, tiles[j], ybuf)


def ones_bf(k):
    if "ones_bf" not in k._sb:
        ap = k.sb("ones_bf", [128, 128], BF16)
        k.memset("vector", ap, 1.0)
        return ap
    return k._sb["ones_bf"]


def emit_L0_ffn(k):
    wdqt = k.tw("L0f_wdq")
    laT = k.tw("L0f_laT2")
    lbT = k.tw("L0f_lbT2")
    bias_in = k.inp("L0f_biases", [6, D, 1], F32)
    gam_in = k.inp("L0f_gam", [3, D, 1], F32)
    bet_in = k.inp("L0f_bet", [3, D, 1], F32)
    idb = ident_bf(k)
    ones = ones_bf(k)

    bias_sb = k.sb("L0f_bias", [128, 48], F32)
    gam_sb = k.sb("L0f_gamc", [128, 24], F32)
    bet_sb = k.sb("L0f_betc", [128, 24], F32)
    for l in range(6):
        for oc in range(8):
            k.dma("sync", bias_sb[:, l * 8 + oc: l * 8 + oc + 1],
                  bias_in[l, oc * 128:(oc + 1) * 128, :])
    for fb in range(3):
        for oc in range(8):
            k.dma("sync", gam_sb[:, fb * 8 + oc: fb * 8 + oc + 1],
                  gam_in[fb, oc * 128:(oc + 1) * 128, :])
            k.dma("sync", bet_sb[:, fb * 8 + oc: fb * 8 + oc + 1],
                  bet_in[fb, oc * 128:(oc + 1) * 128, :])

    hT = k.sb("L0f_hT", [128, 8 * T], BF16)
    y1 = k.sb("L0f_y1", [128, 8 * T], BF16)
    wdq = k.sb("L0f_wdq", [128, 8 * D], BF16)
    la_sb = k.sb("L0f_la", [128, 8 * 32], BF16)
    lb_sb = k.sb("L0f_lb", [32, D], BF16)
    u_sb = k.sb("L0f_u", [32, 512], BF16)
    sq_sb = k.sb("L0f_sq", [128, 8 * 512], BF16)
    muB = k.sb("L0f_muB", [128, 512], F32)
    musqB = k.sb("L0f_musqB", [128, 512], F32)
    varB = k.sb("L0f_varB", [128, 512], F32)
    AB = k.sb("L0f_AB", [128, 512], F32)
    BB = k.sb("L0f_BB", [128, 512], F32)
    tmpB = k.sb("L0f_tmpB", [128, 512], F32)
    ybuf = k.sb("at_ybuf", [128, D], BF16)
    epsc = k.constcol("c_epsln", EPS_LN)

    def load_qlora(l):
        k.dma("sync", wdq.rearrange("p (dc c) -> p dc c", c=D),
              wdqt[l].rearrange("(dc p) c -> p dc c", p=128))
        k.dma("sync", la_sb.rearrange("p (dc r) -> p dc r", r=32),
              laT[l * D:(l + 1) * D, :].rearrange("(dc p) r -> p dc r", p=128))
        k.dma("sync", lb_sb, lbT[l * 32:(l + 1) * 32, :])

    def qlora(l, rhs, out_mode, fb):
        load_qlora(l)
        for tc in range(4):
            ub = k.bank(5)
            for dc in range(8):
                k.mm(ub[0:32, :], la_sb[:, dc * 32:(dc + 1) * 32],
                     rhs[:, dc * T + tc * 512: dc * T + (tc + 1) * 512],
                     start=(dc == 0), stop=(dc == 7))
            k.copy("vector", u_sb, ub[0:32, :])
            for oc in range(8):
                b = k.bank(2 * (oc % 4))
                for dc in range(8):
                    k.mm(b, wdq[:, dc * D + oc * 128: dc * D + (oc + 1) * 128],
                         rhs[:, dc * T + tc * 512: dc * T + (tc + 1) * 512],
                         start=(dc == 0), stop=False)
                k.mm(b, lb_sb[:, oc * 128:(oc + 1) * 128], u_sb,
                     start=False, stop=True)
                dst = slice(oc * T + tc * 512, oc * T + (tc + 1) * 512)
                bc = bias_sb[:, l * 8 + oc: l * 8 + oc + 1]
                if out_mode == "y1":
                    k.ts("vector", y1[:, dst], b, bc, ALU.add)
                else:
                    k.stt("vector", hT[:, dst], b, bc, hT[:, dst], ALU.add, ALU.add)

    for s in range(SEQ):
        pre_ln_xT(k, "L0f", s, list(range(NT)), hT, bank0=6)
        for fb in range(_L0F_NFB):
            qlora(2 * fb, hT, "y1", fb)
            for tc in range(4):
                for oc in range(8):
                    sl = slice(oc * T + tc * 512, oc * T + (tc + 1) * 512)
                    k.tt("vector", sq_sb[:, oc * 512:(oc + 1) * 512], y1[:, sl], y1[:, sl], ALU.mult)
                bS, bQ = k.bank(1), k.bank(3)
                for oc in range(8):
                    sl = slice(oc * T + tc * 512, oc * T + (tc + 1) * 512)
                    k.mm(bS, ones, y1[:, sl], start=(oc == 0), stop=(oc == 7))
                for oc in range(8):
                    k.mm(bQ, ones, sq_sb[:, oc * 512:(oc + 1) * 512],
                         start=(oc == 0), stop=(oc == 7))
                k.ts("vector", muB, bS, 1.0 / D, ALU.mult)
                k.tt("vector", musqB, muB, muB, ALU.mult)
                k.stt("vector", varB, bQ, 1.0 / D, musqB, ALU.mult, ALU.subtract)
                k.act("scalar", varB, varB, AF.Ln, bias=epsc)
                k.act("scalar", AB, varB, AF.Exp, scale=-0.5)
                k.stt("vector", BB, muB, -1.0, AB, ALU.mult, ALU.mult)
                for oc in range(8):
                    sl = slice(oc * T + tc * 512, oc * T + (tc + 1) * 512)
                    k.tt("vector", tmpB, y1[:, sl], AB, ALU.mult)
                    k.tt("vector", tmpB, tmpB, BB, ALU.add)
                    k.act("scalar", y1[:, sl], tmpB, AF.Gelu,
                          bias=bet_sb[:, fb * 8 + oc: fb * 8 + oc + 1],
                          scale=gam_sb[:, fb * 8 + oc: fb * 8 + oc + 1])
            qlora(2 * fb + 1, y1, "resid", fb)
        for t in range(NT):
            bank = k.bank_bf(5 + 2 * (t % 2))
            for dc in range(8):
                k.trans(bank[:, dc * 128:(dc + 1) * 128],
                        hT[:, dc * T + t * 128: dc * T + (t + 1) * 128], idb)
            k.copy("vector", ybuf, bank[:, 0:1024])
            post_ln_add(k, "L0f", s, t, ybuf)



def attn_pair(k, s, pair, QpT, KpT, Vp, OT, causal):
    ones = ones_bf(k)
    cmT = k.sb("c_cmaskT", [128, 2048], BF16)
    PT = k.sb("at_P", [128, 2048], BF16)
    rdn = k.sb("at_rdn", [128, 1024], F32)
    drow = k.sb("at_drow", [1, 512], BF16)

    for qst in range(4):
        nk = 4 * qst + 4 if causal else 16
        qsl = slice(qst * 512, (qst + 1) * 512)

        def smm(kc):
            sl = k.bank2(kc % 3)
            k.mm(sl[:, 0:512], KpT[0:64, kc * 128:(kc + 1) * 128],
                 QpT[0:64, qsl], start=True, stop=True)
            k.mm(sl[:, 512:1024], KpT[64:128, kc * 128:(kc + 1) * 128],
                 QpT[64:128, qsl], start=True, stop=True, tp=(64, 0))

        def expmask(kc):
            slot = PT[:, (kc % 2) * 1024:(kc % 2 + 1) * 1024]
            k.act("scalar", slot, k.bank2(kc % 3), AF.Exp, scale=0.125)
            if causal and kc >= 4 * qst:
                r = kc - 4 * qst
                for h in range(2):
                    k.tt("vector", slot[:, h * 512:(h + 1) * 512],
                         slot[:, h * 512:(h + 1) * 512],
                         cmT[:, r * 512:(r + 1) * 512], ALU.mult)

        def av(kc):
            slot = PT[:, (kc % 2) * 1024:(kc % 2 + 1) * 1024]
            for h in range(2):
                k.mm(k.bank2(3)[0:65, h * 512:(h + 1) * 512],
                     Vp[:, kc * 130 + h * 65: kc * 130 + (h + 1) * 65],
                     slot[:, h * 512:(h + 1) * 512],
                     start=(kc == 0), stop=(kc == nk - 1))

        for i in range(nk + 2):
            if i < nk:
                smm(i)
            if 0 <= i - 1 < nk:
                expmask(i - 1)
            if 0 <= i - 2 < nk:
                av(i - 2)
        bb = k.bank2(0)
        for h in range(2):
            k.copy("vector", drow, k.bank2(3)[64:65, h * 512:(h + 1) * 512])
            k.mm(bb[0:64, h * 512:(h + 1) * 512], ones[0:1, 0:64], drow,
                 start=True, stop=True)
        k.recip(rdn[0:64, :], bb[0:64, :])
        for h in range(2):
            k.stt("vector", OT[h * 64:(h + 1) * 64, pair * 2048 + qst * 512: pair * 2048 + (qst + 1) * 512],
                  k.bank2(3)[0:64, h * 512:(h + 1) * 512], 1.0,
                  rdn[0:64, h * 512:(h + 1) * 512], ALU.bypass, ALU.mult)


def emit_L1_attn(k):
    Wq = k.tw("L1_Wq")
    Wkvd = k.tw("L1_Wkvd")
    Wku = k.tw("L1_Wku")
    Wvu = k.tw("L1_Wvu")
    Wout = k.tw("L1_Wout")
    idb = ident_bf(k)
    xT = k.sb("at_xT", [128, 8 * T], BF16)
    OT = k.sb("at_OT", [128, 8 * T], BF16)
    lat = k.sb("L1_lat", [128, 2 * T], BF16)
    QpT = k.sb("at_QT", [128, T], BF16)
    KpT = k.sb("at_KT", [128, T], BF16)
    Vp = k.sb("at_V", [128, 2080], BF16)
    k.memset("vector", Vp, 1.0)
    wq_sl = k.sb("at_wq", [128, 1024], BF16)
    wk_sl = k.sb("at_wk", [128, 256], BF16)
    wv_sl = k.sb("at_wv", [128, 256], BF16)
    wkvd = k.sb("L1_wkvd", [128, 2048], BF16)
    ybuf = k.sb("at_ybuf", [128, D], BF16)

    for s in range(SEQ):
        pre_ln_xT(k, "at", s, list(range(NT)), xT, bank0=6)
        k.dma("sync", wkvd.rearrange("p (dc c) -> p dc c", c=256),
              Wkvd.rearrange("(dc p) c -> p dc c", p=128))
        for cc in range(2):
            for tc in range(4):
                b = k.bank(2 * (cc % 2))
                for dc in range(8):
                    k.mm(b, wkvd[:, dc * 256 + cc * 128: dc * 256 + (cc + 1) * 128],
                         xT[:, dc * T + tc * 512: dc * T + (tc + 1) * 512],
                         start=(dc == 0), stop=(dc == 7))
                k.copy("vector", lat[:, cc * T + tc * 512: cc * T + (tc + 1) * 512], b)
        for pair in range(8):
            k.dma("sync", wq_sl.rearrange("p (dc c) -> p dc c", c=128),
                  Wq[:, pair * 128:(pair + 1) * 128].rearrange("(dc p) c -> p dc c", p=128))
            k.dma("sync", wk_sl.rearrange("p (cc c) -> p cc c", c=128),
                  Wku[:, pair * 128:(pair + 1) * 128].rearrange("(cc p) c -> p cc c", p=128))
            k.dma("sync", wv_sl.rearrange("p (cc c) -> p cc c", c=128),
                  Wvu[:, pair * 128:(pair + 1) * 128].rearrange("(cc p) c -> p cc c", p=128))
            for tc in range(4):
                bq = k.bank(0)
                for dc in range(8):
                    k.mm(bq, wq_sl[:, dc * 128:(dc + 1) * 128],
                         xT[:, dc * T + tc * 512: dc * T + (tc + 1) * 512],
                         start=(dc == 0), stop=(dc == 7))
                k.copy("vector", QpT[:, tc * 512:(tc + 1) * 512], bq)
                bk = k.bank(2)
                for cc in range(2):
                    k.mm(bk, wk_sl[:, cc * 128:(cc + 1) * 128],
                         lat[:, cc * T + tc * 512: cc * T + (tc + 1) * 512],
                         start=(cc == 0), stop=(cc == 1))
                k.copy("vector", KpT[:, tc * 512:(tc + 1) * 512], bk)
            for tt in range(NT):
                bv = k.bank(4)
                for cc in range(2):
                    k.mm(bv[:, 0:128], lat[:, cc * T + tt * 128: cc * T + (tt + 1) * 128],
                         wv_sl[:, cc * 128:(cc + 1) * 128],
                         start=(cc == 0), stop=(cc == 1))
                k.copy("vector", Vp[:, tt * 130: tt * 130 + 64], bv[:, 0:64])
                k.copy("vector", Vp[:, tt * 130 + 65: tt * 130 + 129], bv[:, 64:128])
            attn_pair(k, s, pair, QpT, KpT, Vp, OT, causal=True)
        wo = k.sb("at_wo", [128, 8192], BF16)
        k.dma("sync", wo.rearrange("p (pc c) -> p pc c", c=1024),
              Wout.rearrange("(pc p) c -> p pc c", p=128))
        for tt in range(NT):
            for half in range(2):
                b = k.bank(half)
                for pc in range(8):
                    k.mm(b, OT[:, pc * 2048 + tt * 128: pc * 2048 + (tt + 1) * 128],
                         wo[:, pc * 1024 + half * 512: pc * 1024 + (half + 1) * 512],
                         start=(pc == 0), stop=(pc == 7))
                k.copy("vector", ybuf[:, half * 512:(half + 1) * 512], b)
            post_ln_add(k, "at", s, tt, ybuf)


def emit_L1_ffn(k):
    wrT = k.tw("L1f_wrT")
    cent = k.inp("L1f_cent", [16, 128], F32)
    temp = k.inp("L1f_temp", [1, 1], F32)
    IE = k.tw("L1f_ie2")
    OE = k.tw("L1f_oe2")
    idb = ident_bf(k)

    xT = k.sb("at_xT", [128, 8 * T], BF16)
    y = k.sb("f1_y", [128, NT * D], BF16)
    hT = k.sb("f1_hT", [128, 4 * T], BF16)
    wr = k.sb("f1_wr", [128, 1024], BF16)
    ie_sb = k.sb("f1_ie", [128, 8 * 512], BF16)
    oe_sb = k.sb("f1_oe", [128, 256], BF16)
    gates = k.sb("f1_gates", [128, NT * 16], F32)
    z_sb = k.sb("f1_z", [128, 128], F32)
    zsq = k.sb("f1_zsq", [128, 128], BF16)
    zbf = k.sb("f1_zbf", [128, 128], BF16)
    sl_sb = k.sb("f1_sl", [128, 16], F32)
    ge1 = k.sb("f1_ge1", [128, 16], F32)
    slx = k.sb("f1_slx", [128, 16], F32)
    maskb = k.sb("f1_mask", [128, 16], F32)
    pe = k.sb("f1_pe", [128, 16], F32)
    cent_sb = k.sb("f1_cent", [16, 128], F32)
    cnb = k.sb("f1_cnb", [16, 128], BF16)
    cnT = k.sb("f1_cnT", [128, 16], BF16)
    csq = k.sb("f1_csq", [16, 128], BF16)
    n2c = k.col("f1_n2")
    rnc = k.col("f1_rn")
    cn2 = k.sb("f1_cn2", [16, 1], F32)
    m1 = k.col("f1_m1")
    negm1 = k.col("f1_negm1")
    m2 = k.col("f1_m2")
    den = k.col("f1_den")
    rden = k.col("f1_rden")
    fcol = k.col("f1_f")
    tcol = k.col("f1_t")

    k.dma("sync", cent_sb, cent)
    k.act("scalar", csq, cent_sb, AF.Square, accum=cn2[0:16, :])
    k.ts("vector", cn2[0:16, :], cn2[0:16, :], 1e-30, ALU.max)
    k.act("scalar", cn2[0:16, :], cn2[0:16, :], AF.Ln)
    k.act("scalar", cn2[0:16, :], cn2[0:16, :], AF.Exp, scale=-0.5)
    k.ts("vector", cnb, cent_sb, cn2[0:16, 0:1], ALU.mult)
    bT = k.bank_bf(1)
    k.trans(bT[:, 0:16], cnb, idb[0:16, 0:16])
    k.copy("vector", cnT, bT[:, 0:16])
    k.dma("sync", tcol, temp.broadcast_to([128, 1]))
    k.act("scalar", fcol, tcol, AF.Exp)
    k.ts("vector", fcol, fcol, 0.08838834764831845, ALU.mult)
    k.dma("sync", wr.rearrange("p (dc c) -> p dc c", c=128),
          wrT.rearrange("(dc p) c -> p dc c", p=128))

    for s in range(SEQ):
        pre_ln_xT(k, "f1", s, list(range(NT)), xT, bank0=6)
        for tt in range(NT):
            bz = k.bank(0)
            for dc in range(8):
                k.mm(bz[:, 0:128], xT[:, dc * T + tt * 128: dc * T + (tt + 1) * 128],
                     wr[:, dc * 128:(dc + 1) * 128], start=(dc == 0), stop=(dc == 7))
            k.copy("vector", z_sb, bz[:, 0:128])
            k.stt("vector", zsq, z_sb, 1.0, z_sb, ALU.mult, ALU.mult, accum=n2c)
            k.ts("vector", n2c, n2c, 1e-30, ALU.max)
            k.act("scalar", n2c, n2c, AF.Ln)
            k.act("scalar", rnc, n2c, AF.Exp, scale=-0.5)
            k.ts("vector", zbf, z_sb, rnc, ALU.mult)
            bzt = k.bank_bf(1)
            k.trans(bzt[:, 0:128], zbf, idb)
            k.copy("vector", zbf, bzt[:, 0:128])
            bl = k.bank(2)
            k.mm(bl[:, 0:16], zbf, cnT, start=True, stop=True)
            k.ts("vector", sl_sb, bl[:, 0:16], fcol, ALU.mult)
            k.red("vector", m1, sl_sb, ALU.max)
            k.ts("vector", ge1, sl_sb, m1, ALU.is_ge)
            k.stt("vector", slx, ge1, -1e30, sl_sb, ALU.mult, ALU.add)
            k.red("vector", m2, slx, ALU.max)
            k.ts("vector", maskb, sl_sb, m2, ALU.is_ge)
            k.ts("vector", negm1, m1, -1.0, ALU.mult)
            k.act("scalar", pe, sl_sb, AF.Exp, bias=negm1, accum=den)
            k.recip(rden, den)
            k.tt("vector", pe, pe, maskb, ALU.mult)
            k.ts("vector", gates[:, tt * 16:(tt + 1) * 16], pe, rden, ALU.mult)
        for e in range(16):
            ie_sb = k.sb("wring1", [128, 8 * 512], BF16) if e % 2 else k.sb("f1_ie", [128, 8 * 512], BF16)
            k.dma("sync", ie_sb.rearrange("p (dc c) -> p dc c", c=512),
                  IE[e * D:(e + 1) * D, :].rearrange("(dc p) c -> p dc c", p=128))
            k.dma("sync", oe_sb.rearrange("p (hc c) -> p hc c", c=64),
                  OE[e * 512:(e + 1) * 512, :].rearrange("(hc p) c -> p hc c", p=128))
            for hc in range(4):
                for tc in range(4):
                    bh = k.bank(2 * (tc % 2))
                    for dc in range(8):
                        k.mm(bh, ie_sb[:, dc * 512 + hc * 128: dc * 512 + (hc + 1) * 128],
                             xT[:, dc * T + tc * 512: dc * T + (tc + 1) * 512],
                             start=(dc == 0), stop=(dc == 7))
                    k.act("scalar", hT[:, hc * T + tc * 512: hc * T + (tc + 1) * 512],
                          bh, AF.Gelu)
            for tt in range(NT):
                bo = k.bank(4 + 2 * (tt % 2))
                for hc in range(4):
                    k.mm(bo[:, 0:64], hT[:, hc * T + tt * 128: hc * T + (tt + 1) * 128],
                         oe_sb[:, hc * 64:(hc + 1) * 64],
                         start=(hc == 0), stop=(hc == 3))
                k.ts("vector", y[:, tt * D + e * 64: tt * D + (e + 1) * 64],
                     bo[:, 0:64], gates[:, tt * 16 + e: tt * 16 + e + 1], ALU.mult)
        for tt in range(NT):
            post_ln_add(k, "f1", s, tt, y[:, tt * D:(tt + 1) * D])


MAGIC = 12582912.0


def quant_mean(k, tag, W, ncols, scale_col, rscale_col, nelem):
    acc = k.col(f"{tag}_acc")
    tmp = k.col(f"{tag}_tmp")
    one = k.col(f"{tag}_one")
    row = k.sb(f"{tag}_row", [1, 128], F32)
    scr = k.idram(f"{tag}_scr", [128, 1], F32)
    scr2 = k.idram(f"{tag}_scr2", [1, 1], F32)
    nch = ncols // 1024
    for pc in range(8):
        for cc in range(nch):
            stg = k.sb(f"ldstage{(pc + cc) % 2}", [128, 1024], F32)
            k.dma("sync", stg, W[pc * 128:(pc + 1) * 128, cc * 1024:(cc + 1) * 1024])
            if pc == 0 and cc == 0:
                k.red("vector", acc, stg, ALU.add, absval=True)
            else:
                k.red("vector", tmp, stg, ALU.add, absval=True)
                k.tt("vector", acc, acc, tmp, ALU.add)
    k.dma("sync", scr, acc)
    k.dma("sync", row, scr.rearrange("a b -> b a"))
    k.red("vector", one[0:1, :], row, ALU.add)
    k.ts("vector", one[0:1, :], one[0:1, :], 1.0 / nelem, ALU.mult, 1e-8, ALU.max)
    k.dma("sync", scr2, one[0:1, :])
    k.dma("sync", scale_col, scr2.broadcast_to([128, 1]))
    k.act("scalar", rscale_col, scale_col, AF.Ln)
    k.act("scalar", rscale_col, rscale_col, AF.Exp, scale=-1.0)


def quant_slice(k, dst, src_dram, stage_tag, scale_col, rscale_col, qt_tag="L2_qt"):
    p, n = src_dram.shape
    stg = k.sb(stage_tag, [128, 1024], F32)
    qt = k.sb(qt_tag, [128, 1024], F32)
    k.dma("sync", stg[0:p, 0:n], src_dram)
    k.ts("vector", qt[0:p, 0:n], stg[0:p, 0:n], rscale_col, ALU.mult, MAGIC, ALU.add)
    k.ts("vector", qt[0:p, 0:n], qt[0:p, 0:n], MAGIC, ALU.subtract, -2.0, ALU.max)
    k.ts("vector", dst, qt[0:p, 0:n], 2.0, ALU.min, scale_col, ALU.mult)


def emit_L2_attn(k):
    waq = k.tw("L2_waq")
    wpq = k.tw("L2_wpq")
    xT = k.sb("at_xT", [128, 8 * T], BF16)
    OT = k.sb("at_OT", [128, 8 * T], BF16)
    QpT = k.sb("at_QT", [128, T], BF16)
    KpT = k.sb("at_KT", [128, T], BF16)
    Vp = k.sb("at_V", [128, 2080], BF16)
    k.memset("vector", Vp, 1.0)
    wq_sl = k.sb("at_wq", [128, 1024], BF16)
    wk_sl = k.sb("L2_wk", [128, 1024], BF16)
    wv_sl = k.sb("L2_wv", [128, 1024], BF16)
    ybuf = k.sb("at_ybuf", [128, D], BF16)

    for s in range(SEQ):
        pre_ln_xT(k, "at", s, list(range(NT)), xT, bank0=6)
        for pair in range(8):
            k.dma("sync", wq_sl.rearrange("p (dc c) -> p dc c", c=128),
                  waq[:, pair * 128:(pair + 1) * 128].rearrange("(dc p) c -> p dc c", p=128))
            k.dma("sync", wk_sl.rearrange("p (dc c) -> p dc c", c=128),
                  waq[:, D + pair * 128: D + (pair + 1) * 128].rearrange("(dc p) c -> p dc c", p=128))
            k.dma("sync", wv_sl.rearrange("p (dc c) -> p dc c", c=128),
                  waq[:, 2 * D + pair * 128: 2 * D + (pair + 1) * 128].rearrange("(dc p) c -> p dc c", p=128))
            for tc in range(4):
                bq = k.bank(0)
                bk = k.bank(2)
                for dc in range(8):
                    k.mm(bq, wq_sl[:, dc * 128:(dc + 1) * 128],
                         xT[:, dc * T + tc * 512: dc * T + (tc + 1) * 512],
                         start=(dc == 0), stop=(dc == 7))
                for dc in range(8):
                    k.mm(bk, wk_sl[:, dc * 128:(dc + 1) * 128],
                         xT[:, dc * T + tc * 512: dc * T + (tc + 1) * 512],
                         start=(dc == 0), stop=(dc == 7))
                k.copy("vector", QpT[:, tc * 512:(tc + 1) * 512], bq)
                k.copy("vector", KpT[:, tc * 512:(tc + 1) * 512], bk)
            for tt in range(NT):
                bv = k.bank(4)
                for dc in range(8):
                    k.mm(bv[:, 0:128], xT[:, dc * T + tt * 128: dc * T + (tt + 1) * 128],
                         wv_sl[:, dc * 128:(dc + 1) * 128],
                         start=(dc == 0), stop=(dc == 7))
                k.copy("vector", Vp[:, tt * 130: tt * 130 + 64], bv[:, 0:64])
                k.copy("vector", Vp[:, tt * 130 + 65: tt * 130 + 129], bv[:, 64:128])
            attn_pair(k, s, pair, QpT, KpT, Vp, OT, causal=True)
        wo = k.sb("at_wo", [128, 8192], BF16)
        k.dma("sync", wo.rearrange("p (pc c) -> p pc c", c=1024),
              wpq.rearrange("(pc p) c -> p pc c", p=128))
        for tt in range(NT):
            for half in range(2):
                b = k.bank(half)
                for pc in range(8):
                    k.mm(b, OT[:, pc * 2048 + tt * 128: pc * 2048 + (tt + 1) * 128],
                         wo[:, pc * 1024 + half * 512: pc * 1024 + (half + 1) * 512],
                         start=(pc == 0), stop=(pc == 7))
                k.copy("vector", ybuf[:, half * 512:(half + 1) * 512], b)
            post_ln_add(k, "at", s, tt, ybuf)


def emit_L2_ffn(k):
    Win = k.tw("L2f_win")
    binc = k.inp("L2f_binc", [64, 1], F32)
    w1c = k.inp("L2f_w1c", [D, 1], F32)
    b1c = k.inp("L2f_b1c", [D, 1], F32)
    w2s = k.tw("L2f_w2s")
    b2c = k.inp("L2f_b2c", [64, 1], F32)
    Wout = k.tw("L2f_wout")
    bout = k.tw("L2f_bout")
    repT = k.inp("cin_repT", [64, 1024], BF16)

    xT = k.sb("at_xT", [128, 8 * T], BF16)
    h_sb = k.sb("f2_h", [128, 8 * 512], BF16)
    win_sb = k.sb("f2_win", [128, 512], BF16)
    w2s_sb = k.sb("f2_w2s", [128, 512], BF16)
    rep_sb = k.sb("f2_rep", [64, 1024], BF16)
    wout_sb = k.sb("f2_wout", [64, D], BF16)
    bout_sb = k.sb("f2_bout", [1, D], BF16)
    s_bf = k.sb("f2_s", [64, 512], BF16)
    g_bf = k.sb("f2_g", [64, 512], BF16)
    w1cols = k.sb("f2_w1c", [128, 8], F32)
    b1cols = k.sb("f2_b1c", [128, 8], F32)
    bin_col = k.sb("f2_bin", [64, 1], F32)
    b2_col = k.sb("f2_b2", [64, 1], F32)
    ybuf = k.sb("at_ybuf", [128, D], BF16)
    ones = ones_bf(k)

    k.dma("sync", rep_sb, repT)
    k.dma("sync", wout_sb, Wout)
    k.dma("sync", bout_sb, bout)
    k.dma("sync", win_sb.rearrange("p (dc c) -> p dc c", c=64),
          Win.rearrange("(dc p) c -> p dc c", p=128))
    k.dma("sync", w2s_sb.rearrange("p (dc c) -> p dc c", c=64),
          w2s.rearrange("(dc p) c -> p dc c", p=128))
    for dc in range(8):
        k.dma("sync", w1cols[:, dc:dc + 1], w1c[dc * 128:(dc + 1) * 128, :])
        k.dma("sync", b1cols[:, dc:dc + 1], b1c[dc * 128:(dc + 1) * 128, :])
    k.dma("sync", bin_col, binc)
    k.dma("sync", b2_col, b2c)

    for s in range(SEQ):
        pre_ln_xT(k, "f2", s, list(range(NT)), xT, bank0=6)
        for tc in range(4):
            b0 = k.bank(0)
            for dc in range(8):
                k.mm(b0[0:64, :], win_sb[:, dc * 64:(dc + 1) * 64],
                     xT[:, dc * T + tc * 512: dc * T + (tc + 1) * 512],
                     start=(dc == 0), stop=(dc == 7))
            k.ts("vector", s_bf, b0[0:64, :], bin_col[0:64, 0:1], ALU.add)
            for mi in range(8):
                b1_ = k.bank(2 + 2 * (mi % 2))
                k.mm(b1_, rep_sb[:, mi * 128:(mi + 1) * 128], s_bf,
                     start=True, stop=True)
                k.ts("vector", h_sb[:, mi * 512:(mi + 1) * 512], b1_,
                     w1cols[:, mi:mi + 1], ALU.mult, b1cols[:, mi:mi + 1], ALU.add)
                k.ts("vector", h_sb[:, mi * 512:(mi + 1) * 512],
                     h_sb[:, mi * 512:(mi + 1) * 512], 0.0, ALU.max)
            b3 = k.bank(6)
            for mi in range(8):
                k.mm(b3[0:64, :], w2s_sb[:, mi * 64:(mi + 1) * 64],
                     h_sb[:, mi * 512:(mi + 1) * 512],
                     start=(mi == 0), stop=(mi == 7))
            k.ts("vector", g_bf, b3[0:64, :], b2_col[0:64, 0:1], ALU.add)
            for j in range(4):
                tt = tc * 4 + j
                for half in range(2):
                    bo = k.bank(1 + 2 * half)
                    k.mm(bo, g_bf[:, j * 128:(j + 1) * 128],
                         wout_sb[:, half * 512:(half + 1) * 512],
                         start=True, stop=False)
                    k.mm(bo, ones[0:1, 0:128], bout_sb[0:1, half * 512:(half + 1) * 512],
                         start=False, stop=True)
                    k.copy("vector", ybuf[:, half * 512:(half + 1) * 512], bo)
                post_ln_add(k, "f2", s, tt, ybuf)


def emit_L3_attn(k):
    Wqkv = k.tw("L3_wqkv")
    Wp = k.tw("L3_wp")
    bproj = k.tw("L3_bproj")
    qs_in = k.inp("L3_qs", [128, 1], F32)
    ks_in = k.inp("L3_ks", [128, 1], F32)
    blk = k.sb("c_blkones", [128, 128], BF16)
    xT = k.sb("at_xT", [128, 8 * T], BF16)
    OT = k.sb("at_OT", [128, 8 * T], BF16)
    QpT = k.sb("at_QT", [128, T], BF16)
    KpT = k.sb("at_KT", [128, T], BF16)
    Vp = k.sb("at_V", [128, 2080], BF16)
    k.memset("vector", Vp, 1.0)
    wq_sl = k.sb("at_wq", [128, 1024], BF16)
    wk_sl = k.sb("L2_wk", [128, 1024], BF16)
    wv_sl = k.sb("L2_wv", [128, 1024], BF16)
    sqb = k.sb("at_sqr", [128, 1024], BF16)
    rr = k.sb("f3_rr", [128, 512], F32)
    ybuf = k.sb("at_ybuf", [128, D], BF16)
    qs = k.col("L3_qsc")
    ks = k.col("L3_ksc")
    bproj_sb = k.sb("L3_bp", [1, D], BF16)
    eps8 = k.constcol("c_eps8", 1e-8)
    negln8 = k.constcol("c_negln8", -2.0794415416798357)
    ones = ones_bf(k)
    k.dma("sync", qs, qs_in)
    k.dma("sync", ks, ks_in)
    k.dma("sync", bproj_sb, bproj)

    def rmsnorm_evac(bq, dstsl, scol):
        sq_sl = sqb[:, 0:512]
        k.act("scalar", sq_sl, bq, AF.Square)
        bN = k.bank(5)
        k.mm(bN, blk, sq_sl, start=True, stop=True)
        k.act("scalar", rr, bN, AF.Ln)
        k.act("scalar", rr, rr, AF.Exp, scale=0.5, bias=negln8)
        k.ts("vector", rr, rr, 1e-8, ALU.add)
        k.recip(rr, rr)
        k.stt("vector", dstsl, bq, scol, rr, ALU.mult, ALU.mult)

    for s in range(SEQ):
        pre_ln_xT(k, "at", s, list(range(NT)), xT, bank0=6)
        for pair in range(8):
            k.dma("sync", wq_sl.rearrange("p (dc c) -> p dc c", c=128),
                  Wqkv[:, pair * 128:(pair + 1) * 128].rearrange("(dc p) c -> p dc c", p=128))
            k.dma("sync", wk_sl.rearrange("p (dc c) -> p dc c", c=128),
                  Wqkv[:, D + pair * 128: D + (pair + 1) * 128].rearrange("(dc p) c -> p dc c", p=128))
            k.dma("sync", wv_sl.rearrange("p (dc c) -> p dc c", c=128),
                  Wqkv[:, 2 * D + pair * 128: 2 * D + (pair + 1) * 128].rearrange("(dc p) c -> p dc c", p=128))
            for tc in range(4):
                bq = k.bank(0)
                bk = k.bank(2)
                for dc in range(8):
                    k.mm(bq, wq_sl[:, dc * 128:(dc + 1) * 128],
                         xT[:, dc * T + tc * 512: dc * T + (tc + 1) * 512],
                         start=(dc == 0), stop=(dc == 7))
                for dc in range(8):
                    k.mm(bk, wk_sl[:, dc * 128:(dc + 1) * 128],
                         xT[:, dc * T + tc * 512: dc * T + (tc + 1) * 512],
                         start=(dc == 0), stop=(dc == 7))
                rmsnorm_evac(bq, QpT[:, tc * 512:(tc + 1) * 512], qs)
                rmsnorm_evac(bk, KpT[:, tc * 512:(tc + 1) * 512], ks)
            for tt in range(NT):
                bv = k.bank(4)
                for dc in range(8):
                    k.mm(bv[:, 0:128], xT[:, dc * T + tt * 128: dc * T + (tt + 1) * 128],
                         wv_sl[:, dc * 128:(dc + 1) * 128],
                         start=(dc == 0), stop=(dc == 7))
                k.copy("vector", Vp[:, tt * 130: tt * 130 + 64], bv[:, 0:64])
                k.copy("vector", Vp[:, tt * 130 + 65: tt * 130 + 129], bv[:, 64:128])
            attn_pair(k, s, pair, QpT, KpT, Vp, OT, causal=False)
        wo = k.sb("at_wo", [128, 8192], BF16)
        k.dma("sync", wo.rearrange("p (pc c) -> p pc c", c=1024),
              Wp.rearrange("(pc p) c -> p pc c", p=128))
        for tt in range(NT):
            for half in range(2):
                b = k.bank(half)
                for pc in range(8):
                    k.mm(b, OT[:, pc * 2048 + tt * 128: pc * 2048 + (tt + 1) * 128],
                         wo[:, pc * 1024 + half * 512: pc * 1024 + (half + 1) * 512],
                         start=(pc == 0), stop=False)
                k.mm(b, ones[0:1, 0:128], bproj_sb[0:1, half * 512:(half + 1) * 512],
                     start=False, stop=True)
                k.copy("vector", ybuf[:, half * 512:(half + 1) * 512], b)
            post_ln_add(k, "at", s, tt, ybuf)


def emit_L3_ffn(k):
    gw = k.tw("L3f_gw")
    rs_in = k.inp("L3f_rs", [1, 1], F32)
    W1 = k.tw("L3f_w12")
    b1f = k.tw("L3f_b1")
    W2 = k.tw("L3f_w22")
    b2_in = k.tw("L3f_b2")
    Ws1 = k.tw("L3f_ws1")
    bs1_in = k.inp("L3f_bs1", [512, 1], F32)
    Ws2 = k.tw("L3f_ws2")
    bs2_in = k.tw("L3f_bs2")
    idb = ident_bf(k)
    ones = ones_bf(k)

    xT = k.sb("at_xT", [128, 8 * T], BF16)
    y = k.sb("f3_y", [128, NT * D], BF16)
    hsT = k.sb("f3_hsT", [128, 4 * T], BF16)
    ws1_sb = k.sb("f3_ws1", [128, 8 * 512], BF16)
    ws2_sb = k.sb("f3_ws2", [128, 4 * D], BF16)
    w1_sb = k.sb("f3_w1", [128, 8 * 512], BF16)
    w2_sb = k.sb("f3_w2", [128, 4 * D], BF16)
    gw_sb = k.sb("f3_gw", [128, 64], BF16)
    gates = k.sb("f3_gates", [128, NT * 8], F32)
    gb = k.sb("f3_gb", [128, 8], BF16)
    gT = k.sb("f3_gT", [8, 128], BF16)
    b1r = k.sb("f3_b1r", [1, 512], BF16)
    b2_sb = k.sb("f3_b2s", [8, D], BF16)
    bs1c = k.sb("f3_bs1c", [128, 4], F32)
    bs2_sb = k.sb("f3_bs2s", [1, D], BF16)
    hg = k.sb("f3_hg", [128, 512], BF16)
    hgT = k.sb("f3_hgT", [128, 512], BF16)
    sc_sb = k.sb("f3_sc", [128, 8], F32)
    ge1 = k.sb("f3_ge1", [128, 8], F32)
    slx = k.sb("f3_slx", [128, 8], F32)
    maskb = k.sb("f3_mask", [128, 8], F32)
    rs = k.col("f3_rs")
    m1 = k.col("f3_m1")
    m2 = k.col("f3_m2")
    den = k.col("f3_den")
    rden = k.col("f3_rden")

    k.dma("sync", rs, rs_in.broadcast_to([128, 1]))
    k.dma("sync", gw_sb.rearrange("p (dc c) -> p dc c", c=8),
          gw.rearrange("(dc p) c -> p dc c", p=128))
    for hc in range(4):
        k.dma("sync", bs1c[:, hc:hc + 1], bs1_in[hc * 128:(hc + 1) * 128, :])
    k.dma("sync", b2_sb, b2_in)
    k.dma("sync", bs2_sb, bs2_in)

    for s in range(SEQ):
        pre_ln_xT(k, "f3", s, list(range(NT)), xT, bank0=6)
        for tt in range(NT):
            bg = k.bank(5)
            for dc in range(8):
                k.mm(bg[:, 0:8], xT[:, dc * T + tt * 128: dc * T + (tt + 1) * 128],
                     gw_sb[:, dc * 8:(dc + 1) * 8], start=(dc == 0), stop=(dc == 7))
            k.act("scalar", sc_sb, bg[:, 0:8], AF.Exp, scale=-1.0)
            k.ts("vector", sc_sb, sc_sb, 1.0, ALU.add)
            k.recip(sc_sb, sc_sb)
            k.ts("vector", sc_sb, sc_sb, rs, ALU.mult)
            k.red("vector", m1, sc_sb, ALU.max)
            k.ts("vector", ge1, sc_sb, m1, ALU.is_ge)
            k.stt("vector", slx, ge1, -1e30, sc_sb, ALU.mult, ALU.add)
            k.red("vector", m2, slx, ALU.max)
            k.ts("vector", maskb, sc_sb, m2, ALU.is_ge)
            k.tt("vector", den, m1, m2, ALU.add)
            self_recip(k, rden, den)
            k.tt("vector", ge1, sc_sb, maskb, ALU.mult)
            k.ts("vector", gates[:, tt * 8:(tt + 1) * 8], ge1, rden, ALU.mult)
        k.dma("sync", ws1_sb.rearrange("p (dc c) -> p dc c", c=512),
              Ws1.rearrange("(dc p) c -> p dc c", p=128))
        for hc in range(4):
            for tc in range(4):
                bh = k.bank(2 * (tc % 2))
                for dc in range(8):
                    k.mm(bh, ws1_sb[:, dc * 512 + hc * 128: dc * 512 + (hc + 1) * 128],
                         xT[:, dc * T + tc * 512: dc * T + (tc + 1) * 512],
                         start=(dc == 0), stop=(dc == 7))
                k.act("scalar", hsT[:, hc * T + tc * 512: hc * T + (tc + 1) * 512],
                      bh, AF.Gelu, bias=bs1c[:, hc:hc + 1])
        k.dma("sync", ws2_sb.rearrange("p (hc c) -> p hc c", c=D),
              Ws2.rearrange("(hc p) c -> p hc c", p=128))
        for tt in range(NT):
            k.copy("vector", gb, gates[:, tt * 8:(tt + 1) * 8])
            bt = k.bank_bf(4)
            k.trans(bt[0:8, 0:128], gb, idb)
            k.copy("vector", gT, bt[0:8, 0:128])
            for half in range(2):
                bo = k.bank(2 + half)
                for hc in range(4):
                    k.mm(bo, hsT[:, hc * T + tt * 128: hc * T + (tt + 1) * 128],
                         ws2_sb[:, hc * D + half * 512: hc * D + (half + 1) * 512],
                         start=(hc == 0), stop=False)
                k.mm(bo, ones[0:1, 0:128], bs2_sb[0:1, half * 512:(half + 1) * 512],
                     start=False, stop=False)
                k.mm(bo, gT, b2_sb[0:8, half * 512:(half + 1) * 512],
                     start=False, stop=True)
                k.copy("vector", y[:, tt * D + half * 512: tt * D + (half + 1) * 512], bo)
        for e in range(8):
            w1_sb = k.sb("wring1", [128, 8 * 512], BF16) if e % 2 else k.sb("f3_w1", [128, 8 * 512], BF16)
            k.dma("sync", b1r, b1f[0:1, e * 512:(e + 1) * 512])
            k.dma("sync", w1_sb.rearrange("p (dc c) -> p dc c", c=512),
                  W1[e * D:(e + 1) * D, :].rearrange("(dc p) c -> p dc c", p=128))
            k.dma("sync", w2_sb.rearrange("p (hc c) -> p hc c", c=D),
                  W2[e * 512:(e + 1) * 512, :].rearrange("(hc p) c -> p hc c", p=128))
            for tt in range(NT):
                bh = k.bank(2 * (tt % 2))
                for dc in range(8):
                    k.mm(bh, xT[:, dc * T + tt * 128: dc * T + (tt + 1) * 128],
                         w1_sb[:, dc * 512:(dc + 1) * 512],
                         start=(dc == 0), stop=False)
                k.mm(bh, ones[0:1, 0:128], b1r,
                     start=False, stop=True)
                k.act("scalar", hg, bh, AF.Gelu)
                k.ts("vector", hg, hg, gates[:, tt * 8 + e: tt * 8 + e + 1], ALU.mult)
                bt = k.bank_bf(4)
                for j in range(4):
                    k.trans(bt[:, j * 128:(j + 1) * 128], hg[:, j * 128:(j + 1) * 128], idb)
                k.copy("vector", hgT, bt[:, 0:512])
                for half in range(2):
                    bo = k.bank(5 + 2 * half)
                    for hc in range(4):
                        k.mm(bo, hgT[:, hc * 128:(hc + 1) * 128],
                             w2_sb[:, hc * D + half * 512: hc * D + (half + 1) * 512],
                             start=(hc == 0), stop=(hc == 3))
                    ysl = y[:, tt * D + half * 512: tt * D + (half + 1) * 512]
                    k.stt("vector", ysl, bo, 1.0, ysl, ALU.bypass, ALU.add)
        for tt in range(NT):
            post_ln_add(k, "f3", s, tt, y[:, tt * D:(tt + 1) * D])


def self_recip(k, out, in_):
    def f(e, o=out, i=in_):
        return e.reciprocal(o, i)
    k._rec("vector", f, reads=[in_], writes=[out])


BLOCK_EMITTERS = {
    0: emit_L0_attn,
    1: emit_L0_ffn,
    2: emit_L1_attn,
    3: emit_L1_ffn,
    4: emit_L2_attn,
    5: emit_L2_ffn,
    6: emit_L3_attn,
    7: emit_L3_ffn,
}



def build_program(blocks, first, last):
    nc = bass.Bass("TRN2", target_bir_lowering=False)
    k = K(nc)
    if first:
        x_in = k.inp("x_in", [SEQ * T, D], F32)
    else:
        x_in = k.inp("x_in", [SEQ * T, D], BF16)
    st = k.sb("stream", [128, SEQ * NT * D], BF16)
    xr = x_in.rearrange("(n p) d -> p n d", p=128)
    if first:
        for i in range(SEQ * NT):
            stg = k.sb(f"ldstage{i % 2}", [128, D], F32)
            k.dma("sync", stg, xr[:, i, :])
            k.copy("scalar" if i % 2 else "vector", st[:, i * D:(i + 1) * D], stg)
    else:
        for g in range(8):
            q = "sync" if g % 2 else "gpsimd"
            k.dma(q, st.rearrange("p (n d) -> p n d", d=D)[:, g * 4:(g + 1) * 4, :],
                  xr[:, g * 4:(g + 1) * 4, :])
    idb = ident_bf(k)
    k.dma("sync", idb, k.inp("c_ident", [128, 128], BF16))
    k.dma("sync", k.sb("c_blkones", [128, 128], BF16), k.inp("cin_blkones", [128, 128], BF16))
    k.dma("sync", k.sb("c_trimask", [128, 128], BF16), k.inp("cin_trimask", [128, 128], BF16))
    k.dma("sync", k.sb("c_cmaskT", [128, 2048], BF16), k.inp("cin_cmaskT", [128, 2048], BF16))
    emit_weight_prep(k)

    for b in blocks:
        BLOCK_EMITTERS[b](k)

    for tag in _DEBUG_TAPS:
        info = getattr(k, "_sbinfo", {}).get(tag)
        if info is not None:
            ap = k.sb(tag, info[0], info[1])
            o = k.outp("dbg_" + tag, list(ap.shape), ap.dtype)
            k.out_dma("sync", o, ap)

    x_out = k.outp("x_out", [SEQ * T, D], F32 if last else BF16)
    xo = x_out.rearrange("(n p) d -> p n d", p=128)
    for g in range(8):
        q = "gpsimd" if last else ("sync" if g % 2 else "gpsimd")
        k.out_dma(q, xo[:, g * 4:(g + 1) * 4, :],
                  st.rearrange("p (n d) -> p n d", d=D)[:, g * 4:(g + 1) * 4, :])
    k.build()
    return nc, sorted(n for n in k._dram_in if not n.startswith('#'))


def host_prep(inputs):
    w = {}
    c = np.ascontiguousarray
    w["c_ident"] = np.eye(128, dtype=BF)
    w["L0_W_T"] = c(inputs["L0_attn_W"].T)
    w["L0_b"] = c(inputs["L0_attn_b"].reshape(D, 1))
    w["L0f_qT"] = c(inputs["L0_ffn_qweights"].transpose(0, 2, 1))
    w["L0f_scT"] = c(inputs["L0_ffn_scales"].transpose(0, 2, 1))
    w["L0f_laT"] = c(inputs["L0_ffn_lora_a"].transpose(0, 2, 1))
    w["L0f_lbT"] = c(inputs["L0_ffn_lora_b"].transpose(0, 2, 1))
    w["L0f_biases"] = c(inputs["L0_ffn_biases"].reshape(6, D, 1))
    w["L0f_gam"] = c(inputs["L0_ffn_ln_gamma"].reshape(3, D, 1))
    w["L0f_bet"] = c(inputs["L0_ffn_ln_beta"].reshape(3, D, 1))
    blk = np.zeros((128, 128), dtype=BF)
    blk[0:64, 0:64] = 1
    blk[64:128, 64:128] = 1
    w["cin_blkones"] = blk
    w["cin_trimask"] = np.tril(np.ones((128, 128))).astype(BF)
    cmT = np.zeros((128, 4, 512), np.float32)
    for r in range(4):
        for i_ in range(128):
            cmT[i_, r, :] = (128 * r + i_ <= np.arange(512))
    w["cin_cmaskT"] = np.ascontiguousarray(cmT.reshape(128, 2048)).astype(BF)
    w["L1_Wq"] = c(inputs["L1_attn_Wq"].reshape(D, D))
    w["L1_Wkvd"] = c(inputs["L1_attn_Wkv_down"])
    w["L1_Wku"] = c(inputs["L1_attn_Wk_up"].reshape(256, D))
    w["L1_Wvu"] = c(inputs["L1_attn_Wv_up"].reshape(256, D))
    w["L1_Wout"] = c(inputs["L1_attn_Wout"].reshape(D, D))
    w["L1f_wrT"] = c(inputs["L1_ffn_w_router"].T)
    w["L1f_cent"] = c(inputs["L1_ffn_head_centroids"])
    w["L1f_temp"] = c(inputs["L1_ffn_temperature"].reshape(1, 1))
    w["L1f_ie"] = c(inputs["L1_ffn_input_experts"])
    w["L1f_oe"] = c(inputs["L1_ffn_output_experts"])
    w["L2_waT"] = c(inputs["L2_attn_w_attn"].T)
    w["L2_wpT"] = c(inputs["L2_attn_w_proj"].T)
    w["L2f_win"] = c(inputs["L2_ffn_Win"])
    w["L2f_binc"] = c(inputs["L2_ffn_bin_"].reshape(64, 1))
    w["L2f_w1c"] = c(inputs["L2_ffn_W1"].reshape(D, 1))
    w["L2f_b1c"] = c(inputs["L2_ffn_b1"].reshape(D, 1))
    w2s = np.zeros((D, 64), np.float32)
    for m in range(64):
        w2s[m * 16:(m + 1) * 16, m] = inputs["L2_ffn_W2"][m]
    w["L2f_w2s"] = w2s
    w["L2f_b2c"] = c(inputs["L2_ffn_b2"].reshape(64, 1))
    w["L2f_wout"] = c(inputs["L2_ffn_Wout"])
    w["L2f_bout"] = c(inputs["L2_ffn_bout"].reshape(1, D))
    w["L3_wqkv"] = c(inputs["L3_attn_Wqkv"])
    w["L3_wp"] = c(inputs["L3_attn_Wproj"])
    w["L3_bproj"] = c(inputs["L3_attn_bproj"].reshape(1, D))
    w["L3_qs"] = c(np.tile(inputs["L3_attn_q_scale"], 2).reshape(128, 1))
    w["L3_ks"] = c(np.tile(inputs["L3_attn_k_scale"], 2).reshape(128, 1))
    w["L3f_gw"] = c(inputs["L3_ffn_gate_W"])
    w["L3f_rs"] = c(np.asarray(inputs["L3_ffn_route_scale"], np.float32).reshape(1, 1))
    w["L3f_w1"] = c(inputs["L3_ffn_W1"])
    w["L3f_b1"] = c(inputs["L3_ffn_b1"].reshape(1, 8 * 512))
    w["L3f_w2"] = c(inputs["L3_ffn_W2"])
    w["L3f_b2"] = c(inputs["L3_ffn_b2"])
    w["L3f_ws1"] = c(inputs["L3_ffn_Ws1"])
    w["L3f_bs1"] = c(inputs["L3_ffn_bs1"].reshape(512, 1))
    w["L3f_ws2"] = c(inputs["L3_ffn_Ws2"])
    w["L3f_bs2"] = c(inputs["L3_ffn_bs2"].reshape(1, D))
    w["L0f_laT2"] = w["L0f_laT"].reshape(6 * D, 32)
    w["L0f_lbT2"] = w["L0f_lbT"].reshape(6 * 32, D)
    w["L1f_ie2"] = w["L1f_ie"].reshape(16 * D, 512)
    w["L1f_oe2"] = w["L1f_oe"].reshape(16 * 512, 64)
    w["L3f_w12"] = w["L3f_w1"].reshape(8 * D, 512)
    w["L3f_w22"] = w["L3f_w2"].reshape(8 * 512, D)
    rep = np.zeros((64, 1024), dtype=BF)
    for m in range(64):
        rep[m, m * 16:(m + 1) * 16] = 1
    w["cin_repT"] = rep
    return w


def kernel(**inputs):
    groups = _GROUPS if _GROUPS is not None else [list(_BLOCKS)]
    x = inputs["L0_attn_x"].astype(np.float32)
    shards = [c_.reshape(SEQ * T, D) for c_ in np.split(x, NCORES, axis=0)]
    hw = host_prep(inputs)

    cur = shards
    for gi, blocks in enumerate(groups):
        first = (gi == 0)
        last = (gi == len(groups) - 1)
        nc, in_names = build_program(blocks, first, last)
        in_maps = []
        for ci in range(NCORES):
            m = {}
            for name in in_names:
                if name == "x_in":
                    m[name] = np.ascontiguousarray(cur[ci])
                else:
                    m[name] = hw[name]
            in_maps.append(m)
        res = run_bass_kernel_spmd(nc, in_maps, core_ids=list(range(NCORES)))
        for name in res.results[0]:
            if name.startswith("dbg_"):
                _DEBUG_RES[name] = [np.asarray(res.results[ci][name]) for ci in range(NCORES)]
        cur = [res.results[ci]["x_out"] for ci in range(NCORES)]

    out = np.stack([np.asarray(cur[ci], dtype=np.float32).reshape(SEQ, T, D)
                    for ci in range(NCORES)], axis=0).reshape(16, T, D)
    return out
